# version 21
# baseline (speedup 1.0000x reference)
import math
from contextlib import ExitStack

import numpy as np
import ml_dtypes
import concourse.bass as bass
import concourse.mybir as mybir
from concourse.bass_utils import run_bass_kernel_spmd

F32 = mybir.dt.float32
BF16 = mybir.dt.bfloat16
U32 = mybir.dt.uint32
AF = mybir.ActivationFunctionType
ALU = mybir.AluOpType
AX = mybir.AxisListType
BF = ml_dtypes.bfloat16

ENGS = ("sync", "tensor", "vector", "scalar", "gpsimd")


class Prog:

    def __init__(self, nc, es):
        self.nc = nc
        self.es = es
        self.items = {e: [] for e in ENGS}
        self.sems = {}
        self.cnt = {}
        self.pending_barrier = {e: [] for e in ENGS}
        self.last_tok = {e: None for e in ENGS}

    def sem(self, name):
        if name not in self.sems:
            self.sems[name] = self.es.enter_context(self.nc.semaphore(name))
            self.cnt[name] = 0
        return name

    def _flat(self, deps):
        out = []
        for d in deps:
            if d is None:
                continue
            if isinstance(d, list):
                out.extend(x for x in d if x is not None)
            else:
                out.append(d)
        return out

    def task(self, eng, fn, deps=(), sem=None, amt=1):
        if sem is None:
            sem = eng[0:2] + "_d"
        self.sem(sem)
        deps = self._flat(deps) + self.pending_barrier[eng]
        self.pending_barrier[eng] = []
        self.cnt[sem] += amt
        tok = (sem, self.cnt[sem])
        self.items[eng].append((fn, tuple(deps), sem, amt))
        self.last_tok[eng] = tok
        return tok

    def barrier(self):
        toks = [(s, self.cnt[s]) for s in list(self.sems) if self.cnt[s] > 0]
        for e in ENGS:
            self.pending_barrier[e] = list(toks)

    def finish(self):
        deps = [(s, self.cnt[s]) for s in list(self.sems) if self.cnt[s] > 0]
        return self.task("sync", lambda e: e.nop(), deps=deps, sem="fin")

    def finalize(self, block):
        for eng in ENGS:
            if not self.items[eng]:
                continue

            def run(e, eng=eng):
                hw = {}
                for fn, deps, sem, amt in self.items[eng]:
                    for (s, v) in deps:
                        if hw.get(s, 0) < v:
                            e.wait_ge(self.sems[s], v)
                            hw[s] = v
                    inst = fn(e)
                    if isinstance(inst, list):
                        inst = inst[-1]
                    inst.then_inc(self.sems[sem], amt)
            getattr(block, eng)(run)



def make_consts(P, nc, es):
    eyeb = np.eye(128, dtype=BF)
    bd16 = np.zeros((128, 8), dtype=np.float32)
    for p in range(128):
        bd16[p, p // 16] = 1.0
    sel16 = np.ascontiguousarray(bd16.T)
    maskbd_nd = np.zeros((128, 128), dtype=BF)
    for p in range(128):
        q0 = (p // 16) * 16
        maskbd_nd[p, q0:q0 + 16] = 1.0
        maskbd_nd[p, p] = 0.0
    ones1x128f = np.ones((1, 128), dtype=np.float32)
    ones1x128b = np.ones((1, 128), dtype=BF)
    onescb = np.ones((128, 1), dtype=BF)
    onescf = np.ones((128, 1), dtype=np.float32)

    masknd4 = np.ascontiguousarray(np.tile(maskbd_nd, (1, 4)))

    c = {}
    toks = []
    for name, arr in [("eyeb", eyeb), ("masknd", maskbd_nd),
                      ("masknd4", masknd4),
                      ("ones1f", ones1x128f),
                      ("ones1b", ones1x128b), ("onescb", onescb),
                      ("onescf", onescf)]:
        dr = nc.inline_tensor(arr, name="cd_" + name).ap()
        dt = {np.dtype(np.float32): F32, np.dtype(BF): BF16}[arr.dtype]
        t = es.enter_context(nc.sbuf_tensor("c_" + name, list(arr.shape), dt))
        c[name] = t[:, :]
        toks.append(P.task("sync", lambda e, t=t, dr=dr: e.dma_start(
            out=t[:, :], in_=dr), sem="c_ld", amt=16))
    c["_tok"] = toks
    return c



def emit_stageA(P, nc, es, c, io, T):
    NS = T // 128
    x, y1n, r11 = io["x"], io["y1n"], io["r_sigma"]

    sb = lambda n, s, d=F32: es.enter_context(nc.sbuf_tensor(n, s, d))
    ps = lambda n, s, d=F32: es.enter_context(nc.psum_tensor(n, s, d))

    xsup = [sb(f"a_xsup{k}", [128, 1024]) for k in range(2)]
    sq = [sb(f"a_sq{k}", [128, 1024]) for k in range(2)]
    xn = [sb(f"a_xn{k}", [128, 1024]) for k in range(2)]
    xnb = [sb(f"a_xnb{k}", [128, 1024], BF16) for k in range(2)]
    ssum = [sb(f"a_s{k}", [128, 1024]) for k in range(2)]
    y1b = [sb(f"a_y1b{k}", [128, 1024], BF16) for k in range(2)]
    rs32 = [sb(f"a_rs32{k}", [128, 32]) for k in range(2)]
    mrsb = [sb(f"a_mrsb{k}", [8, 32]) for k in range(2)]
    tA = [sb(f"a_tA{k}", [8, 16]) for k in range(2)]
    tB = [sb(f"a_tB{k}", [8, 16]) for k in range(2)]
    sd8 = [sb(f"a_sd8{k}", [8, 16]) for k in range(2)]
    mr128 = [sb(f"a_mr128{k}", [128, 32]) for k in range(2)]
    n16 = [sb(f"a_n16{k}", [128, 16]) for k in range(2)]
    n16h = [sb(f"a_n16h{k}", [128, 16], BF16) for k in range(2)]
    nTall = [sb(f"a_nTall{k}", [16, 128], BF16) for k in range(2)]
    nrow = [sb(f"a_nrow{k}", [1, 2048], BF16) for k in range(2)]
    xnbT = [sb(f"a_xnbT{k}", [64, 2048], BF16) for k in range(2)]
    Kb4 = [sb(f"a_Kb4{k}", [128, 512], BF16) for k in range(2)]
    nr128 = sb("a_nr128", [128, 1])
    r2128 = sb("a_r2128", [128, 1])
    rsb = sb("a_rsb", [1, 1])
    negr = sb("a_negr", [1, 1])
    _bd16n = np.zeros((128, 8), dtype=np.float32)
    for _p in range(128):
        _bd16n[_p, _p // 16] = 1.0
    _bd16d = nc.inline_tensor(_bd16n, name="cd_a_bd16").ap()
    _sel16d = nc.inline_tensor(np.ascontiguousarray(_bd16n.T), name="cd_a_sel16").ap()
    bd16_sb = sb("a_bd16", [128, 8])
    sel16_sb = sb("a_sel16", [8, 128])
    P.task("sync", lambda e: e.dma_start(out=bd16_sb[:, :], in_=_bd16d),
           sem="a_cl", amt=16)
    t_selld = P.task("sync", lambda e: e.dma_start(out=sel16_sb[:, :], in_=_sel16d),
                     sem="a_cl", amt=16)

    pT = [ps(f"a_pT{k}", [64, 1024], BF16) for k in range(2)]
    pG2 = [ps(f"a_pG2{k}", [128, 512]) for k in range(2)]
    pY2 = [ps(f"a_pY2{k}", [128, 512]) for k in range(2)]
    pSm = ps("a_pSm", [128, 512])
    pmr = pSm[:, 128:160]
    pst = pSm[0:8, 160:192]

    sy, te, ve, sc = "a_sy", "a_te", "a_ve", "a_sc"

    xr = x.rearrange("(jt j i) (h d) -> jt (i h) j d", j=16, i=8, h=16, d=64)
    yr = y1n.rearrange("(jt j i) (h d) -> jt (i h) j d", j=16, i=8, h=16, d=64)

    t0 = P.task("sync", lambda e: e.dma_start(out=rsb[:], in_=r11), sem="a_sr", amt=16)
    t1 = P.task("vector", lambda e: e.tensor_scalar_mul(negr[:], rsb[:], -1.0),
                deps=[t0], sem=ve)
    t2 = P.task("tensor", lambda e: e.matmul(pmr[:, 0:1], c["ones1f"], negr[:],
                                             start=True, stop=True),
                deps=[t1], sem=te)
    t_nr = P.task("vector", lambda e: e.tensor_copy(nr128[:], pmr[:, 0:1]),
                  deps=[t2], sem=ve)
    t_r2 = P.task("vector", lambda e: e.tensor_scalar_mul(r2128[:], nr128[:], -2.0),
                  deps=[t_nr], sem=ve)

    fr = {k: {} for k in range(2)}
    fps = {"pst": None, "pmr": [t_nr]}
    fg = {}

    def a_lnphase(jt):
        k = jt % 2
        F = fr[k]
        t_ld = P.task("sync", lambda e, jt=jt, k=k: e.dma_start(
            out=xsup[k].rearrange("p (j d) -> p j d", j=16), in_=xr[jt]),
            deps=[F.get("xsup")], sem=f"a_sx{k}", amt=16)

        t_sq = P.task("scalar", lambda e, k=k: e.activation(
            sq[k][:], xsup[k][:], AF.Square), deps=[t_ld, F.get("sq")], sem=sc)
        t_r1 = P.task("vector", lambda e, k=k: e.tensor_reduce(
            rs32[k][:, 0:16], xsup[k].rearrange("p (s d) -> p s d", s=16, d=64),
            AX.X, ALU.add), deps=[t_ld, F.get("rs32")], sem=ve)
        t_r2s = P.task("vector", lambda e, k=k: e.tensor_reduce(
            rs32[k][:, 16:32], sq[k].rearrange("p (s d) -> p s d", s=16, d=64),
            AX.X, ALU.add), deps=[t_sq], sem=ve)
        t_st = P.task("tensor", lambda e, k=k: e.matmul(
            pst[:], bd16_sb[:, :], rs32[k][:], start=True, stop=True),
            deps=[t_r2s, t_selld, fps["pst"]], sem=te)
        t_m = P.task("vector", lambda e, k=k: e.tensor_scalar_mul(
            mrsb[k][:, 0:16], pst[:, 0:16], 1.0 / 1024.0),
            deps=[t_st, F.get("mrsb")], sem=ve)
        t_ex2 = P.task("vector", lambda e, k=k: e.tensor_scalar(
            tA[k][:], pst[:, 16:32], 1.0 / 1024.0, 1e-5, ALU.mult, ALU.add),
            deps=[t_st, F.get("tA")], sem=ve)
        t_m2 = P.task("vector", lambda e, k=k: e.tensor_mul(
            tB[k][:], mrsb[k][:, 0:16], mrsb[k][:, 0:16]), deps=[t_m], sem=ve)
        t_var = P.task("vector", lambda e, k=k: e.tensor_sub(
            tA[k][:], tA[k][:], tB[k][:]), deps=[t_ex2, t_m2], sem=ve)
        t_sd = P.task("scalar", lambda e, k=k: e.activation(
            sd8[k][:], tA[k][:], AF.Sqrt),
            deps=[t_var, F.get("sd8")], sem=sc)
        t_rs = P.task("vector", lambda e, k=k: e.reciprocal(
            mrsb[k][:, 16:32], sd8[k][:]), deps=[t_sd], sem=ve)
        t_bc = P.task("tensor", lambda e, k=k: e.matmul(
            pmr[:], sel16_sb[:, :], mrsb[k][:], start=True, stop=True),
            deps=[t_m, t_ex2, t_rs] + fps["pmr"], sem=te)
        t_mr = P.task("vector", lambda e, k=k: e.tensor_copy(
            mr128[k][:], pmr[:]), deps=[t_bc, F.get("mr128")], sem=ve)
        fps["pst"] = [t_m, t_ex2, t_mr]
        fps["pmr"] = [t_mr]
        F["mrsb"] = t_bc
        F["rs32"] = t_st
        F["tA"] = t_sd
        F["sd8"] = t_rs

        t_xn0 = P.task("vector", lambda e, k=k: e.tensor_scalar(
            xn[k][:, 0:64], xsup[k][:, 0:64], mr128[k][:, 0:1],
            mr128[k][:, 16:17], ALU.subtract, ALU.mult),
            deps=[t_mr, F.get("xn")], sem=ve)
        for j in range(1, 16):
            P.task("vector", lambda e, k=k, j=j: e.tensor_scalar(
                xn[k][:, j * 64:(j + 1) * 64], xsup[k][:, j * 64:(j + 1) * 64],
                mr128[k][:, j:j + 1], mr128[k][:, 16 + j:17 + j],
                ALU.subtract, ALU.mult), deps=[], sem=ve)
        t_xn_all = P.last_tok["vector"]
        t_xnb = P.task("vector", lambda e, k=k: e.tensor_copy(
            xnb[k][:], xn[k][:]), deps=[t_xn_all, F.get("xnb")], sem=ve)
        t_sqn = P.task("scalar", lambda e, k=k: e.activation(
            sq[k][:], xn[k][:], AF.Square), deps=[t_xn_all, t_r2s], sem=sc)
        t_n16 = P.task("vector", lambda e, k=k: e.tensor_reduce(
            n16[k][:], sq[k].rearrange("p (s d) -> p s d", s=16, d=64),
            AX.X, ALU.add), deps=[t_sqn, F.get("n16")], sem=ve)
        t_n16h = P.task("vector", lambda e, k=k: e.tensor_scalar_mul(
            n16h[k][:], n16[k][:], -0.5), deps=[t_n16, F.get("n16h")], sem=ve)
        t_s = P.task("vector", lambda e, k=k: e.scalar_tensor_tensor(
            ssum[k][:], xn[k][:], 2.0, xsup[k][:], ALU.mult, ALU.add),
            deps=[F.get("ssum")], sem=ve)

        return dict(k=k, F=F, t_xnb=t_xnb, t_n16h=t_n16h, t_s=t_s,
                    t_sqn=t_sqn, t_r1=t_r1, t_sq=t_sq, t_xn_all=t_xn_all,
                    t_n16=t_n16)

    def a_tilephase(jt, S):
        k = S["k"]; F = S["F"]; t_xnb = S["t_xnb"]; t_n16h = S["t_n16h"]
        t_s = S["t_s"]; t_sqn = S["t_sqn"]; t_r1 = S["t_r1"]; t_sq = S["t_sq"]
        t_xn_all = S["t_xn_all"]; t_n16 = S["t_n16"]

        t_ntr = P.task("tensor", lambda e, k=k: e.transpose(
            pT[0][0:16, 0:128], n16h[k][:], c["eyeb"]),
            deps=[t_n16h, fg.get(("pT", 0))], sem=te)
        t_ncp = P.task("vector", lambda e, k=k: e.tensor_copy(
            nTall[k][:], pT[0][0:16, 0:128]),
            deps=[t_ntr, F.get("nTall")], sem=ve)
        t_ndma = P.task("sync", lambda e, k=k: e.dma_start(
            out=nrow[k][0:1, :], in_=nTall[k][:, :]),
            deps=[t_ncp, F.get("nrow")], sem=f"a_sn{k}", amt=16)
        F["nTall"] = t_ndma

        t_cp = [None, None]
        for half in (0, 1):
            t_tr = []
            for jj in range(8):
                j = half * 8 + jj
                t_tr.append(P.task("tensor", lambda e, k=k, j=j, jj=jj,
                                   half=half: e.transpose(
                    pT[half][:, jj * 128:(jj + 1) * 128],
                    xnb[k][:, j * 64:(j + 1) * 64], c["eyeb"]),
                    deps=[t_xnb] +
                         ([t_ncp if half == 0 else None,
                           fg.get(("pT", half))] if jj == 0 else []),
                    sem=te))
            t_cp[half] = P.task("vector", lambda e, k=k, half=half: e.tensor_copy(
                xnbT[k][:, half * 1024:(half + 1) * 1024], pT[half][:]),
                deps=[t_tr[-1], F.get(("xnbT", half))], sem=ve)
            fg[("pT", half)] = t_cp[half]

        t_ap_all = []
        t_y = []
        t_e_last = None
        for jg in range(4):
            kg = jg % 2
            t_mm_last = None
            for jj in range(4):
                j = jg * 4 + jj
                csl = slice(jj * 128, (jj + 1) * 128)
                P.task("tensor", lambda e, k=k, j=j, csl=csl, kg=kg: e.matmul(
                    pG2[kg][:, csl], xnbT[k][:, j * 128:(j + 1) * 128],
                    xnbT[k][:, j * 128:(j + 1) * 128], start=True, stop=False),
                    deps=[t_cp[j // 8]] + ([fg.get(("pG2", kg))] if jj == 0 else []),
                    sem=te)
                P.task("tensor", lambda e, k=k, j=j, csl=csl, kg=kg: e.matmul(
                    pG2[kg][:, csl], c["ones1b"],
                    nrow[k][0:1, j * 128:(j + 1) * 128], start=False, stop=False),
                    deps=[t_ndma], sem=te)
                t_mm_last = P.task("tensor", lambda e, k=k, j=j, csl=csl,
                                   kg=kg: e.matmul(
                    pG2[kg][:, csl], nrow[k][0:1, j * 128:(j + 1) * 128],
                    c["ones1b"], start=False, stop=True),
                    deps=[], sem=te)
            t_e = P.task("scalar", lambda e, k=k, kg=kg: e.activation(
                Kb4[kg][:], pG2[kg][:], AF.Exp, scale=r2128[:, 0:1]),
                deps=[t_mm_last, t_r2, fg.get(("Kb4", kg))], sem=sc)
            t_e_last = t_e
            t_m = P.task("vector", lambda e, kg=kg: e.tensor_mul(
                Kb4[kg][:], Kb4[kg][:], c["masknd4"]), deps=[t_e], sem=ve)
            t_ap = []
            for jj in range(4):
                j = jg * 4 + jj
                half = j // 8
                t_ap.append(P.task("tensor", lambda e, k=k, j=j, jj=jj, kg=kg,
                                   half=half: e.matmul(
                    pY2[half][:, (j % 8) * 64:((j % 8) + 1) * 64],
                    Kb4[kg][:, jj * 128:(jj + 1) * 128],
                    xnb[k][:, j * 64:(j + 1) * 64], start=True, stop=True),
                    deps=[t_m] + ([fg.get(("pY2", half))] if j % 8 == 0 else []),
                    sem=te))
            fg[("pG2", kg)] = t_e
            fg[("Kb4", kg)] = t_ap[-1]
            t_ap_all.extend(t_ap)
            if jg % 2 == 1:
                half = jg // 2
                hs = slice(half * 512, (half + 1) * 512)
                t_y.append(P.task("vector", lambda e, k=k, half=half, hs=hs:
                    e.tensor_add(y1b[k][:, hs], pY2[half][:], ssum[k][:, hs]),
                    deps=t_ap_all[half * 8:half * 8 + 8] + [t_s, F.get("y1b")],
                    sem=ve))
                fg[("pY2", half)] = t_y[-1]

        t_wr = P.task("sync", lambda e, jt=jt, k=k: e.dma_start(
            out=yr[jt], in_=y1b[k].rearrange("p (j d) -> p j d", j=16)),
            deps=t_y, sem=f"a_sw{k}", amt=16)

        F["xsup"] = [t_s, t_sq, t_r1]
        F["sq"] = [t_n16]
        F["xn"] = [t_sqn, t_xnb, t_s]
        F["xnb"] = [t_ap_all[-1]]
        F["n16"] = [t_n16h]
        F["n16h"] = [t_ntr]
        F["nrow"] = [t_e_last]
        F[("xnbT", 0)] = t_e_last
        F[("xnbT", 1)] = t_e_last
        F["ssum"] = [t_y[-1]]
        F["y1b"] = t_wr


    pend_a = None
    for jt in range(NS):
        st_a = a_lnphase(jt)
        if pend_a is not None:
            a_tilephase(*pend_a)
        pend_a = (jt, st_a)
    a_tilephase(*pend_a)
    return None



def build_program(T=4096, stages=("A",), expose=(), inputs_override=()):
    nc = bass.Bass("TRN2", target_bir_lowering=False)
    D = 1024

    def dram(name, shape, dt=F32, kind="ExternalInput"):
        return nc.dram_tensor(name, shape, dt, kind=kind).ap()

    def scratch(name, shape, dt):
        kind = ("ExternalInput" if name in inputs_override else
                "ExternalOutput" if name in expose else "Internal")
        return dram(name, shape, dt, kind=kind)

    io = {}
    io["x"] = dram("x", [T, D])
    io["r_sigma"] = dram("r_sigma", [1, 1])
    io["y1n"] = scratch("y1n", [T, D], BF16)
    if any(st in stages for st in ("B1", "B2", "B3", "B4", "C", "D")):
        io["qw"] = dram("qw", [D, D])
        io["kw"] = dram("kw", [D, D])
        io["vw"] = dram("vw", [D, D])
        io["sw_bd"] = dram("sw_bd", [D, 16])
        io["y1T"] = scratch("y1T", [D, T], BF16)
        io["xnT"] = scratch("xnT", [D, T], BF16)
        io["qsT"] = scratch("qsT", [D, T], BF16)
        io["kTh"] = scratch("kTh", [D, T], BF16)
        io["outT"] = scratch("outT", [D, T], BF16)
    if any(st in stages for st in ("B3", "B4", "C", "D")):
        io["gw"] = dram("gw", [D, D])
        io["ow"] = dram("ow", [D, D])
        io["gb"] = dram("gb", [128, 8])
        io["n1g"] = dram("n1g", [128, 8])
        io["n1b"] = dram("n1b", [128, 8])
        io["mgw"] = dram("mgw", [D, 8])
        io["mgb"] = dram("mgb", [1, 8])
        io["x1T"] = scratch("x1T", [D, T], BF16)
        io["indh"] = scratch("indh", [1, 8 * T], BF16)
        io["w1"] = dram("w1", [8, D, 2048])
        io["w2"] = dram("w2", [8, 2048, D])
        io["w1b"] = scratch("w1b", [8, D, 2048], BF16)
        io["w2b"] = scratch("w2b", [8, 2048, D], BF16)
        io["b1r"] = dram("b1r", [8, 128, 16])
        io["lgr"] = dram("lgr", [8, 128, 16])
        io["lbr"] = dram("lbr", [8, 128, 16])
        io["b2r"] = dram("b2r", [8, 128, 8])
        io["rsc"] = dram("rsc", [1, 8])
        io["n2g"] = dram("n2g", [128, 8])
        io["n2b"] = dram("n2b", [128, 8])
        io["y2T"] = scratch("y2T", [D, T], BF16)
    if any(st in stages for st in ("C", "D")):
        io["wq"] = dram("wq", [D, D])
        io["wkv"] = dram("wkv", [D, 2 * D])
        io["wo"] = dram("wo", [D, D])
        io["bo"] = dram("bo", [128, 8])
        io["medT"] = dram("medT", [D, 64])
        io["xn3T"] = scratch("xn3T", [D, T], BF16)
        io["y3T"] = scratch("y3T", [D, T], BF16)
    if "D" in stages:
        io["wfcT"] = dram("wfcT", [D, 4096])
        io["wpT"] = dram("wpT", [4096, D])
        io["lnw"] = dram("lnw", [128, 32])
        io["lnb"] = dram("lnb", [128, 32])
        io["wqfch"] = scratch("wqfch", [D, 4096], BF16)
        io["wqph"] = scratch("wqph", [4096, D], BF16)
        io["d0thr"] = scratch("d0thr", [1, 1], F32)
        io["y"] = dram("y", [T, D], F32, kind="ExternalOutput")

    es = ExitStack()
    with es:
        P = Prog(nc, es)
        c = make_consts(P, nc, es)
        P.barrier()
        ind8sb = None
        if "A" in stages:
            with ExitStack() as es_a:
                emit_stageA(P, nc, es_a, c, io, T)
            P.barrier()
        if any(st in stages for st in ("B1", "B2")):
            with ExitStack() as es_b12:
                vaugsb = es_b12.enter_context(
                    nc.sbuf_tensor("vaug", [128, (T // 128) * 1040], BF16))
                if "B1" in stages:
                    with ExitStack() as es_b1:
                        emit_stageB1(P, nc, es_b1, c, io, T, None, vaugsb)
                    P.barrier()
                if "B4" in stages:
                    for ei in range(8):
                        P.task("gpsimd", lambda e, ei=ei: e.dma_start(
                            out=io["w1b"][ei], in_=io["w1"][ei]), sem="wc1", amt=16)
                        P.task("gpsimd", lambda e, ei=ei: e.dma_start(
                            out=io["w2b"][ei], in_=io["w2"][ei]), sem="wc2", amt=16)
                if "B2" in stages:
                    with ExitStack() as es_b2:
                        d0gen = (emit_stageD0_gen(P, nc, es_b2, c, io, T)
                                 if "D" in stages else None)
                        emit_stageB2(P, nc, es_b2, c, io, T, None, vaugsb,
                                     d0gen=d0gen)
                        if d0gen is not None:
                            for _ in d0gen:
                                pass
                    P.barrier()
        if "B3" in stages:
            with ExitStack() as es_b3:
                emit_stageB3(P, nc, es_b3, c, io, T, ind8sb)
            P.barrier()
        if "B4" in stages:
            with ExitStack() as es_b4:
                emit_stageB4(P, nc, es_b4, c, io, T, ind8sb)
            P.barrier()
        if "D" in stages and "B2" not in stages:
            with ExitStack() as es_d0:
                for _ in emit_stageD0_gen(P, nc, es_d0, c, io, T):
                    pass
            P.barrier()
        if "C" in stages:
            with ExitStack() as es_c:
                emit_stageC(P, nc, es_c, c, io, T)
            P.barrier()
        if "D" in stages:
            with ExitStack() as es_d:
                emit_stageD(P, nc, es_d, c, io, T)
            P.barrier()
        if "dump_ind8" in expose:
            ind_o = nc.dram_tensor("ind8_dump", [1, 8 * T], BF16,
                                   kind="ExternalOutput").ap()
            P.task("sync", lambda e: e.dma_start(out=ind_o, in_=ind8sb[:, :]),
                   sem="dmp", amt=16)
        if "dump_b1" in expose:
            _dump_b1(P, nc, es, io, T, kTsb, vaugsb)
        P.finish()
        with nc.Block() as block:
            P.finalize(block)
    return nc


def _dump_b1(P, nc, es, io, T, kTsb, vaugsb):
    kT_o = nc.dram_tensor("kT_dump", [1024, T], BF16, kind="ExternalOutput").ap()
    va_o = nc.dram_tensor("va_dump", [128, (T // 128) * 1040], BF16,
                          kind="ExternalOutput").ap()
    for i in range(8):
        P.task("sync", lambda e, i=i: e.dma_start(
            out=kT_o[i * 128:(i + 1) * 128, :], in_=kTsb[i][:, :]),
            sem="dmp", amt=16)
    P.task("sync", lambda e: e.dma_start(out=va_o, in_=vaugsb[:, :]),
           sem="dmp", amt=16)



def np_ln(x, eps=1e-5):
    m = x.mean(-1, keepdims=True)
    v = x.var(-1, keepdims=True)
    return (x - m) / np.sqrt(v + eps)


def np_stageA(x, r):
    T, D = x.shape
    xn = np_ln(x)
    xh = xn.reshape(T, 16, 64)
    diff = xh[:, None, :, :] - xh[:, :, None, :]
    dist = (diff * diff).sum(-1)
    K = np.exp(-dist * r)
    y = xh + np.einsum('thg,tgd->thd', K, xh)
    return x + y.reshape(T, D)



def emit_pln(P, nc, c, pref, W, chunks_in, chunks_out, aux, deps_in,
             gain=None, bias=None, extra_res=None, fuse_act=None):
    ve, te, sc = pref + "ve", pref + "te", pref + "sc"
    sq2, pS1, pS2 = aux["sq2"], aux["pS1"], aux["pS2"]
    r_m, r_e, r_v, r_s = aux["rows"]
    pB1, pB2 = aux["pB1"], aux["pB2"]
    D = 128 * len(chunks_in)

    t_s1 = []
    for i, (ap, tok) in enumerate(chunks_in):
        t_s1.append(P.task("tensor", lambda e, ap=ap, i=i: e.matmul(
            pS1[:], c["onescb"], ap, start=(i == 0), stop=(i == len(chunks_in) - 1)),
            deps=[tok] + ([aux.get("f_pS1")] if i == 0 else []), sem=te))
    t_sq = []
    t_sqm = [None, None]
    for i, (ap, tok) in enumerate(chunks_in):
        kq = i % 2
        t_v = P.task("scalar", lambda e, ap=ap, kq=kq: e.activation(
            sq2[kq][:], ap, AF.Square),
            deps=[tok, t_sqm[kq] if t_sqm[kq] else aux.get(("f_sq", kq))],
            sem=pref + "sc")
        t_m_ = P.task("tensor", lambda e, i=i, kq=kq: e.matmul(
            pS2[:], c["onescb"], sq2[kq][:],
            start=(i == 0), stop=(i == len(chunks_in) - 1)),
            deps=[t_v] + ([aux.get("f_pS2")] if i == 0 else []), sem=te)
        t_sqm[kq] = t_m_
        t_sq.append(t_m_)
    t_m = P.task("vector", lambda e: e.tensor_scalar_mul(r_m[:], pS1[:], 1.0 / D),
                 deps=[t_s1[-1], t_sq[-1], aux.get("f_rows")], sem=ve)
    t_e2 = P.task("vector", lambda e: e.tensor_scalar(
        r_e[:], pS2[:], 1.0 / D, 1e-5, ALU.mult, ALU.add), deps=[t_sq[-1]], sem=ve)
    t_v = P.task("vector", lambda e: e.scalar_tensor_tensor(
        r_v[:], r_m[:], -1.0, r_m[:], ALU.mult, ALU.mult), deps=[t_m, t_e2], sem=ve)
    t_v2 = P.task("vector", lambda e: e.tensor_add(r_v[:], r_v[:], r_e[:]),
                  deps=[t_v], sem=ve)
    t_sd = P.task("scalar", lambda e: e.activation(r_s[:], r_v[:], AF.Sqrt),
                  deps=[t_v2, aux.get("f_rs")], sem=sc)
    t_rc = P.task("vector", lambda e: e.reciprocal(r_v[:], r_s[:]), deps=[t_sd], sem=ve)
    NW = (W + 511) // 512
    t_bc1 = []
    t_bc2 = []
    for i in range(NW):
        slw = slice(i * 512, min((i + 1) * 512, W))
        t_bc1.append(P.task("tensor", lambda e, slw=slw: e.matmul(
            pB1[:, slw], c["ones1f"], r_m[0:1, slw], start=True, stop=True),
            deps=[t_m] + ([aux.get("f_pB1")] if i == 0 else []), sem=te))
        t_bc2.append(P.task("tensor", lambda e, slw=slw: e.matmul(
            pB2[:, slw], c["ones1f"], r_v[0:1, slw], start=True, stop=True),
            deps=[t_rc] + ([aux.get("f_pB2")] if i == 0 else []), sem=te))
    t_mB = t_bc1[-1]
    t_rB = t_bc2[-1]
    aux["f_pS1"] = [t_m, t_e2]
    aux["f_pS2"] = t_e2
    aux[("f_sq", 0)] = t_sqm[0]
    aux[("f_sq", 1)] = t_sqm[1]
    aux["f_rows"] = [t_bc1[-1], t_bc2[-1], t_rc]
    aux["f_rs"] = t_rc

    t_out = []
    for i, (ap, tok) in enumerate(chunks_in):
        oap = chunks_out[i]
        t1 = P.task("vector", lambda e, ap=ap, oap=oap: e.tensor_sub(
            oap, ap, pB1[:]), deps=[t_mB, tok] + (list(deps_in) if i == 0 else []),
            sem=ve)
        t2 = P.task("vector", lambda e, oap=oap: e.tensor_mul(
            oap, oap, pB2[:]), deps=[t_rB, t1], sem=ve)
        if fuse_act is not None:
            t2 = P.task("scalar", lambda e, oap=oap, i=i: e.activation(
                oap, oap, fuse_act, scale=gain[i], bias=bias[i]),
                deps=[t2], sem=pref + "sc")
        elif gain is not None:
            t2 = P.task("vector", lambda e, oap=oap, i=i: e.tensor_scalar(
                oap, oap, gain[i], bias[i], ALU.mult, ALU.add), deps=[t2], sem=ve)
        if extra_res is not None:
            t2 = P.task("vector", lambda e, oap=oap, i=i: e.tensor_add(
                oap, oap, extra_res[i]), deps=[t2], sem=ve)
        t_out.append(t2)
    aux["f_pB1"] = t_out[-1]
    aux["f_pB2"] = t_out[-1]
    return t_out



def emit_stageB1(P, nc, es, c, io, T, kTsb, vaugsb):
    NT = T // 512
    y1n, y1T, xnT, qsT = io["y1n"], io["y1T"], io["xnT"], io["qsT"]

    sb = lambda n, s, d=F32: es.enter_context(nc.sbuf_tensor(n, s, d))
    ps = lambda n, s, d=F32: es.enter_context(nc.psum_tensor(n, s, d))

    qwb = sb("b1_qw", [128, 8192], BF16)
    kwb = sb("b1_kw", [128, 8192], BF16)
    vwb = sb("b1_vw", [128, 8192], BF16)
    swb = sb("b1_sw", [128, 128], BF16)
    t_w = []
    for wsb, wname in [(qwb, "qw"), (kwb, "kw"), (vwb, "vw")]:
        for ci in range(8):
            t_w.append(P.task("gpsimd", lambda e, wsb=wsb, wname=wname, ci=ci:
                e.dma_start(out=wsb[:, ci * 1024:(ci + 1) * 1024],
                            in_=io[wname][ci * 128:(ci + 1) * 128, :]),
                sem="b1_gw", amt=16))
    for ci in range(8):
        t_w.append(P.task("gpsimd", lambda e, ci=ci: e.dma_start(
            out=swb[:, ci * 16:(ci + 1) * 16],
            in_=io["sw_bd"][ci * 128:(ci + 1) * 128, :]), sem="b1_gw", amt=16))

    y1Tsb = [sb(f"b1_y1T{k}", [128, 4096], BF16) for k in range(2)]
    xnTsb = [sb(f"b1_xnT{k}", [128, 4096], BF16) for k in range(2)]
    scalT = [sb(f"b1_scal{k}", [16, 512], BF16) for k in range(2)]
    _spn = np.zeros((16, 1024), dtype=BF)
    for _cp in range(8):
        _spn[2 * _cp, _cp * 128: _cp * 128 + 64] = 0.25
        _spn[2 * _cp + 1, _cp * 128 + 64: _cp * 128 + 128] = 0.25
    _spd = nc.inline_tensor(_spn, name="cd_selpair").ap()
    selpair_sb = sb("b1_selpair", [16, 1024], BF16)
    t_sp_ld = P.task("sync", lambda e: e.dma_start(out=selpair_sb[:, :], in_=_spd),
                     sem="b1_spl", amt=16)
    qT8 = [sb(f"b1_qT8{k}", [128, 4096], BF16) for k in range(2)]
    qsTst = [sb(f"b1_qsT{k}", [128, 512], BF16) for k in range(2)]
    kTst = [sb(f"b1_kTst{k}", [128, 512], BF16) for k in range(2)]
    aux = {
        "sq2": [sb("b1_sq0", [128, 512], BF16), sb("b1_sq1", [128, 512], BF16)],
        "rows": [sb(f"b1_row{i}", [1, 512]) for i in range(4)],
    }
    pSrow = ps("b1_pSrow", [64, 512])
    aux["pS1"], aux["pS2"] = pSrow[0:1, :], pSrow[32:33, :]
    aux["pB1"] = ps("b1_pB1", [128, 512])
    aux["pB2"] = ps("b1_pB2", [128, 512])
    pQK = [ps(f"b1_pQK{k}", [128, 512]) for k in range(2)]
    pSP = ps("b1_pSP", [16, 512])
    pSB = ps("b1_pSB", [128, 512])

    te, ve, sc = "b1_te", "b1_ve", "b1_sc"
    f = {}

    for g in range(NT):
        k = g % 2
        t0 = g * 512
        t_ld = []
        for ci in range(8):
            t_ld.append(P.task("sync", lambda e, ci=ci, t0=t0, k=k: e.dma_start(
                out=y1Tsb[k][:, ci * 512:(ci + 1) * 512],
                in_=y1n[t0:t0 + 512, ci * 128:(ci + 1) * 128], transpose=True),
                deps=[f.get(("y1T", k, ci))], sem=f"b1_sl{k}", amt=16))
        t_ld_all = t_ld[-1]
        for ci in range(8):
            P.task("sync", lambda e, ci=ci, t0=t0, k=k: e.dma_start(
                out=y1T[ci * 128:(ci + 1) * 128, t0:t0 + 512],
                in_=y1Tsb[k][:, ci * 512:(ci + 1) * 512]),
                deps=[t_ld_all], sem=f"b1_sw{k}", amt=16)
        t_y1w_last = P.last_tok["sync"]

        chunks_in = [(y1Tsb[k][:, ci * 512:(ci + 1) * 512], t_ld_all) for ci in range(8)]
        chunks_out = [xnTsb[k][:, ci * 512:(ci + 1) * 512] for ci in range(8)]
        t_xn = emit_pln(P, nc, c, "b1_", 512, chunks_in, chunks_out, aux,
                        deps_in=[f.get(("xnT", k))])
        for ci in range(8):
            P.task("sync", lambda e, ci=ci, t0=t0, k=k: e.dma_start(
                out=xnT[ci * 128:(ci + 1) * 128, t0:t0 + 512],
                in_=xnTsb[k][:, ci * 512:(ci + 1) * 512]),
                deps=[t_xn[ci]], sem=f"b1_sx{k}", amt=16)
        t_xnw_last = P.last_tok["sync"]

        t_qev = []
        for cp in range(8):
            kq = cp % 2
            t_qm = []
            for ci in range(8):
                t_qm.append(P.task("tensor", lambda e, ci=ci, cp=cp, k=k, kq=kq: e.matmul(
                    pQK[kq][:], qwb[:, ci * 1024 + cp * 128: ci * 1024 + (cp + 1) * 128],
                    xnTsb[k][:, ci * 512:(ci + 1) * 512],
                    start=(ci == 0), stop=(ci == 7)),
                    deps=[t_xn[ci], t_w[-1]] + ([f.get(("pQK", kq))] if ci == 0 else []),
                    sem=te))
            t_qev.append(P.task("vector", lambda e, cp=cp, k=k, kq=kq: e.tensor_copy(
                qT8[k][:, cp * 512:(cp + 1) * 512], pQK[kq][:]),
                deps=[t_qm[-1]] + ([f.get(("qT8", k))] if cp == 0 else []), sem=ve))
            f[("pQK", kq)] = t_qev[-1]

        t_sp = []
        for ci in range(8):
            t_sp.append(P.task("tensor", lambda e, ci=ci, k=k: e.matmul(
                pSP[:], swb[:, ci * 16:(ci + 1) * 16], qT8[k][:, ci * 512:(ci + 1) * 512],
                start=(ci == 0), stop=(ci == 7)),
                deps=[t_qev[ci]] + ([f.get("pSP")] if ci == 0 else []), sem=te))
        t_sg = P.task("scalar", lambda e, k=k: e.activation(
            scalT[k][:], pSP[:], AF.Sigmoid), deps=[t_sp[-1], f.get(("scalT", k))], sem=sc)
        f["pSP"] = t_sg

        t_q_last = None
        for cp in range(8):
            kq = cp % 2
            t_sb2 = P.task("tensor", lambda e, cp=cp, k=k: e.matmul(
                pSB[:], selpair_sb[:, cp * 128:(cp + 1) * 128], scalT[k][:],
                start=True, stop=True), deps=[t_sg, t_sp_ld, f.get("pSB")], sem=te)
            t_qs = P.task("vector", lambda e, cp=cp, k=k, kq=kq: e.tensor_mul(
                qsTst[kq][:], qT8[k][:, cp * 512:(cp + 1) * 512], pSB[:]),
                deps=[t_sb2, f.get(("qsTst", kq))], sem=ve)
            f["pSB"] = t_qs
            t_qw = P.task("sync", lambda e, cp=cp, t0=t0, kq=kq: e.dma_start(
                out=qsT[cp * 128:(cp + 1) * 128, t0:t0 + 512], in_=qsTst[kq][:]),
                deps=[t_qs], sem=f"b1_sq{kq}", amt=16)
            f[("qsTst", kq)] = t_qw
            t_km = []
            for ci in range(8):
                t_km.append(P.task("tensor", lambda e, ci=ci, cp=cp, k=k, kq=kq: e.matmul(
                    pQK[kq][:], kwb[:, ci * 1024 + cp * 128: ci * 1024 + (cp + 1) * 128],
                    xnTsb[k][:, ci * 512:(ci + 1) * 512],
                    start=(ci == 0), stop=(ci == 7)),
                    deps=[f.get(("pQK", kq))] if ci == 0 else [], sem=te))
            t_ke = P.task("vector", lambda e, kq=kq: e.tensor_copy(
                kTst[kq][:], pQK[kq][:]), deps=[t_km[-1], f.get(("kTst", kq))], sem=ve)
            f[("pQK", kq)] = t_ke
            t_kw2 = P.task("sync", lambda e, cp=cp, t0=t0, kq=kq: e.dma_start(
                out=io["kTh"][cp * 128:(cp + 1) * 128, t0:t0 + 512], in_=kTst[kq][:]),
                deps=[t_ke], sem=f"b1_sk{kq}", amt=16)
            f[("kTst", kq)] = t_kw2
            t_q_last = t_ke

        for tt_ in range(4):
            ta = t0 // 128 + tt_
            for cg in range(2):
                kq = cg % 2
                t_vm = []
                for ci in range(8):
                    t_vm.append(P.task("tensor", lambda e, ci=ci, tt_=tt_, cg=cg, k=k, kq=kq: e.matmul(
                        pQK[kq][:],
                        xnTsb[k][:, ci * 512 + tt_ * 128: ci * 512 + (tt_ + 1) * 128],
                        vwb[:, ci * 1024 + cg * 512: ci * 1024 + (cg + 1) * 512],
                        start=(ci == 0), stop=(ci == 7)),
                        deps=[f.get(("pQK", kq))] if ci == 0 else [], sem=te))
                t_ve_ = P.task("vector", lambda e, ta=ta, cg=cg, kq=kq: e.tensor_copy(
                    vaugsb[:, ta * 1040 + cg * 520: ta * 1040 + (cg + 1) * 520]
                    .rearrange("p (h d) -> p h d", h=8, d=65)[:, :, 0:64],
                    pQK[kq].rearrange("p (h d) -> p h d", h=8, d=64)),
                    deps=[t_vm[-1]], sem=ve)
                f[("pQK", kq)] = t_ve_

        f[("xnT", k)] = [t_q_last, t_xnw_last, P.last_tok["tensor"]]
        f[("qT8", k)] = [P.last_tok["tensor"], P.last_tok["vector"]]
        for ci in range(8):
            f[("y1T", k, ci)] = [t_xn[ci], t_y1w_last]
        f[("scalT", k)] = P.last_tok["tensor"]

    t_ones = []
    for ta in range(T // 128):
        t_ones.append(P.task("vector", lambda e, ta=ta: e.memset(
            vaugsb[:, ta * 1040:(ta + 1) * 1040]
            .rearrange("p (h d) -> p h d", h=16, d=65)[:, :, 64:65], 1.0), sem=ve))
    return None



def emit_stageB2(P, nc, es, c, io, T, kTsb_unused, vaugsb, d0gen=None):
    QG = min(1024, T)
    NQ = T // QG
    NK = T // 128
    outT, qsT = io["outT"], io["qsT"]

    sb = lambda n, s, d=F32: es.enter_context(nc.sbuf_tensor(n, s, d))
    ps = lambda n, s, d=F32: es.enter_context(nc.psum_tensor(n, s, d))

    kTsb = [sb(f"b2_kT{i}", [128, T], BF16) for i in range(8)]
    t_lk = []
    for cp in range(8):
        t_lk.append(P.task("sync", lambda e, cp=cp: e.dma_start(
            out=kTsb[cp][:, :], in_=io["kTh"][cp * 128:(cp + 1) * 128, :]),
            sem="b2_lk", amt=16))
    qsg = sb("b2_qsg", [128, 8 * QG], BF16)
    expS = [[sb(f"b2_eS{hi}_{k}", [128, QG], BF16) for k in range(2)]
            for hi in range(2)]
    srow = [sb(f"b2_srow{hi}", [1, QG]) for hi in range(2)]
    rrow = [sb(f"b2_rrow{hi}", [1, QG]) for hi in range(2)]
    rrb = [sb(f"b2_rrb{hi}", [1, QG], BF16) for hi in range(2)]
    rBsb = [sb(f"b2_rB{hi}", [64, QG], BF16) for hi in range(2)]
    outst = [sb(f"b2_out{hi}", [64, QG], BF16) for hi in range(2)]
    outst32 = [sb(f"b2_o32{hi}", [64, QG]) for hi in range(2)]

    pS = [ps(f"b2_pS{hi}", [128, QG]) for hi in range(2)]
    pAV = [ps(f"b2_pAV{hi}", [65, QG]) for hi in range(2)]

    te, ve, sc = "b2_te", "b2_ve", "b2_sc"
    NCS = QG // 512
    f = {}

    for qg in range(NQ):
        q0 = QG * qg
        t_lq = []
        for cp in range(8):
            t_lq.append(P.task("sync", lambda e, cp=cp, q0=q0: e.dma_start(
                out=qsg[:, cp * QG:(cp + 1) * QG],
                in_=qsT[cp * 128:(cp + 1) * 128, q0:q0 + QG]),
                deps=[f.get("qsg")], sem="b2_lq", amt=16))
        t_lq_all = t_lq[-1]

        NFL = 3
        pend_norm = []
        for hp in range(8):
            heads = (2 * hp, 2 * hp + 1)
            t_ex = {0: [], 1: []}
            t_av = {0: [], 1: []}

            def avphase(kc, kp):
                for hi in (0, 1):
                    h = heads[hi]
                    for cs in range(NCS):
                        t_av[hi].append(P.task("tensor", lambda e, h=h, kc=kc,
                                               cs=cs, hi=hi, kp=kp: e.matmul(
                            pAV[hi][:, cs * 512:(cs + 1) * 512],
                            vaugsb[:, kc * 1040 + h * 65: kc * 1040 + (h + 1) * 65],
                            expS[hi][kp][:, cs * 512:(cs + 1) * 512],
                            start=(kc == 0), stop=(kc == NK - 1)),
                            deps=[t_ex[hi][kc]] +
                                 ([f.get(("pAV", hi))] if kc == 0 and cs == 0 else []),
                            sem=te))
                    f[("expS", hi, kp)] = t_av[hi][-1]

            def norm_tail(h, hi, t_rdy):
                t_rc = P.task("vector", lambda e, hi=hi: e.reciprocal(
                    rrow[hi][:], srow[hi][:]), deps=[t_rdy, f.get(("rrow", hi))],
                    sem=ve)
                t_rb = P.task("vector", lambda e, hi=hi: e.tensor_copy(
                    rrb[hi][:], rrow[hi][:]), deps=[t_rc, f.get(("rrb", hi))], sem=ve)
                t_bc = []
                for cs in range(NCS):
                    t_bc.append(P.task("tensor", lambda e, hi=hi, cs=cs: e.matmul(
                        pS[hi][0:64, cs * 512:(cs + 1) * 512], c["ones1b"][0:1, 0:64],
                        rrb[hi][0:1, cs * 512:(cs + 1) * 512], start=True, stop=True),
                        deps=[t_rb] + ([f.get(("pS", hi))] if cs == 0 else []), sem=te))
                t_rB = P.task("vector", lambda e, hi=hi: e.tensor_copy(
                    rBsb[hi][:], pS[hi][0:64, :]),
                    deps=[t_bc[-1], f.get(("rBsb", hi))], sem=ve)
                f[("pS", hi)] = t_rB
                t_o = P.task("vector", lambda e, hi=hi: e.tensor_mul(
                    outst[hi][:], outst32[hi][:], rBsb[hi][:]),
                    deps=[t_rB, f.get(("outst", hi))], sem=ve)
                f[("srow", hi)] = t_rc
                f[("rrow", hi)] = t_rb
                f[("rrb", hi)] = t_bc[-1]
                f[("rBsb", hi)] = t_o
                f[("o32", hi)] = t_o
                t_w = P.task("sync", lambda e, h=h, q0=q0, hi=hi: e.dma_start(
                    out=outT[h * 64:(h + 1) * 64, q0:q0 + QG], in_=outst[hi][:]),
                    deps=[t_o], sem=f"b2_so{hi}", amt=16)
                f[("outst", hi)] = t_w

            for kc in range(NK):
                kp = kc % 2
                for hi in (0, 1):
                    h = heads[hi]
                    hr = slice((h % 2) * 64, (h % 2) * 64 + 64)
                    qcol = (h // 2) * QG
                    t_s = []
                    for cs in range(NCS):
                        t_s.append(P.task("tensor", lambda e, h=h, kc=kc, cs=cs,
                                          hi=hi, hr=hr, qcol=qcol: e.matmul(
                            pS[hi][:, cs * 512:(cs + 1) * 512],
                            kTsb[h // 2][hr, kc * 128:(kc + 1) * 128],
                            qsg[hr, qcol + cs * 512: qcol + (cs + 1) * 512],
                            start=True, stop=True),
                            deps=([t_lq_all, t_lk[-1], f.get(("pS", hi))]
                                  if cs == 0 else []), sem=te))
                    t_e = P.task("scalar", lambda e, hi=hi, kp=kp: e.activation(
                        expS[hi][kp][:], pS[hi][:], AF.Exp),
                        deps=[t_s[-1], f.get(("expS", hi, kp))], sem=sc)
                    f[("pS", hi)] = t_e
                    t_ex[hi].append(t_e)
                if kc > 0:
                    avphase(kc - 1, 1 - kp)
                if kc == NFL and pend_norm:
                    for args in pend_norm:
                        norm_tail(*args)
                    pend_norm = []
            avphase(NK - 1, (NK - 1) % 2)
            for hi in (0, 1):
                t_sr = P.task("vector", lambda e, hi=hi: e.tensor_copy(
                    srow[hi][:], pAV[hi][64:65, :]),
                    deps=[t_av[hi][-1], f.get(("srow", hi))], sem=ve)
                t_oc = P.task("vector", lambda e, hi=hi: e.tensor_copy(
                    outst32[hi][:], pAV[hi][0:64, :]),
                    deps=[t_av[hi][-1], f.get(("o32", hi))], sem=ve)
                f[("pAV", hi)] = t_oc
                pend_norm.append((heads[hi], hi, [t_sr, t_oc]))
            if d0gen is not None:
                for _ in range(8):
                    try:
                        next(d0gen)
                    except StopIteration:
                        break
        for args in pend_norm:
            norm_tail(*args)
        pend_norm = []
        f["qsg"] = P.last_tok["tensor"]
    return None



def emit_stageB3(P, nc, es, c, io, T, ind8sb_unused):
    NT = T // 512
    outT, xnT, x1T = io["outT"], io["xnT"], io["x1T"]

    sb = lambda n, s, d=F32: es.enter_context(nc.sbuf_tensor(n, s, d))
    ps = lambda n, s, d=F32: es.enter_context(nc.psum_tensor(n, s, d))

    gwb = sb("b3_gw", [128, 8192], BF16)
    owb = sb("b3_ow", [128, 8192], BF16)
    mgwb = sb("b3_mgw", [128, 64], BF16)
    gbs = sb("b3_gb", [128, 8])
    n1gs = sb("b3_n1g", [128, 8])
    n1bs = sb("b3_n1b", [128, 8])
    mgbs = sb("b3_mgb", [1, 8])
    mgbB = sb("b3_mgbB", [128, 8])
    t_w = []
    for wsb, wname in [(gwb, "gw"), (owb, "ow")]:
        for ci in range(8):
            t_w.append(P.task("gpsimd", lambda e, wsb=wsb, wname=wname, ci=ci:
                e.dma_start(out=wsb[:, ci * 1024:(ci + 1) * 1024],
                            in_=io[wname][ci * 128:(ci + 1) * 128, :]),
                sem="b3_gw", amt=16))
    for ci in range(8):
        t_w.append(P.task("gpsimd", lambda e, ci=ci: e.dma_start(
            out=mgwb[:, ci * 8:(ci + 1) * 8],
            in_=io["mgw"][ci * 128:(ci + 1) * 128, :]), sem="b3_gw", amt=16))
    t_v = []
    for tsb, name in [(gbs, "gb"), (n1gs, "n1g"), (n1bs, "n1b"), (mgbs, "mgb")]:
        t_v.append(P.task("sync", lambda e, tsb=tsb, name=name: e.dma_start(
            out=tsb[:, :], in_=io[name]), sem="b3_sv", amt=16))

    oTsb = [sb(f"b3_oT{k}", [128, 4096], BF16) for k in range(2)]
    xnTsb = [sb(f"b3_xnT{k}", [128, 4096], BF16) for k in range(2)]
    gTst = sb("b3_gT", [128, 512], BF16)
    ogsb = sb("b3_og", [128, 4096], BF16)
    zsb = sb("b3_z", [128, 4096], BF16)
    x1sb = [sb(f"b3_x1{k}", [128, 4096], BF16) for k in range(2)]
    lgsb = sb("b3_lg", [128, 8])
    m8 = sb("b3_m8", [128, 8])
    i8 = sb("b3_i8", [128, 8], U32)
    chu = sb("b3_chu", [128, 1], U32)
    chb = sb("b3_chb", [128, 1], BF16)
    chrow = sb("b3_chrow", [1, 512], BF16)
    indloc = [sb(f"b3_indloc{k}", [1, 512], BF16) for k in range(2)]
    aux = {
        "sq2": [sb("b3_sq0", [128, 512], BF16), sb("b3_sq1", [128, 512], BF16)],
        "rows": [sb(f"b3_row{i}", [1, 512]) for i in range(4)],
    }
    pSrow = ps("b3_pSrow", [64, 512])
    aux["pS1"], aux["pS2"] = pSrow[0:1, :], pSrow[32:33, :]
    aux["pB1"] = ps("b3_pB1", [128, 512])
    aux["pB2"] = ps("b3_pB2", [128, 512])
    pG1 = ps("b3_pG1", [128, 512])
    pG2 = ps("b3_pG2", [128, 512])
    pM = ps("b3_pM", [128, 512])
    pChB = ps("b3_pChB", [1, 128], BF16)

    te, ve, sc = "b3_te", "b3_ve", "b3_sc"
    f = {}

    t_mb = P.task("tensor", lambda e: e.matmul(
        pM[:, 0:8], c["ones1f"], mgbs[:], start=True, stop=True),
        deps=[t_v[-1]], sem=te)
    t_mb2 = P.task("vector", lambda e: e.tensor_copy(mgbB[:, 0:8], pM[:, 0:8]),
                   deps=[t_mb], sem=ve)
    f["pM"] = t_mb2

    for g in range(NT):
        k = g % 2
        t0 = g * 512
        t_lo = []
        t_lx = []
        for ci in range(8):
            t_lo.append(P.task("sync", lambda e, ci=ci, t0=t0, k=k: e.dma_start(
                out=oTsb[k][:, ci * 512:(ci + 1) * 512],
                in_=outT[ci * 128:(ci + 1) * 128, t0:t0 + 512]),
                deps=[f.get(("oT", k))], sem=f"b3_so{k}", amt=16))
            t_lx.append(P.task("sync", lambda e, ci=ci, t0=t0, k=k: e.dma_start(
                out=xnTsb[k][:, ci * 512:(ci + 1) * 512],
                in_=xnT[ci * 128:(ci + 1) * 128, t0:t0 + 512]),
                deps=[f.get(("xnT", k))], sem=f"b3_sx{k}", amt=16))
        t_lo_all, t_lx_all = t_lo[-1], t_lx[-1]

        t_og = []
        for cp in range(8):
            t_gm = []
            for ci in range(8):
                t_gm.append(P.task("tensor", lambda e, ci=ci, cp=cp, k=k: e.matmul(
                    pG1[:], gwb[:, ci * 1024 + cp * 128: ci * 1024 + (cp + 1) * 128],
                    oTsb[k][:, ci * 512:(ci + 1) * 512],
                    start=(ci == 0), stop=(ci == 7)),
                    deps=[t_lo_all, t_w[-1]] + ([f.get("pG1")] if ci == 0 else []),
                    sem=te))
            t_sg = P.task("scalar", lambda e, cp=cp: e.activation(
                gTst[:], pG1[:], AF.Sigmoid, bias=gbs[:, cp:cp + 1]),
                deps=[t_gm[-1], f.get("gTst")], sem=sc)
            f["pG1"] = t_sg
            t_og.append(P.task("vector", lambda e, cp=cp, k=k: e.tensor_mul(
                ogsb[:, cp * 512:(cp + 1) * 512], oTsb[k][:, cp * 512:(cp + 1) * 512],
                gTst[:]), deps=[t_sg, f.get(("og", cp))], sem=ve))
            f["gTst"] = t_og[-1]

        t_z = []
        for cp in range(8):
            t_am = []
            for ci in range(8):
                t_am.append(P.task("tensor", lambda e, ci=ci, cp=cp: e.matmul(
                    pG2[:], owb[:, ci * 1024 + cp * 128: ci * 1024 + (cp + 1) * 128],
                    ogsb[:, ci * 512:(ci + 1) * 512],
                    start=(ci == 0), stop=(ci == 7)),
                    deps=[t_og[ci]] + ([f.get("pG2")] if ci == 0 else []), sem=te))
            t_z.append(P.task("vector", lambda e, cp=cp, k=k: e.tensor_add(
                zsb[:, cp * 512:(cp + 1) * 512], pG2[:],
                xnTsb[k][:, cp * 512:(cp + 1) * 512]),
                deps=[t_am[-1], t_lx_all, f.get(("z", cp))], sem=ve))
            f["pG2"] = t_z[-1]
        for cp in range(8):
            f[("og", cp)] = P.last_tok["tensor"]

        chunks_in = [(zsb[:, ci * 512:(ci + 1) * 512], t_z[ci]) for ci in range(8)]
        chunks_out = [x1sb[k][:, ci * 512:(ci + 1) * 512] for ci in range(8)]
        gain = [n1gs[:, ci:ci + 1] for ci in range(8)]
        bias = [n1bs[:, ci:ci + 1] for ci in range(8)]
        t_x1 = emit_pln(P, nc, c, "b3_", 512, chunks_in, chunks_out, aux,
                        deps_in=[f.get(("x1", k)), t_v[-1]], gain=gain, bias=bias)
        for ci in range(8):
            f[("z", ci)] = [t_x1[-1]] + [P.last_tok["tensor"]]
            P.task("sync", lambda e, ci=ci, t0=t0, k=k: e.dma_start(
                out=x1T[ci * 128:(ci + 1) * 128, t0:t0 + 512],
                in_=x1sb[k][:, ci * 512:(ci + 1) * 512]),
                deps=[t_x1[ci]], sem=f"b3_sw{k}", amt=16)
        t_x1w = P.last_tok["sync"]

        t_tile = []
        for tt_ in range(4):
            t_lm = []
            for ci in range(8):
                t_lm.append(P.task("tensor", lambda e, ci=ci, tt_=tt_, k=k: e.matmul(
                    pM[:, 0:8],
                    x1sb[k][:, ci * 512 + tt_ * 128: ci * 512 + (tt_ + 1) * 128],
                    mgwb[:, ci * 8:(ci + 1) * 8],
                    start=(ci == 0), stop=(ci == 7)),
                    deps=[t_x1[ci]] + ([f.get("pM")] if ci == 0 else []), sem=te))
            t_lg = P.task("vector", lambda e: e.tensor_add(
                lgsb[:], pM[:, 0:8], mgbB[:, 0:8]),
                deps=[t_lm[-1], t_mb2, f.get("lgsb")], sem=ve)
            f["pM"] = t_lg
            t_m8 = P.task("vector", lambda e: e.max(m8[:], lgsb[:]),
                          deps=[t_lg, f.get("m8")], sem=ve)
            t_i8 = P.task("vector", lambda e: e.max_index(i8[:], m8[:], lgsb[:]),
                          deps=[t_m8, f.get("i8")], sem=ve)
            f["lgsb"] = t_i8
            f["m8"] = t_i8
            t_ch = P.task("vector", lambda e: e.tensor_tensor(
                chu[:], i8[:, 0:1], i8[:, 1:2], ALU.max),
                deps=[t_i8, f.get("chu")], sem=ve)
            f["i8"] = t_ch
            t_cb = P.task("vector", lambda e: e.tensor_copy(chb[:], chu[:]),
                          deps=[t_ch, f.get("chb")], sem=ve)
            f["chu"] = t_cb
            t_tr = P.task("tensor", lambda e: e.transpose(
                pChB[:], chb[:], c["eyeb"]), deps=[t_cb, f.get("pChB")], sem=te)
            f["chb"] = t_tr
            t_cr = P.task("vector", lambda e, tt_=tt_: e.tensor_copy(
                chrow[0:1, tt_ * 128:(tt_ + 1) * 128], pChB[:]),
                deps=[t_tr] + ([f.get("chrow")] if tt_ == 0 else []), sem=ve)
            f["pChB"] = t_cr
            t_tile.append(t_cr)
        t_i = None
        for ei in range(8):
            ke = ei % 2
            t_i = P.task("vector", lambda e, ei=ei, t0=t0, ke=ke: e.tensor_scalar(
                indloc[ke][:], chrow[:], float(ei), None, ALU.is_equal),
                deps=[t_tile[-1], f.get(("indloc", ke))], sem=ve)
            t_iw = P.task("sync", lambda e, ei=ei, t0=t0, ke=ke: e.dma_start(
                out=io["indh"][0:1, ei * T + t0: ei * T + t0 + 512],
                in_=indloc[ke][:]), deps=[t_i], sem=f"b3_si{ke}", amt=16)
            f[("indloc", ke)] = t_iw
        f["chrow"] = t_i
        f[("oT", k)] = [P.last_tok["tensor"]]
        f[("xnT", k)] = [t_z[-1]]
        f[("x1", k)] = [P.last_tok["tensor"], t_x1w]
    return None



def emit_stageB4(P, nc, es, c, io, T, ind8sb):
    NT = T // 512
    x1T, y1T, y2T = io["x1T"], io["y1T"], io["y2T"]

    sb = lambda n, s, d=F32: es.enter_context(nc.sbuf_tensor(n, s, d))
    ps = lambda n, s, d=F32: es.enter_context(nc.psum_tensor(n, s, d))

    macc = [sb(f"b4_ma_{i}", [128, T], BF16) for i in range(8)]
    x1g = [sb(f"b4_x1g{k}", [128, 4096], BF16) for k in range(2)]
    indB2 = [sb(f"b4_indB{k}", [128, 512], BF16) for k in range(2)]
    w1sb = sb("b4_w1", [128, 16384], BF16)
    w2sb = sb("b4_w2", [128, 16384], BF16)
    heb2 = [sb("b4_he0", [128, 8192], BF16), sb("b4_he1", [128, 8192], BF16)]
    sqb = sb("b4_sqb", [128, 512], BF16)
    b1s = sb("b4_b1", [128, 16])
    lgs = sb("b4_lg", [128, 16])
    lbs = sb("b4_lb", [128, 16])
    b2s = sb("b4_b2", [128, 8])
    rscs = sb("b4_rsc", [1, 8])
    rscB = sb("b4_rscB", [128, 8])
    n2gs = sb("b4_n2g", [128, 8])
    n2bs = sb("b4_n2b", [128, 8])
    indB = indB2
    tmpo = sb("b4_tmpo", [128, 512], BF16)
    zq = heb2[0][:, 0:4096]
    ffn = heb2[0][:, 4096:8192]
    y1sb = w1sb[:, 0:4096]
    y2sb = w1sb[:, 4096:8192]
    _br = [sb(f"b4_row{i}", [1, 512]) for i in range(3)]
    aux = {
        "sq2": [sqb, sb("b4_sqb1", [128, 512], BF16)],
        "rows": [_br[0], _br[1], _br[2], _br[1]],
    }
    pSrow = ps("b4_pSrow", [64, 512])
    aux["pS1"], aux["pS2"] = pSrow[0:1, :], pSrow[32:33, :]
    aux["pB1"] = ps("b4_pB1", [128, 512])
    aux["pB2"] = ps("b4_pB2", [128, 512])
    pH = [ps(f"b4_pH{k}", [128, 512]) for k in range(2)]
    pO = [ps(f"b4_pO{k}", [128, 512]) for k in range(2)]

    te, ve, sc = "b4_te", "b4_ve", "b4_sc"
    f = {}

    x1r = x1T.rearrange("(c p) t -> p c t", p=128)
    t_lv = []
    for tsb, name in [(rscs, "rsc"), (n2gs, "n2g"), (n2bs, "n2b")]:
        t_lv.append(P.task("sync", lambda e, tsb=tsb, name=name: e.dma_start(
            out=tsb[:, :], in_=io[name]), sem="b4_sv", amt=16))
    t_rb = P.task("tensor", lambda e: e.matmul(
        pH[0][:, 0:8], c["ones1f"], rscs[:], start=True, stop=True),
        deps=[t_lv[-1]], sem=te)
    t_rb2 = P.task("vector", lambda e: e.tensor_copy(rscB[:, 0:8], pH[0][:, 0:8]),
                   deps=[t_rb], sem=ve)
    f[("pH", 0)] = t_rb2
    t_mz = []
    for ci in range(8):
        t_mz.append(P.task("vector", lambda e, ci=ci: e.memset(
            macc[ci][:, :], 0.0), sem=ve))

    def load_w1(ei, dep):
        tok = P.task("sync", lambda e, ei=ei: e.dma_start(
            out=w1sb.rearrange("p (ci m) -> p ci m", ci=8),
            in_=io["w1b"][ei].rearrange("(ci p) m -> p ci m", p=128)),
            deps=[dep], sem="b4_g1", amt=16)
        f["w1tok"] = tok
        return tok

    load_w1(0, None)
    for ei in range(8):
        par = ei % 2
        t_w1 = [f["w1tok"]]
        t_w2 = [P.task("sync", lambda e, ei=ei: e.dma_start(
            out=w2sb.rearrange("p (fc m) -> p fc m", fc=16),
            in_=io["w2b"][ei].rearrange("(fc p) m -> p fc m", p=128)),
            deps=[f.get("w2sb")], sem="b4_g2", amt=16)]
        t_vecs = []
        for tsb, name in [(b1s, "b1r"), (lgs, "lgr"), (lbs, "lbr"), (b2s, "b2r")]:
            t_vecs.append(P.task("sync", lambda e, tsb=tsb, name=name, ei=ei:
                e.dma_start(out=tsb[:, :], in_=io[name][ei]),
                deps=[f.get("vecs")], sem="b4_se", amt=16))
        t_w1a, t_w2a, t_va = t_w1[-1], t_w2[-1], t_vecs[-1]

        def ophase(kg, tgs, t_zg, t_ibc, ei):
            for cp in range(8):
                kp = cp % 2
                t_om = []
                for fc in range(16):
                    t_om.append(P.task("tensor", lambda e, fc=fc, cp=cp, kp=kp,
                                       kg=kg: e.matmul(
                        pO[kp][:],
                        w2sb[:, fc * 1024 + cp * 128: fc * 1024 + (cp + 1) * 128],
                        heb2[kg][:, fc * 512:(fc + 1) * 512],
                        start=(fc == 0), stop=(fc == 15)),
                        deps=[t_zg[fc], t_w2a] + ([f.get(("pO", kp))] if fc == 0 else []),
                        sem=te))
                t_o1 = P.task("vector", lambda e, kp=kp, cp=cp, ei=ei: e.tensor_scalar(
                    tmpo[:], pO[kp][:], b2s[:, cp:cp + 1], rscB[:, ei:ei + 1],
                    ALU.add, ALU.mult),
                    deps=[t_om[-1], t_rb2, t_va, f.get("tmpo")], sem=ve)
                f[("pO", kp)] = t_o1
                t_o2 = P.task("vector", lambda e, kg=kg: e.tensor_mul(
                    tmpo[:], tmpo[:], indB[kg][:]),
                    deps=[t_o1, t_ibc], sem=ve)
                t_o3 = P.task("vector", lambda e, cp=cp, tgs=tgs: e.tensor_add(
                    macc[cp][:, tgs], macc[cp][:, tgs], tmpo[:]),
                    deps=[t_o2, t_mz[cp]], sem=ve)
                f["tmpo"] = t_o3
            f[("indB", kg)] = t_o2
            for fc in range(16):
                f[("heb", kg, fc)] = P.last_tok["tensor"]

        pend = None
        last_te_of_e = None
        for g in range(NT):
            kg = g % 2
            t0 = g * 512
            tgs = slice(t0, t0 + 512)
            t_lx1 = P.task("sync" if kg == 0 else "gpsimd",
                           lambda e, t0=t0, kg=kg: e.dma_start(
                out=x1g[kg].rearrange("p (c t) -> p c t", c=8),
                in_=x1r[:, :, t0:t0 + 512]),
                deps=[f.get(("x1g", kg))], sem=f"b4_sx{kg}", amt=16)
            t_ibc = P.task("gpsimd", lambda e, ei=ei, t0=t0, kg=kg: e.dma_start(
                out=indB[kg][:],
                in_=bass.AP(io["indh"].tensor, ei * T + t0, [[0, 128], [1, 512]])),
                deps=[f.get(("indB", kg))], sem=f"b4_gi{kg}", amt=16)

            t_he = []
            for fc in range(16):
                kp = fc % 2
                t_hm = []
                for ci in range(8):
                    t_hm.append(P.task("tensor", lambda e, ei=ei, fc=fc, ci=ci,
                                       kp=kp, kg=kg, par=par, tgs=tgs: e.matmul(
                        pH[kp][:],
                        w1sb[:, ci * 2048 + fc * 128: ci * 2048 + (fc + 1) * 128],
                        x1g[kg][:, ci * 512:(ci + 1) * 512],
                        start=(ci == 0), stop=(ci == 7)),
                        deps=([t_w1a, t_lx1, f.get(("pH", kp))] if ci == 0 else []),
                        sem=te))
                t_he.append(P.task("vector", lambda e, fc=fc, kp=kp, kg=kg: e.tensor_scalar(
                    heb2[kg][:, fc * 512:(fc + 1) * 512], pH[kp][:], b1s[:, fc:fc + 1],
                    None, ALU.add),
                    deps=[t_hm[-1], t_va, f.get(("heb", kg, fc))], sem=ve))
                f[("pH", kp)] = t_he[-1]
            f[("x1g", kg)] = [P.last_tok["tensor"]]

            t_he_last_mm = P.last_tok["tensor"]
            if pend is not None:
                ophase(*pend)

            chunks_in = [(heb2[kg][:, fc * 512:(fc + 1) * 512], t_he[fc])
                         for fc in range(16)]
            chunks_out = [heb2[kg][:, fc * 512:(fc + 1) * 512] for fc in range(16)]
            gain = [lgs[:, fc:fc + 1] for fc in range(16)]
            bias = [lbs[:, fc:fc + 1] for fc in range(16)]
            t_zg = emit_pln(P, nc, c, "b4_", 512, chunks_in, chunks_out, aux,
                            deps_in=[t_va], gain=gain, bias=bias, fuse_act=AF.Gelu)
            pend = (kg, tgs, t_zg, t_ibc, ei)
            if g == NT - 1 and ei + 1 < 8:
                load_w1(ei + 1, t_he_last_mm)
        ophase(*pend)
        last_te_of_e = P.last_tok["tensor"]
        f["w2sb"] = last_te_of_e
        f["vecs"] = P.last_tok["vector"]

    for g in range(NT):
        kg = g % 2
        t0 = g * 512
        tgs = slice(t0, t0 + 512)
        t_lx1 = P.task("sync", lambda e, t0=t0, kg=kg: e.dma_start(
            out=x1g[kg].rearrange("p (c t) -> p c t", c=8),
            in_=x1r[:, :, t0:t0 + 512]),
            deps=[f.get(("x1g", kg))], sem=f"b4_sx{kg}", amt=16)
        t_ly = []
        for ci in range(8):
            t_ly.append(P.task("sync", lambda e, ci=ci, t0=t0: e.dma_start(
                out=y1sb[:, ci * 512:(ci + 1) * 512],
                in_=y1T[ci * 128:(ci + 1) * 128, t0:t0 + 512]),
                deps=[f.get("y1sb"), f.get("w1sb"), f.get("vecs")],
                sem="b4_sy", amt=16))
        t_zq = []
        for ci in range(8):
            t_zq.append(P.task("vector", lambda e, ci=ci, kg=kg, tgs=tgs:
                e.scalar_tensor_tensor(
                    zq[:, ci * 512:(ci + 1) * 512],
                    x1g[kg][:, ci * 512:(ci + 1) * 512], 2.0,
                    macc[ci][:, tgs], ALU.mult, ALU.add),
                deps=[P.last_tok["vector"], t_lx1, f.get(("zq", ci)),
                      f.get(("heb", 0, 0)), f.get(("heb", 0, 15)),
                      f.get(("heb", 1, 0)), f.get(("heb", 1, 15))], sem=ve))
        f[("x1g", kg)] = [t_zq[-1]]
        chunks_in = [(zq[:, ci * 512:(ci + 1) * 512], t_zq[ci]) for ci in range(8)]
        chunks_out = [ffn[:, ci * 512:(ci + 1) * 512] for ci in range(8)]
        gain = [n2gs[:, ci:ci + 1] for ci in range(8)]
        bias = [n2bs[:, ci:ci + 1] for ci in range(8)]
        t_f = emit_pln(P, nc, c, "b4_", 512, chunks_in, chunks_out, aux,
                       deps_in=[t_lv[-1], f.get("ffnw")], gain=gain, bias=bias)
        for ci in range(8):
            t_y2 = P.task("vector", lambda e, ci=ci: e.tensor_add(
                y2sb[:, ci * 512:(ci + 1) * 512], ffn[:, ci * 512:(ci + 1) * 512],
                y1sb[:, ci * 512:(ci + 1) * 512]),
                deps=[t_f[ci], t_ly[-1], f.get(("y2", ci))], sem=ve)
            t_wr = P.task("sync", lambda e, ci=ci, t0=t0: e.dma_start(
                out=y2T[ci * 128:(ci + 1) * 128, t0:t0 + 512],
                in_=y2sb[:, ci * 512:(ci + 1) * 512]),
                deps=[t_y2], sem="b4_sw", amt=16)
            f[("y2", ci)] = t_wr
            f[("zq", ci)] = t_f[-1]
        f["y1sb"] = [t_y2]
        f["ffnw"] = t_wr
    return None



def emit_stageC(P, nc, es, c, io, T):
    NT = T // 512
    NL = T // 128
    y2T, xn3T, y3T = io["y2T"], io["xn3T"], io["y3T"]

    sb = lambda n, s, d=F32: es.enter_context(nc.sbuf_tensor(n, s, d))

    vC = sb("c_v", [128, NL * 1024], BF16)
    medb = sb("c_med", [128, 512], BF16)
    mkv_all = sb("c_mkv", [64, 1024], BF16)
    t_w = []
    for ci in range(8):
        t_w.append(P.task("gpsimd", lambda e, ci=ci: e.dma_start(
            out=medb[:, ci * 64:(ci + 1) * 64],
            in_=io["medT"][ci * 128:(ci + 1) * 128, :]), sem="c_gw", amt=16))

    te, ve, sc = "c_te", "c_ve", "c_sc"
    f = {}

    es12 = ExitStack()
    kCT = [es12.enter_context(nc.sbuf_tensor(f"c_kT{i}", [128, T], BF16))
           for i in range(8)]
    with ExitStack() as es1:
        sb1 = lambda n, s, d=F32: es1.enter_context(nc.sbuf_tensor(n, s, d))
        ps1 = lambda n, s, d=F32: es1.enter_context(nc.psum_tensor(n, s, d))
        wkvb = sb1("c_wkv", [128, 16384], BF16)
        for ci in range(8):
            t_w.append(P.task("gpsimd", lambda e, ci=ci: e.dma_start(
                out=wkvb[:, ci * 2048:(ci + 1) * 2048],
                in_=io["wkv"][ci * 128:(ci + 1) * 128, :]), sem="c_gw", amt=16))
        t_w_all = t_w[-1]
        _y2t = sb1("c_y2", [128, 4096], BF16)
        _xnt = sb1("c_xn", [128, 4096], BF16)
        y2sb = [_y2t, _y2t]
        xnsb = [_xnt, _xnt]
        aux = {
            "sq2": [sb1("c_sq0", [128, 512], BF16), sb1("c_sq1", [128, 512], BF16)],
            "rows": [sb1(f"c_row{i}", [1, 512]) for i in range(4)],
        }
        pSrow = ps1("c_pSrow", [64, 512])
        aux["pS1"], aux["pS2"] = pSrow[0:1, :], pSrow[32:33, :]
        aux["pB1"] = ps1("c_pB1", [128, 512])
        aux["pB2"] = ps1("c_pB2", [128, 512])
        pKV = [ps1(f"c_pKV{k}", [128, 512]) for k in range(2)]
        for g in range(NT):
            k = g % 2
            t0 = g * 512
            t_ld = []
            for ci in range(8):
                t_ld.append(P.task("sync", lambda e, ci=ci, t0=t0, k=k: e.dma_start(
                    out=y2sb[k][:, ci * 512:(ci + 1) * 512],
                    in_=y2T[ci * 128:(ci + 1) * 128, t0:t0 + 512]),
                    deps=[f.get(("y2",))], sem=f"c_sl{k}", amt=16))
            t_ld_all = t_ld[-1]
            chunks_in = [(y2sb[k][:, ci * 512:(ci + 1) * 512], t_ld_all) for ci in range(8)]
            chunks_out = [xnsb[k][:, ci * 512:(ci + 1) * 512] for ci in range(8)]
            t_xn = emit_pln(P, nc, c, "c_", 512, chunks_in, chunks_out, aux,
                            deps_in=[f.get(("xn",))])
            for ci in range(8):
                P.task("sync", lambda e, ci=ci, t0=t0, k=k: e.dma_start(
                    out=xn3T[ci * 128:(ci + 1) * 128, t0:t0 + 512],
                    in_=xnsb[k][:, ci * 512:(ci + 1) * 512]),
                    deps=[t_xn[ci]], sem=f"c_sx{k}", amt=16)
            t_xnw = P.last_tok["sync"]
            for cp in range(8):
                kq = cp % 2
                t_km = []
                for ci in range(8):
                    t_km.append(P.task("tensor", lambda e, ci=ci, cp=cp, k=k, kq=kq: e.matmul(
                        pKV[kq][:], wkvb[:, ci * 2048 + cp * 128: ci * 2048 + (cp + 1) * 128],
                        xnsb[k][:, ci * 512:(ci + 1) * 512],
                        start=(ci == 0), stop=(ci == 7)),
                        deps=[t_xn[ci]] + ([t_w_all, f.get(("pKV", kq))] if ci == 0 else []),
                        sem=te))
                t_ke = P.task("vector", lambda e, cp=cp, t0=t0, kq=kq: e.tensor_copy(
                    kCT[cp][:, t0:t0 + 512], pKV[kq][:]), deps=[t_km[-1]], sem=ve)
                f[("pKV", kq)] = t_ke
            for tt_ in range(4):
                ta = t0 // 128 + tt_
                for cg in range(2):
                    kq = cg % 2
                    t_vm = []
                    for ci in range(8):
                        t_vm.append(P.task("tensor", lambda e, ci=ci, tt_=tt_, cg=cg,
                                           k=k, kq=kq: e.matmul(
                            pKV[kq][:],
                            xnsb[k][:, ci * 512 + tt_ * 128: ci * 512 + (tt_ + 1) * 128],
                            wkvb[:, ci * 2048 + 1024 + cg * 512: ci * 2048 + 1024 + (cg + 1) * 512],
                            start=(ci == 0), stop=(ci == 7)),
                            deps=[f.get(("pKV", kq))] if ci == 0 else [], sem=te))
                    t_ve_ = P.task("vector", lambda e, ta=ta, cg=cg, kq=kq: e.tensor_copy(
                        vC[:, ta * 1024 + cg * 512: ta * 1024 + (cg + 1) * 512],
                        pKV[kq][:]), deps=[t_vm[-1]], sem=ve)
                    f[("pKV", kq)] = t_ve_
            f[("y2",)] = [t_xn[-1], P.last_tok["tensor"]]
            f[("xn",)] = [P.last_tok["tensor"], t_xnw]
        t_c1 = [P.last_tok[e] for e in ("tensor", "vector", "sync")]
    P.barrier()

    with ExitStack() as es2:
        sb2 = lambda n, s, d=F32: es2.enter_context(nc.sbuf_tensor(n, s, d))
        ps2 = lambda n, s, d=F32: es2.enter_context(nc.psum_tensor(n, s, d))
        expMK = [sb2(f"c_eMK{k}", [128, NL * 64], BF16) for k in range(2)]
        srow = sb2("c_ms", [1, 64])
        rrow = sb2("c_mr", [1, 64])
        rrowb = sb2("c_mrb", [1, 64], BF16)
        rcol = sb2("c_rcol", [64, 1])
        pMS = [ps2(f"c_pMS{k}", [128, 512]) for k in range(2)]
        pMKV = ps2("c_pMKV", [64, 64])
        pSums = ps2("c_pSums", [64, 64])
        pTc = ps2("c_pTc", [64, 64], BF16)
        for h in range(16):
            kh = h % 2
            hr = slice((h % 2) * 64, (h % 2) * 64 + 64)
            mc = (h // 2) * 64
            t_ex = []
            for lc in range(NL):
                kb = lc % 2
                t_ms = P.task("tensor", lambda e, h=h, lc=lc, kb=kb, hr=hr,
                              mc=mc: e.matmul(
                    pMS[kb][:, 0:64],
                    kCT[h // 2][hr, lc * 128:(lc + 1) * 128],
                    medb[hr, mc:mc + 64], start=True, stop=True),
                    deps=t_c1 + [f.get(("pMS", kb))], sem=te)
                t_e = P.task("scalar", lambda e, lc=lc, kb=kb, kh=kh: e.activation(
                    expMK[kh][:, lc * 64:(lc + 1) * 64], pMS[kb][:, 0:64],
                    AF.Exp, scale=0.125),
                    deps=[t_ms, f.get(("expMK", kh))] if lc == 0 else [t_ms], sem=sc)
                f[("pMS", kb)] = t_e
                t_ex.append(t_e)
            t_sm = []
            t_mk = []
            for lc in range(NL):
                t_sm.append(P.task("tensor", lambda e, lc=lc, kh=kh: e.matmul(
                    pSums[0:1, :], c["onescb"], expMK[kh][:, lc * 64:(lc + 1) * 64],
                    start=(lc == 0), stop=(lc == NL - 1)),
                    deps=[t_ex[lc]] + ([f.get("pSums")] if lc == 0 else []), sem=te))
                t_mk.append(P.task("tensor", lambda e, lc=lc, kh=kh, h=h: e.matmul(
                    pMKV[:], expMK[kh][:, lc * 64:(lc + 1) * 64],
                    vC[:, lc * 1024 + h * 64: lc * 1024 + (h + 1) * 64],
                    start=(lc == 0), stop=(lc == NL - 1)),
                    deps=[t_ex[lc]] + ([f.get("pMKV")] if lc == 0 else []), sem=te))
            f[("expMK", kh)] = [t_sm[-1], t_mk[-1]]
            t_s1 = P.task("vector", lambda e: e.tensor_copy(srow[:], pSums[0:1, :]),
                          deps=[t_sm[-1], f.get("srow")], sem=ve)
            f["pSums"] = t_s1
            t_r1 = P.task("vector", lambda e: e.reciprocal(rrow[:], srow[:]),
                          deps=[t_s1, f.get("rrow")], sem=ve)
            f["srow"] = t_r1
            t_r2 = P.task("vector", lambda e: e.tensor_copy(rrowb[:], rrow[:]),
                          deps=[t_r1, f.get("rrowb")], sem=ve)
            f["rrow"] = t_r2
            t_tr = P.task("tensor", lambda e: e.transpose(
                pTc[:, 0:1], rrowb[:], c["eyeb"][0:1, 0:1]),
                deps=[t_r2, f.get("pTc")], sem=te)
            f["rrowb"] = t_tr
            t_rc = P.task("vector", lambda e: e.tensor_copy(rcol[:], pTc[:, 0:1]),
                          deps=[t_tr, f.get("rcol")], sem=ve)
            f["pTc"] = t_rc
            t_mv = P.task("vector", lambda e, h=h: e.tensor_scalar(
                mkv_all[:, h * 64:(h + 1) * 64], pMKV[:], rcol[:, 0:1], None,
                ALU.mult), deps=[t_mk[-1], t_rc, f.get(("mkv", h))], sem=ve)
            f["pMKV"] = t_mv
            f["rcol"] = t_mv
        t_c2v = P.last_tok["vector"]
    es12.close()
    P.barrier()

    with ExitStack() as es3:
        sb3 = lambda n, s, d=F32: es3.enter_context(nc.sbuf_tensor(n, s, d))
        ps3 = lambda n, s, d=F32: es3.enter_context(nc.psum_tensor(n, s, d))
        wqb = sb3("c_wq", [128, 8192], BF16)
        wob = sb3("c_wo", [128, 8192], BF16)
        bos = sb3("c_bo", [128, 8])
        t_w2 = []
        for ci in range(8):
            t_w2.append(P.task("gpsimd", lambda e, ci=ci: e.dma_start(
                out=wqb[:, ci * 1024:(ci + 1) * 1024],
                in_=io["wq"][ci * 128:(ci + 1) * 128, :]), sem="c_gq", amt=16))
            t_w2.append(P.task("gpsimd", lambda e, ci=ci: e.dma_start(
                out=wob[:, ci * 1024:(ci + 1) * 1024],
                in_=io["wo"][ci * 128:(ci + 1) * 128, :]), sem="c_gq", amt=16))
        t_bo = P.task("sync", lambda e: e.dma_start(out=bos[:, :], in_=io["bo"]),
                      sem="c_sb", amt=16)
        t_w2_all = t_w2[-1]

        xng = [sb3(f"c_xng{k}", [128, 4096], BF16) for k in range(2)]
        y2g = [sb3(f"c_y2g{k}", [128, 4096], BF16) for k in range(2)]
        qTg = sb3("c_qTg", [128, 4096], BF16)
        oCT = sb3("c_oCT", [128, 4096], BF16)
        expQM = [sb3(f"c_eQM{k}", [64, 512], BF16) for k in range(2)]
        qsr = [sb3(f"c_qsr{k}", [1, 512]) for k in range(2)]
        qrr = [sb3(f"c_qrr{k}", [1, 512]) for k in range(2)]
        qrb = [sb3(f"c_qrb{k}", [1, 512], BF16) for k in range(2)]
        rb2 = [sb3(f"c_rb2{k}", [64, 512], BF16) for k in range(2)]
        y3sb = sb3("c_y3", [128, 4096], BF16)
        pQ = [ps3(f"c_pQ{k}", [128, 512]) for k in range(2)]
        pQM = ps3("c_pQM", [64, 512])
        pQS2 = ps3("c_pQS", [33, 512])
        pRB2 = ps3("c_pRB2", [64, 512])
        pOU = [ps3(f"c_pOU{k}", [64, 512]) for k in range(2)]
        pA = ps3("c_pA", [128, 512])

        for g in range(NT):
            k = g % 2
            t0 = g * 512
            t_lx = []
            t_ly = []
            for ci in range(8):
                t_lx.append(P.task("sync", lambda e, ci=ci, t0=t0, k=k: e.dma_start(
                    out=xng[k][:, ci * 512:(ci + 1) * 512],
                    in_=xn3T[ci * 128:(ci + 1) * 128, t0:t0 + 512]),
                    deps=[f.get(("xng", k))], sem=f"c_s3{k}", amt=16))
                t_ly.append(P.task("sync", lambda e, ci=ci, t0=t0, k=k: e.dma_start(
                    out=y2g[k][:, ci * 512:(ci + 1) * 512],
                    in_=y2T[ci * 128:(ci + 1) * 128, t0:t0 + 512]),
                    deps=[f.get(("y2g", k))], sem=f"c_s4{k}", amt=16))
            t_lx_all, t_ly_all = t_lx[-1], t_ly[-1]
            t_qe = []
            for cp in range(8):
                kq = cp % 2
                t_qm = []
                for ci in range(8):
                    t_qm.append(P.task("tensor", lambda e, ci=ci, cp=cp, k=k, kq=kq: e.matmul(
                        pQ[kq][:], wqb[:, ci * 1024 + cp * 128: ci * 1024 + (cp + 1) * 128],
                        xng[k][:, ci * 512:(ci + 1) * 512],
                        start=(ci == 0), stop=(ci == 7)),
                        deps=([t_lx_all, t_w2_all, f.get(("pQ", kq))] if ci == 0 else []),
                        sem=te))
                t_qe.append(P.task("vector", lambda e, cp=cp, kq=kq: e.tensor_copy(
                    qTg[:, cp * 512:(cp + 1) * 512], pQ[kq][:]),
                    deps=[t_qm[-1]] + ([f.get("qTg")] if cp == 0 else []), sem=ve))
                f[("pQ", kq)] = t_qe[-1]
            t_on = []

            def c3_norm(h, kh, t_ss, t_ou):
                hr = slice((h % 2) * 64, (h % 2) * 64 + 64)
                t_s2 = P.task("vector", lambda e, kh=kh: e.tensor_copy(
                    qsr[kh][:], pQS2[kh * 32:kh * 32 + 1, :]),
                    deps=[t_ss, f.get(("qsr", kh))], sem=ve)
                f[("pQS", kh)] = t_s2
                t_r1 = P.task("vector", lambda e, kh=kh: e.reciprocal(
                    qrr[kh][:], qsr[kh][:]), deps=[t_s2, f.get(("qrr", kh))], sem=ve)
                f[("qsr", kh)] = t_r1
                t_r2 = P.task("vector", lambda e, kh=kh: e.tensor_copy(
                    qrb[kh][:], qrr[kh][:]), deps=[t_r1, f.get(("qrb", kh))], sem=ve)
                f[("qrr", kh)] = t_r2
                t_bc = P.task("tensor", lambda e, kh=kh: e.matmul(
                    pRB2[:], c["ones1b"][0:1, 0:64], qrb[kh][:],
                    start=True, stop=True),
                    deps=[t_r2, f.get("pRB2")], sem=te)
                f[("qrb", kh)] = t_bc
                t_rb = P.task("vector", lambda e, kh=kh: e.tensor_copy(
                    rb2[kh][:], pRB2[:]), deps=[t_bc, f.get(("rb2", kh))], sem=ve)
                f["pRB2"] = t_rb
                t_o = P.task("vector", lambda e, h=h, hr=hr, kh=kh: e.tensor_mul(
                    oCT[hr, (h // 2) * 512:(h // 2) * 512 + 512], pOU[kh][:],
                    rb2[kh][:]),
                    deps=[t_ou, t_rb, f.get(("oCT", h))], sem=ve)
                f[("pOU", kh)] = t_o
                f[("rb2", kh)] = t_o
                t_on.append(t_o)

            pend_n = []
            for h in range(16):
                kh = h % 2
                if len(pend_n) >= 2:
                    c3_norm(*pend_n.pop(0))
                hr = slice((h % 2) * 64, (h % 2) * 64 + 64)
                mc = (h // 2) * 64
                t_qs = P.task("tensor", lambda e, h=h, hr=hr, mc=mc: e.matmul(
                    pQM[:], medb[hr, mc:mc + 64],
                    qTg[hr, (h // 2) * 512:(h // 2) * 512 + 512],
                    start=True, stop=True),
                    deps=[t_qe[h // 2], f.get("pQM"), t_c2v], sem=te)
                t_eq = P.task("scalar", lambda e, kh=kh: e.activation(
                    expQM[kh][:], pQM[:], AF.Exp, scale=0.125),
                    deps=[t_qs, f.get(("expQM", kh))], sem=sc)
                f["pQM"] = t_eq
                t_ss = P.task("tensor", lambda e, kh=kh: e.matmul(
                    pQS2[kh * 32:kh * 32 + 1, :], c["onescb"][0:64, :], expQM[kh][:],
                    start=True, stop=True),
                    deps=[t_eq, f.get(("pQS", kh))], sem=te)
                t_ou = P.task("tensor", lambda e, h=h, kh=kh: e.matmul(
                    pOU[kh][:], mkv_all[:, h * 64:(h + 1) * 64], expQM[kh][:],
                    start=True, stop=True),
                    deps=[t_eq, t_c2v, f.get(("pOU", kh))], sem=te)
                f[("expQM", kh)] = [t_ss, t_ou]
                pend_n.append((h, kh, t_ss, t_ou))
            for args in pend_n:
                c3_norm(*args)
            pend_n = []
            for cp in range(8):
                t_am = []
                for ci in range(8):
                    t_am.append(P.task("tensor", lambda e, ci=ci, cp=cp: e.matmul(
                        pA[:], wob[:, ci * 1024 + cp * 128: ci * 1024 + (cp + 1) * 128],
                        oCT[:, ci * 512:(ci + 1) * 512],
                        start=(ci == 0), stop=(ci == 7)),
                        deps=([t_on[-1], f.get("pA")] if ci == 0 else []), sem=te))
                t_y3 = P.task("vector", lambda e, cp=cp, k=k: e.scalar_tensor_tensor(
                    y3sb[:, cp * 512:(cp + 1) * 512], pA[:], bos[:, cp:cp + 1],
                    y2g[k][:, cp * 512:(cp + 1) * 512], ALU.add, ALU.add),
                    deps=[t_am[-1], t_ly_all, t_bo, f.get(("y3", cp))], sem=ve)
                f["pA"] = t_y3
                t_wr = P.task("sync", lambda e, cp=cp, t0=t0: e.dma_start(
                    out=y3T[cp * 128:(cp + 1) * 128, t0:t0 + 512],
                    in_=y3sb[:, cp * 512:(cp + 1) * 512]),
                    deps=[t_y3], sem="c_sw", amt=16)
            t_wr_last = P.last_tok["sync"]
            for cp in range(8):
                f[("y3", cp)] = t_wr_last
            for h in range(16):
                f[("oCT", h)] = P.last_tok["tensor"]
            f[("xng", k)] = [P.last_tok["tensor"]]
            f[("y2g", k)] = [t_y3]
            f["qTg"] = [P.last_tok["tensor"]]
    return None



def emit_stageD0_gen(P, nc, es, c, io, T):
    sb0 = lambda n, s, d=F32: es.enter_context(nc.sbuf_tensor(n, s, d))
    ps0 = lambda n, s, d=F32: es.enter_context(nc.psum_tensor(n, s, d))
    te, ve, sc = "d0_te", "d0_ve", "d0_sc"
    f = {}
    wst = [sb0(f"d0_wst{k}", [128, 512]) for k in range(2)]
    absv = sb0("d0_absv", [128, 512])
    sgn = sb0("d0_sgn", [128, 512], BF16)
    stp = sb0("d0_stp", [128, 512], BF16)
    qout = [sb0(f"d0_q{k}", [128, 512], BF16) for k in range(2)]
    gacc = sb0("d0_gacc", [128, 64])
    grow = sb0("d0_grow", [1, 64])
    g11 = sb0("d0_g11", [1, 1])
    nthr = sb0("d0_nthr", [128, 1])
    nthr_h = io["d0thr"]

    for mi, (srcname, dstname, W, NC) in enumerate(
            [("wfcT", "wqfch", 4096, 8), ("wpT", "wqph", 1024, 32)]):
        src = io[srcname]
        dst = io[dstname]
        SW = 512
        blocks = [(ch, co) for ch in range(NC) for co in range(0, W, SW)]
        NB = len(blocks)
        t_r = []
        for bi, (ch, co) in enumerate(blocks):
            k = bi % 2
            gi = bi % 64
            t_l = P.task("gpsimd", lambda e, src=src, ch=ch, co=co, k=k: e.dma_start(
                out=wst[k][:, :], in_=src[ch * 128:(ch + 1) * 128, co:co + 512]),
                deps=[f.get(("wst", k))], sem=f"d0_sl{k}", amt=16)
            t_a = P.task("scalar", lambda e, k=k: e.activation(
                absv[:, :], wst[k][:, :], AF.Abs),
                deps=[t_l, f.get("absv")], sem=sc)
            f[("wst", k)] = t_a
            t_r.append(P.task("vector", lambda e, gi=gi: e.tensor_reduce(
                gacc[:, gi:gi + 1], absv[:, :], AX.X, ALU.add),
                deps=[t_a, f.get("gacc") if bi == 0 else None], sem=ve))
            f["absv"] = t_r[-1]
            yield
        t_g2 = P.task("gpsimd", lambda e: e.tensor_reduce(
            grow[0:1, 0:64], gacc[:, 0:64], mybir.AxisListType.C, ALU.add),
            deps=[t_r[-1], f.get("grow")], sem="d0_gp")
        f["gacc"] = t_g2
        t_g3 = P.task("vector", lambda e: e.tensor_reduce(
            g11[:], grow[0:1, 0:64], AX.X, ALU.add), deps=[t_g2, f.get("g11")], sem=ve)
        NEL = float(NC * 128 * W)
        t_g4 = P.task("vector", lambda e, NEL=NEL: e.tensor_scalar(
            g11[:], g11[:], 1.0 / NEL, 1e-6, ALU.mult, ALU.add), deps=[t_g3], sem=ve)
        t_g5 = P.task("vector", lambda e: e.tensor_scalar_mul(
            g11[:], g11[:], -0.5), deps=[t_g4], sem=ve)
        t_g6 = P.task("gpsimd", lambda e: e.dma_start(out=nthr_h, in_=g11[:]),
                      deps=[t_g5], sem="d0_th", amt=16)
        t_g7 = P.task("gpsimd", lambda e: e.dma_start(
            out=nthr[:], in_=bass.AP(nthr_h.tensor, 0, [[0, 128], [1, 1]])),
            deps=[t_g6, f.get("nthr")], sem="d0_th2", amt=16)
        f["grow"] = t_g3
        f["g11"] = t_g6
        for bi, (ch, co) in enumerate(blocks):
            k = bi % 2
            t_l = P.task("gpsimd", lambda e, src=src, ch=ch, co=co, k=k: e.dma_start(
                out=wst[k][:, :], in_=src[ch * 128:(ch + 1) * 128, co:co + 512]),
                deps=[f.get(("wst", k))], sem=f"d0_sl{k}", amt=16)
            t_a = P.task("scalar", lambda e, k=k: e.activation(
                absv[:, :], wst[k][:, :], AF.Abs),
                deps=[t_l, f.get("absv")], sem=sc)
            t_s = P.task("scalar", lambda e, k=k: e.activation(
                sgn[:, :], wst[k][:, :], AF.Sign),
                deps=[t_l, f.get("sgn")], sem=sc)
            f[("wst", k)] = t_s
            t_t = P.task("scalar", lambda e: e.activation(
                stp[:, :], absv[:, :], AF.Sign, bias=nthr[:, 0:1]),
                deps=[t_a, t_g7, f.get("stp")], sem=sc)
            f["absv"] = t_t
            t_t2 = P.task("vector", lambda e: e.tensor_scalar_max(
                stp[:, :], stp[:, :], 0.0), deps=[t_t], sem=ve)
            t_q = P.task("vector", lambda e, k=k: e.tensor_mul(
                qout[k][:, :], sgn[:, :], stp[:, :]),
                deps=[t_t2, t_s, f.get(("qout", k))], sem=ve)
            f["sgn"] = t_q
            f["stp"] = t_q
            t_w = P.task("gpsimd", lambda e, dst=dst, ch=ch, co=co, k=k: e.dma_start(
                out=dst[ch * 128:(ch + 1) * 128, co:co + 512], in_=qout[k][:, :]),
                deps=[t_q], sem=f"d0_sw{k}", amt=16)
            f[("qout", k)] = t_w
            yield


def emit_stageD(P, nc, es, c, io, T):
    NT = T // 512
    y3T, yout = io["y3T"], io["y"]

    sb = lambda n, s, d=F32: es.enter_context(nc.sbuf_tensor(n, s, d))
    ps = lambda n, s, d=F32: es.enter_context(nc.psum_tensor(n, s, d))

    wqfc = sb("d_wqfc", [128, 32768], BF16)
    wqp = sb("d_wqp", [128, 32768], BF16)
    te, ve, sc = "d_te", "d_ve", "d_sc"
    f = {}
    t_lwq = []
    for ci in range(8):
        t_lwq.append(P.task("sync", lambda e, ci=ci: e.dma_start(
            out=wqfc[:, ci * 4096:(ci + 1) * 4096],
            in_=io["wqfch"][ci * 128:(ci + 1) * 128, :]), sem="d_lq", amt=16))
    for ch in range(32):
        t_lwq.append(P.task("sync", lambda e, ch=ch: e.dma_start(
            out=wqp[:, ch * 1024:(ch + 1) * 1024],
            in_=io["wqph"][ch * 128:(ch + 1) * 128, :]), sem="d_lq", amt=16))
    t_lwq_all = t_lwq[-1]
    with ExitStack() as es1:
        sb1 = lambda n, s, d=F32: es1.enter_context(nc.sbuf_tensor(n, s, d))
        ps1 = lambda n, s, d=F32: es1.enter_context(nc.psum_tensor(n, s, d))
        y3g = [sb1(f"d_y3{k}", [128, 4096], BF16) for k in range(2)]
        xn4 = sb1("d_xn4", [128, 4096], BF16)
        zsb = sb1("d_z", [128, 16384], BF16)
        lnws = sb1("d_lnw", [128, 32])
        lnbs = sb1("d_lnb", [128, 32])
        y3nat = sb1("d_y3n", [128, 1024], BF16)
        y4sb = [sb1(f"d_y4{k}", [128, 512]) for k in range(2)]
        _dr = [sb1(f"d_row{i}", [1, 512]) for i in range(3)]
        aux = {
            "sq2": [sb1("d_sq0", [128, 512], BF16), sb1("d_sq1", [128, 512], BF16)],
            "rows": [_dr[0], _dr[1], _dr[2], _dr[1]],
        }
        pSrow = ps1("d_pSrow", [64, 512])
        aux["pS1"], aux["pS2"] = pSrow[0:1, :], pSrow[32:33, :]
        aux["pB1"] = ps1("d_pB1", [128, 512])
        aux["pB2"] = ps1("d_pB2", [128, 512])
        pH = [ps1(f"d_pH{k}", [128, 512]) for k in range(2)]
        pO = [ps1(f"d_pO{k}", [128, 512]) for k in range(2)]
        pTt = ps1("d_pTt", [128, 128], BF16)

        t_lw = P.task("sync", lambda e: e.dma_start(out=lnws[:, :], in_=io["lnw"]),
                      sem="d_sv", amt=16)
        t_lb = P.task("sync", lambda e: e.dma_start(out=lnbs[:, :], in_=io["lnb"]),
                      sem="d_sv", amt=16)

        for g in range(NT):
            k = g % 2
            t0 = g * 512
            t_ld = []
            for ci in range(8):
                t_ld.append(P.task("sync", lambda e, ci=ci, t0=t0, k=k: e.dma_start(
                    out=y3g[k][:, ci * 512:(ci + 1) * 512],
                    in_=y3T[ci * 128:(ci + 1) * 128, t0:t0 + 512]),
                    deps=[f.get(("y3g", k))], sem=f"d_s{k}", amt=16))
            t_ld_all = t_ld[-1]
            chunks_in = [(y3g[k][:, ci * 512:(ci + 1) * 512], t_ld_all) for ci in range(8)]
            chunks_out = [xn4[:, ci * 512:(ci + 1) * 512] for ci in range(8)]
            t_xn = emit_pln(P, nc, c, "d_", 512, chunks_in, chunks_out, aux,
                            deps_in=[f.get("xn4")])
            t_z = []
            for fc in range(32):
                kp = fc % 2
                t_hm = []
                for ci in range(8):
                    t_hm.append(P.task("tensor", lambda e, fc=fc, ci=ci, kp=kp: e.matmul(
                        pH[kp][:],
                        wqfc[:, ci * 4096 + fc * 128: ci * 4096 + (fc + 1) * 128],
                        xn4[:, ci * 512:(ci + 1) * 512],
                        start=(ci == 0), stop=(ci == 7)),
                        deps=[t_xn[ci]] + ([f.get(("pH", kp)), t_lwq_all]
                                           if ci == 0 else []), sem=te))
                t_z.append(P.task("scalar", lambda e, fc=fc, kp=kp: e.activation(
                    zsb[:, fc * 512:(fc + 1) * 512], pH[kp][:], AF.Gelu),
                    deps=[t_hm[-1], f.get(("z", fc))], sem=sc))
                f[("pH", kp)] = t_z[-1]
            chunks_in2 = [(zsb[:, fc * 512:(fc + 1) * 512], t_z[fc]) for fc in range(32)]
            chunks_out2 = [zsb[:, fc * 512:(fc + 1) * 512] for fc in range(32)]
            gain = [lnws[:, fc:fc + 1] for fc in range(32)]
            bias = [lnbs[:, fc:fc + 1] for fc in range(32)]
            t_hg = emit_pln(P, nc, c, "d_", 512, chunks_in2, chunks_out2, aux,
                            deps_in=[t_lb], gain=gain, bias=bias)
            for tt_ in range(4):
                t_tp = []
                for ci in range(8):
                    t_t = P.task("tensor", lambda e, ci=ci, tt_=tt_, k=k: e.transpose(
                        pTt[:], y3g[k][:, ci * 512 + tt_ * 128: ci * 512 + (tt_ + 1) * 128],
                        c["eyeb"]), deps=[t_ld_all, f.get("pTt")], sem=te)
                    t_c = P.task("vector", lambda e, ci=ci: e.tensor_copy(
                        y3nat[:, ci * 128:(ci + 1) * 128], pTt[:]),
                        deps=[t_t] + ([f.get("y3nat")] if ci == 0 else []), sem=ve)
                    f["pTt"] = t_c
                    t_tp.append(t_c)
                for cg in range(2):
                    kp = cg % 2
                    t_om = []
                    for fc in range(32):
                        t_om.append(P.task("tensor", lambda e, fc=fc, tt_=tt_, cg=cg,
                                           kp=kp: e.matmul(
                            pO[kp][:],
                            zsb[:, fc * 512 + tt_ * 128: fc * 512 + (tt_ + 1) * 128],
                            wqp[:, fc * 1024 + cg * 512: fc * 1024 + (cg + 1) * 512],
                            start=(fc == 0), stop=(fc == 31)),
                            deps=[t_hg[fc]] + ([f.get(("pO", kp)), t_lwq_all]
                                               if fc == 0 else []), sem=te))
                    t_y4 = P.task("vector", lambda e, cg=cg, kp=kp: e.tensor_add(
                        y4sb[kp][:], pO[kp][:], y3nat[:, cg * 512:(cg + 1) * 512]),
                        deps=[t_om[-1], t_tp[-1], f.get(("y4", kp))], sem=ve)
                    f[("pO", kp)] = t_y4
                    t_wr = P.task("sync", lambda e, t0=t0, tt_=tt_, cg=cg, kp=kp:
                        e.dma_start(
                            out=yout[t0 + tt_ * 128: t0 + (tt_ + 1) * 128,
                                     cg * 512:(cg + 1) * 512],
                            in_=y4sb[kp][:]), deps=[t_y4], sem=f"d_sw{kp}", amt=16)
                    f[("y4", kp)] = t_wr
                f["y3nat"] = P.last_tok["vector"]
            f[("y3g", k)] = [P.last_tok["tensor"], t_xn[-1]]
            for fc in range(32):
                f[("z", fc)] = P.last_tok["tensor"]
            f["xn4"] = [P.last_tok["tensor"]]
    return None




def _core_inputs(b, L0_attn_x, L0_attn_r_sigma, L0_ffn_q_w, L0_ffn_k_w,
                 L0_ffn_v_w, L0_ffn_out_w, L0_ffn_gate_w, L0_ffn_gate_b,
                 L0_ffn_scale_w, L0_ffn_n1_g, L0_ffn_n1_b, L0_ffn_n2_g,
                 L0_ffn_n2_b, L0_ffn_moe_gate_w, L0_ffn_moe_gate_b, L0_ffn_w1,
                 L0_ffn_b1, L0_ffn_ln_g, L0_ffn_ln_b, L0_ffn_w2, L0_ffn_b2,
                 L0_ffn_res_scale, L1_attn_Wq, L1_attn_Wkv, L1_attn_Wo,
                 L1_attn_bo, L1_attn_mediator, L1_ffn_W_fc, L1_ffn_ln_w,
                 L1_ffn_ln_b, L1_ffn_W_proj, _shared={}):
    if not _shared:
        sw_bd = np.zeros((1024, 16), dtype=np.float32)
        for h in range(16):
            sw_bd[h * 64:(h + 1) * 64, h] = L0_ffn_scale_w[h]
        col8 = lambda a: np.ascontiguousarray(np.asarray(a, np.float32).reshape(8, 128).T)
        col32 = lambda a: np.ascontiguousarray(np.asarray(a, np.float32).reshape(32, 128).T)
        f32c = lambda a: np.ascontiguousarray(np.asarray(a, np.float32))
        _shared.update({
            "r_sigma": f32c(L0_attn_r_sigma).reshape(1, 1),
            "qw": f32c(L0_ffn_q_w), "kw": f32c(L0_ffn_k_w), "vw": f32c(L0_ffn_v_w),
            "sw_bd": sw_bd, "gw": f32c(L0_ffn_gate_w), "ow": f32c(L0_ffn_out_w),
            "gb": col8(L0_ffn_gate_b),
            "n1g": col8(L0_ffn_n1_g), "n1b": col8(L0_ffn_n1_b),
            "n2g": col8(L0_ffn_n2_g), "n2b": col8(L0_ffn_n2_b),
            "mgw": f32c(L0_ffn_moe_gate_w),
            "mgb": f32c(L0_ffn_moe_gate_b).reshape(1, 8),
            "w1": f32c(L0_ffn_w1), "w2": f32c(L0_ffn_w2),
            "b1r": np.ascontiguousarray(np.asarray(L0_ffn_b1, np.float32)
                                        .reshape(8, 16, 128).transpose(0, 2, 1)),
            "lgr": np.ascontiguousarray(np.asarray(L0_ffn_ln_g, np.float32)
                                        .reshape(8, 16, 128).transpose(0, 2, 1)),
            "lbr": np.ascontiguousarray(np.asarray(L0_ffn_ln_b, np.float32)
                                        .reshape(8, 16, 128).transpose(0, 2, 1)),
            "b2r": np.ascontiguousarray(np.asarray(L0_ffn_b2, np.float32)
                                        .reshape(8, 8, 128).transpose(0, 2, 1)),
            "rsc": f32c(L0_ffn_res_scale).reshape(1, 8),
            "wq": f32c(L1_attn_Wq), "wkv": f32c(L1_attn_Wkv), "wo": f32c(L1_attn_Wo),
            "bo": col8(L1_attn_bo),
            "medT": np.ascontiguousarray(np.asarray(L1_attn_mediator, np.float32)[0].T),
            "wfcT": np.ascontiguousarray(np.asarray(L1_ffn_W_fc, np.float32).T),
            "wpT": np.ascontiguousarray(np.asarray(L1_ffn_W_proj, np.float32).T),
            "lnw": col32(L1_ffn_ln_w), "lnb": col32(L1_ffn_ln_b),
        })
    d = dict(_shared)
    d["x"] = np.ascontiguousarray(np.asarray(L0_attn_x, np.float32)[b])
    return d


def kernel(**inputs):
    T = 4096
    nc = build_program(T=T, stages=("A", "B1", "B2", "B3", "B4", "C", "D"))
    shared = {}
    in_maps = [_core_inputs(b, _shared=shared, **inputs) for b in range(8)]
    res = run_bass_kernel_spmd(nc, in_maps, core_ids=list(range(8)))
    return np.stack([np.asarray(r["y"]) for r in res.results], axis=0)


# version 22
# speedup vs baseline: 1.0440x; 1.0440x over previous
import math
from contextlib import ExitStack

import numpy as np
import ml_dtypes
import concourse.bass as bass
import concourse.mybir as mybir
from concourse.bass_utils import run_bass_kernel_spmd

F32 = mybir.dt.float32
BF16 = mybir.dt.bfloat16
U32 = mybir.dt.uint32
AF = mybir.ActivationFunctionType
ALU = mybir.AluOpType
AX = mybir.AxisListType
BF = ml_dtypes.bfloat16

ENGS = ("sync", "tensor", "vector", "scalar", "gpsimd")


class Prog:

    def __init__(self, nc, es):
        self.nc = nc
        self.es = es
        self.items = {e: [] for e in ENGS}
        self.sems = {}
        self.cnt = {}
        self.pending_barrier = {e: [] for e in ENGS}
        self.last_tok = {e: None for e in ENGS}

    def sem(self, name):
        if name not in self.sems:
            self.sems[name] = self.es.enter_context(self.nc.semaphore(name))
            self.cnt[name] = 0
        return name

    def _flat(self, deps):
        out = []
        for d in deps:
            if d is None:
                continue
            if isinstance(d, list):
                out.extend(x for x in d if x is not None)
            else:
                out.append(d)
        return out

    def task(self, eng, fn, deps=(), sem=None, amt=1):
        if sem is None:
            sem = eng[0:2] + "_d"
        self.sem(sem)
        deps = self._flat(deps) + self.pending_barrier[eng]
        self.pending_barrier[eng] = []
        self.cnt[sem] += amt
        tok = (sem, self.cnt[sem])
        self.items[eng].append((fn, tuple(deps), sem, amt))
        self.last_tok[eng] = tok
        return tok

    def barrier(self):
        toks = [(s, self.cnt[s]) for s in list(self.sems) if self.cnt[s] > 0]
        for e in ENGS:
            self.pending_barrier[e] = list(toks)

    def finish(self):
        deps = [(s, self.cnt[s]) for s in list(self.sems) if self.cnt[s] > 0]
        return self.task("sync", lambda e: e.nop(), deps=deps, sem="fin")

    def finalize(self, block):
        for eng in ENGS:
            if not self.items[eng]:
                continue

            def run(e, eng=eng):
                hw = {}
                for fn, deps, sem, amt in self.items[eng]:
                    for (s, v) in deps:
                        if hw.get(s, 0) < v:
                            e.wait_ge(self.sems[s], v)
                            hw[s] = v
                    inst = fn(e)
                    if isinstance(inst, list):
                        inst = inst[-1]
                    inst.then_inc(self.sems[sem], amt)
            getattr(block, eng)(run)



def make_consts(P, nc, es):
    eyeb = np.eye(128, dtype=BF)
    bd16 = np.zeros((128, 8), dtype=np.float32)
    for p in range(128):
        bd16[p, p // 16] = 1.0
    sel16 = np.ascontiguousarray(bd16.T)
    maskbd_nd = np.zeros((128, 128), dtype=BF)
    for p in range(128):
        q0 = (p // 16) * 16
        maskbd_nd[p, q0:q0 + 16] = 1.0
        maskbd_nd[p, p] = 0.0
    ones1x128f = np.ones((1, 128), dtype=np.float32)
    ones1x128b = np.ones((1, 128), dtype=BF)
    onescb = np.ones((128, 1), dtype=BF)
    onescf = np.ones((128, 1), dtype=np.float32)

    masknd4 = np.ascontiguousarray(np.tile(maskbd_nd, (1, 4)))

    c = {}
    toks = []
    for name, arr in [("eyeb", eyeb), ("masknd", maskbd_nd),
                      ("masknd4", masknd4),
                      ("ones1f", ones1x128f),
                      ("ones1b", ones1x128b), ("onescb", onescb),
                      ("onescf", onescf)]:
        dr = nc.inline_tensor(arr, name="cd_" + name).ap()
        dt = {np.dtype(np.float32): F32, np.dtype(BF): BF16}[arr.dtype]
        t = es.enter_context(nc.sbuf_tensor("c_" + name, list(arr.shape), dt))
        c[name] = t[:, :]
        toks.append(P.task("sync", lambda e, t=t, dr=dr: e.dma_start(
            out=t[:, :], in_=dr), sem="c_ld", amt=16))
    c["_tok"] = toks
    return c



def emit_stageA(P, nc, es, c, io, T):
    NS = T // 128
    x, y1n, r11 = io["x"], io["y1n"], io["r_sigma"]

    sb = lambda n, s, d=F32: es.enter_context(nc.sbuf_tensor(n, s, d))
    ps = lambda n, s, d=F32: es.enter_context(nc.psum_tensor(n, s, d))

    xsup = [sb(f"a_xsup{k}", [128, 1024]) for k in range(2)]
    sq = [sb(f"a_sq{k}", [128, 1024]) for k in range(2)]
    xn = [sb(f"a_xn{k}", [128, 1024]) for k in range(2)]
    xnb = [sb(f"a_xnb{k}", [128, 1024], BF16) for k in range(2)]
    ssum = [sb(f"a_s{k}", [128, 1024]) for k in range(2)]
    y1b = [sb(f"a_y1b{k}", [128, 1024], BF16) for k in range(2)]
    rs32 = [sb(f"a_rs32{k}", [128, 32]) for k in range(2)]
    mrsb = [sb(f"a_mrsb{k}", [8, 32]) for k in range(2)]
    tA = [sb(f"a_tA{k}", [8, 16]) for k in range(2)]
    tB = [sb(f"a_tB{k}", [8, 16]) for k in range(2)]
    sd8 = [sb(f"a_sd8{k}", [8, 16]) for k in range(2)]
    mr128 = [sb(f"a_mr128{k}", [128, 32]) for k in range(2)]
    n16 = [sb(f"a_n16{k}", [128, 16]) for k in range(2)]
    n16h = [sb(f"a_n16h{k}", [128, 16], BF16) for k in range(2)]
    nTall = [sb(f"a_nTall{k}", [16, 128], BF16) for k in range(2)]
    nrow = [sb(f"a_nrow{k}", [1, 2048], BF16) for k in range(2)]
    xnbT = [sb(f"a_xnbT{k}", [64, 2048], BF16) for k in range(2)]
    Kb4 = [sb(f"a_Kb4{k}", [128, 512], BF16) for k in range(2)]
    nr128 = sb("a_nr128", [128, 1])
    r2128 = sb("a_r2128", [128, 1])
    rsb = sb("a_rsb", [1, 1])
    negr = sb("a_negr", [1, 1])
    _bd16n = np.zeros((128, 8), dtype=np.float32)
    for _p in range(128):
        _bd16n[_p, _p // 16] = 1.0
    _bd16d = nc.inline_tensor(_bd16n, name="cd_a_bd16").ap()
    _sel16d = nc.inline_tensor(np.ascontiguousarray(_bd16n.T), name="cd_a_sel16").ap()
    bd16_sb = sb("a_bd16", [128, 8])
    sel16_sb = sb("a_sel16", [8, 128])
    P.task("sync", lambda e: e.dma_start(out=bd16_sb[:, :], in_=_bd16d),
           sem="a_cl", amt=16)
    t_selld = P.task("sync", lambda e: e.dma_start(out=sel16_sb[:, :], in_=_sel16d),
                     sem="a_cl", amt=16)

    pT = [ps(f"a_pT{k}", [64, 1024], BF16) for k in range(2)]
    pG2 = [ps(f"a_pG2{k}", [128, 512]) for k in range(2)]
    pY2 = [ps(f"a_pY2{k}", [128, 512]) for k in range(2)]
    pSm = ps("a_pSm", [128, 512])
    pmr = pSm[:, 128:160]
    pst = pSm[0:8, 160:192]

    sy, te, ve, sc = "a_sy", "a_te", "a_ve", "a_sc"

    xr = x.rearrange("(jt j i) (h d) -> jt (i h) j d", j=16, i=8, h=16, d=64)
    yr = y1n.rearrange("(jt j i) (h d) -> jt (i h) j d", j=16, i=8, h=16, d=64)

    t0 = P.task("sync", lambda e: e.dma_start(out=rsb[:], in_=r11), sem="a_sr", amt=16)
    t1 = P.task("vector", lambda e: e.tensor_scalar_mul(negr[:], rsb[:], -1.0),
                deps=[t0], sem=ve)
    t2 = P.task("tensor", lambda e: e.matmul(pmr[:, 0:1], c["ones1f"], negr[:],
                                             start=True, stop=True),
                deps=[t1], sem=te)
    t_nr = P.task("vector", lambda e: e.tensor_copy(nr128[:], pmr[:, 0:1]),
                  deps=[t2], sem=ve)
    t_r2 = P.task("vector", lambda e: e.tensor_scalar_mul(r2128[:], nr128[:], -2.0),
                  deps=[t_nr], sem=ve)

    fr = {k: {} for k in range(2)}
    fps = {"pst": None, "pmr": [t_nr]}
    fg = {}

    def a_lnphase(jt):
        k = jt % 2
        F = fr[k]
        t_ld = P.task("sync", lambda e, jt=jt, k=k: e.dma_start(
            out=xsup[k].rearrange("p (j d) -> p j d", j=16), in_=xr[jt]),
            deps=[F.get("xsup")], sem=f"a_sx{k}", amt=16)

        t_sq = P.task("scalar", lambda e, k=k: e.activation(
            sq[k][:], xsup[k][:], AF.Square), deps=[t_ld, F.get("sq")], sem=sc)
        t_r1 = P.task("vector", lambda e, k=k: e.tensor_reduce(
            rs32[k][:, 0:16], xsup[k].rearrange("p (s d) -> p s d", s=16, d=64),
            AX.X, ALU.add), deps=[t_ld, F.get("rs32")], sem=ve)
        t_r2s = P.task("vector", lambda e, k=k: e.tensor_reduce(
            rs32[k][:, 16:32], sq[k].rearrange("p (s d) -> p s d", s=16, d=64),
            AX.X, ALU.add), deps=[t_sq], sem=ve)
        t_st = P.task("tensor", lambda e, k=k: e.matmul(
            pst[:], bd16_sb[:, :], rs32[k][:], start=True, stop=True),
            deps=[t_r2s, t_selld, fps["pst"]], sem=te)
        t_m = P.task("vector", lambda e, k=k: e.tensor_scalar_mul(
            mrsb[k][:, 0:16], pst[:, 0:16], 1.0 / 1024.0),
            deps=[t_st, F.get("mrsb")], sem=ve)
        t_ex2 = P.task("vector", lambda e, k=k: e.tensor_scalar(
            tA[k][:], pst[:, 16:32], 1.0 / 1024.0, 1e-5, ALU.mult, ALU.add),
            deps=[t_st, F.get("tA")], sem=ve)
        t_m2 = P.task("vector", lambda e, k=k: e.tensor_mul(
            tB[k][:], mrsb[k][:, 0:16], mrsb[k][:, 0:16]), deps=[t_m], sem=ve)
        t_var = P.task("vector", lambda e, k=k: e.tensor_sub(
            tA[k][:], tA[k][:], tB[k][:]), deps=[t_ex2, t_m2], sem=ve)
        t_sd = P.task("scalar", lambda e, k=k: e.activation(
            sd8[k][:], tA[k][:], AF.Sqrt),
            deps=[t_var, F.get("sd8")], sem=sc)
        t_rs = P.task("vector", lambda e, k=k: e.reciprocal(
            mrsb[k][:, 16:32], sd8[k][:]), deps=[t_sd], sem=ve)
        t_bc = P.task("tensor", lambda e, k=k: e.matmul(
            pmr[:], sel16_sb[:, :], mrsb[k][:], start=True, stop=True),
            deps=[t_m, t_ex2, t_rs] + fps["pmr"], sem=te)
        t_mr = P.task("vector", lambda e, k=k: e.tensor_copy(
            mr128[k][:], pmr[:]), deps=[t_bc, F.get("mr128")], sem=ve)
        fps["pst"] = [t_m, t_ex2, t_mr]
        fps["pmr"] = [t_mr]
        F["mrsb"] = t_bc
        F["rs32"] = t_st
        F["tA"] = t_sd
        F["sd8"] = t_rs

        t_xn0 = P.task("vector", lambda e, k=k: e.tensor_scalar(
            xn[k][:, 0:64], xsup[k][:, 0:64], mr128[k][:, 0:1],
            mr128[k][:, 16:17], ALU.subtract, ALU.mult),
            deps=[t_mr, F.get("xn")], sem=ve)
        for j in range(1, 16):
            P.task("vector", lambda e, k=k, j=j: e.tensor_scalar(
                xn[k][:, j * 64:(j + 1) * 64], xsup[k][:, j * 64:(j + 1) * 64],
                mr128[k][:, j:j + 1], mr128[k][:, 16 + j:17 + j],
                ALU.subtract, ALU.mult), deps=[], sem=ve)
        t_xn_all = P.last_tok["vector"]
        t_xnb = P.task("vector", lambda e, k=k: e.tensor_copy(
            xnb[k][:], xn[k][:]), deps=[t_xn_all, F.get("xnb")], sem=ve)
        t_sqn = P.task("scalar", lambda e, k=k: e.activation(
            sq[k][:], xn[k][:], AF.Square), deps=[t_xn_all, t_r2s], sem=sc)
        t_n16 = P.task("vector", lambda e, k=k: e.tensor_reduce(
            n16[k][:], sq[k].rearrange("p (s d) -> p s d", s=16, d=64),
            AX.X, ALU.add), deps=[t_sqn, F.get("n16")], sem=ve)
        t_n16h = P.task("vector", lambda e, k=k: e.tensor_scalar_mul(
            n16h[k][:], n16[k][:], -0.5), deps=[t_n16, F.get("n16h")], sem=ve)
        t_s = P.task("vector", lambda e, k=k: e.scalar_tensor_tensor(
            ssum[k][:], xn[k][:], 2.0, xsup[k][:], ALU.mult, ALU.add),
            deps=[F.get("ssum")], sem=ve)

        return dict(k=k, F=F, t_xnb=t_xnb, t_n16h=t_n16h, t_s=t_s,
                    t_sqn=t_sqn, t_r1=t_r1, t_sq=t_sq, t_xn_all=t_xn_all,
                    t_n16=t_n16)

    def a_tilephase(jt, S):
        k = S["k"]; F = S["F"]; t_xnb = S["t_xnb"]; t_n16h = S["t_n16h"]
        t_s = S["t_s"]; t_sqn = S["t_sqn"]; t_r1 = S["t_r1"]; t_sq = S["t_sq"]
        t_xn_all = S["t_xn_all"]; t_n16 = S["t_n16"]

        t_ntr = P.task("tensor", lambda e, k=k: e.transpose(
            pT[0][0:16, 0:128], n16h[k][:], c["eyeb"]),
            deps=[t_n16h, fg.get(("pT", 0))], sem=te)
        t_ncp = P.task("vector", lambda e, k=k: e.tensor_copy(
            nTall[k][:], pT[0][0:16, 0:128]),
            deps=[t_ntr, F.get("nTall")], sem=ve)
        t_ndma = P.task("sync", lambda e, k=k: e.dma_start(
            out=nrow[k][0:1, :], in_=nTall[k][:, :]),
            deps=[t_ncp, F.get("nrow")], sem=f"a_sn{k}", amt=16)
        F["nTall"] = t_ndma

        t_cp = [None, None]
        for half in (0, 1):
            t_tr = []
            for jj in range(8):
                j = half * 8 + jj
                t_tr.append(P.task("tensor", lambda e, k=k, j=j, jj=jj,
                                   half=half: e.transpose(
                    pT[half][:, jj * 128:(jj + 1) * 128],
                    xnb[k][:, j * 64:(j + 1) * 64], c["eyeb"]),
                    deps=[t_xnb] +
                         ([t_ncp if half == 0 else None,
                           fg.get(("pT", half))] if jj == 0 else []),
                    sem=te))
            t_cp[half] = P.task("vector", lambda e, k=k, half=half: e.tensor_copy(
                xnbT[k][:, half * 1024:(half + 1) * 1024], pT[half][:]),
                deps=[t_tr[-1], F.get(("xnbT", half))], sem=ve)
            fg[("pT", half)] = t_cp[half]

        t_ap_all = []
        t_y = []
        t_e_last = None
        for jg in range(4):
            kg = jg % 2
            t_mm_last = None
            for jj in range(4):
                j = jg * 4 + jj
                csl = slice(jj * 128, (jj + 1) * 128)
                P.task("tensor", lambda e, k=k, j=j, csl=csl, kg=kg: e.matmul(
                    pG2[kg][:, csl], xnbT[k][:, j * 128:(j + 1) * 128],
                    xnbT[k][:, j * 128:(j + 1) * 128], start=True, stop=False),
                    deps=[t_cp[j // 8]] + ([fg.get(("pG2", kg))] if jj == 0 else []),
                    sem=te)
                P.task("tensor", lambda e, k=k, j=j, csl=csl, kg=kg: e.matmul(
                    pG2[kg][:, csl], c["ones1b"],
                    nrow[k][0:1, j * 128:(j + 1) * 128], start=False, stop=False),
                    deps=[t_ndma], sem=te)
                t_mm_last = P.task("tensor", lambda e, k=k, j=j, csl=csl,
                                   kg=kg: e.matmul(
                    pG2[kg][:, csl], nrow[k][0:1, j * 128:(j + 1) * 128],
                    c["ones1b"], start=False, stop=True),
                    deps=[], sem=te)
            t_e = P.task("scalar", lambda e, k=k, kg=kg: e.activation(
                Kb4[kg][:], pG2[kg][:], AF.Exp, scale=r2128[:, 0:1]),
                deps=[t_mm_last, t_r2, fg.get(("Kb4", kg))], sem=sc)
            t_e_last = t_e
            t_m = P.task("vector", lambda e, kg=kg: e.tensor_mul(
                Kb4[kg][:], Kb4[kg][:], c["masknd4"]), deps=[t_e], sem=ve)
            t_ap = []
            for jj in range(4):
                j = jg * 4 + jj
                half = j // 8
                t_ap.append(P.task("tensor", lambda e, k=k, j=j, jj=jj, kg=kg,
                                   half=half: e.matmul(
                    pY2[half][:, (j % 8) * 64:((j % 8) + 1) * 64],
                    Kb4[kg][:, jj * 128:(jj + 1) * 128],
                    xnb[k][:, j * 64:(j + 1) * 64], start=True, stop=True),
                    deps=[t_m] + ([fg.get(("pY2", half))] if j % 8 == 0 else []),
                    sem=te))
            fg[("pG2", kg)] = t_e
            fg[("Kb4", kg)] = t_ap[-1]
            t_ap_all.extend(t_ap)
            if jg % 2 == 1:
                half = jg // 2
                hs = slice(half * 512, (half + 1) * 512)
                t_y.append(P.task("vector", lambda e, k=k, half=half, hs=hs:
                    e.tensor_add(y1b[k][:, hs], pY2[half][:], ssum[k][:, hs]),
                    deps=t_ap_all[half * 8:half * 8 + 8] + [t_s, F.get("y1b")],
                    sem=ve))
                fg[("pY2", half)] = t_y[-1]

        t_wr = P.task("sync", lambda e, jt=jt, k=k: e.dma_start(
            out=yr[jt], in_=y1b[k].rearrange("p (j d) -> p j d", j=16)),
            deps=t_y, sem=f"a_sw{k}", amt=16)

        F["xsup"] = [t_s, t_sq, t_r1]
        F["sq"] = [t_n16]
        F["xn"] = [t_sqn, t_xnb, t_s]
        F["xnb"] = [t_ap_all[-1]]
        F["n16"] = [t_n16h]
        F["n16h"] = [t_ntr]
        F["nrow"] = [t_e_last]
        F[("xnbT", 0)] = t_e_last
        F[("xnbT", 1)] = t_e_last
        F["ssum"] = [t_y[-1]]
        F["y1b"] = t_wr


    pend_a = None
    for jt in range(NS):
        st_a = a_lnphase(jt)
        if pend_a is not None:
            a_tilephase(*pend_a)
        pend_a = (jt, st_a)
    a_tilephase(*pend_a)
    return None



def build_program(T=4096, stages=("A",), expose=(), inputs_override=()):
    nc = bass.Bass("TRN2", target_bir_lowering=False)
    D = 1024

    def dram(name, shape, dt=F32, kind="ExternalInput"):
        return nc.dram_tensor(name, shape, dt, kind=kind).ap()

    def scratch(name, shape, dt):
        kind = ("ExternalInput" if name in inputs_override else
                "ExternalOutput" if name in expose else "Internal")
        return dram(name, shape, dt, kind=kind)

    io = {}
    io["x"] = dram("x", [T, D])
    io["r_sigma"] = dram("r_sigma", [1, 1])
    io["y1n"] = scratch("y1n", [T, D], BF16)
    if any(st in stages for st in ("B1", "B2", "B3", "B4", "C", "D")):
        io["qw"] = dram("qw", [D, D])
        io["kw"] = dram("kw", [D, D])
        io["vw"] = dram("vw", [D, D])
        io["sw_bd"] = dram("sw_bd", [D, 16])
        io["y1T"] = scratch("y1T", [D, T], BF16)
        io["xnT"] = scratch("xnT", [D, T], BF16)
        io["qsT"] = scratch("qsT", [D, T], BF16)
        io["kTh"] = scratch("kTh", [D, T], BF16)
        io["outT"] = scratch("outT", [D, T], BF16)
    if any(st in stages for st in ("B3", "B4", "C", "D")):
        io["gw"] = dram("gw", [D, D])
        io["ow"] = dram("ow", [D, D])
        io["gb"] = dram("gb", [128, 8])
        io["n1g"] = dram("n1g", [128, 8])
        io["n1b"] = dram("n1b", [128, 8])
        io["mgw"] = dram("mgw", [D, 8])
        io["mgb"] = dram("mgb", [1, 8])
        io["x1T"] = scratch("x1T", [D, T], BF16)
        io["indh"] = scratch("indh", [1, 8 * T], BF16)
        io["w1"] = dram("w1", [8, D, 2048])
        io["w2"] = dram("w2", [8, 2048, D])
        io["w1b"] = scratch("w1b", [8, D, 2048], BF16)
        io["w2b"] = scratch("w2b", [8, 2048, D], BF16)
        io["b1r"] = dram("b1r", [8, 128, 16])
        io["lgr"] = dram("lgr", [8, 128, 16])
        io["lbr"] = dram("lbr", [8, 128, 16])
        io["b2r"] = dram("b2r", [8, 128, 8])
        io["rsc"] = dram("rsc", [1, 8])
        io["n2g"] = dram("n2g", [128, 8])
        io["n2b"] = dram("n2b", [128, 8])
        io["y2T"] = scratch("y2T", [D, T], BF16)
    if any(st in stages for st in ("C", "D")):
        io["wq"] = dram("wq", [D, D])
        io["wkv"] = dram("wkv", [D, 2 * D])
        io["wo"] = dram("wo", [D, D])
        io["bo"] = dram("bo", [128, 8])
        io["medT"] = dram("medT", [D, 64])
        io["xn3T"] = scratch("xn3T", [D, T], BF16)
        io["y3T"] = scratch("y3T", [D, T], BF16)
    if "D" in stages:
        io["wfcT"] = dram("wfcT", [D, 4096])
        io["wpT"] = dram("wpT", [4096, D])
        io["lnw"] = dram("lnw", [128, 32])
        io["lnb"] = dram("lnb", [128, 32])
        io["wqfch"] = scratch("wqfch", [D, 4096], BF16)
        io["wqph"] = scratch("wqph", [4096, D], BF16)
        io["d0thr"] = scratch("d0thr", [1, 1], F32)
        io["y"] = dram("y", [T, D], F32, kind="ExternalOutput")

    es = ExitStack()
    with es:
        P = Prog(nc, es)
        c = make_consts(P, nc, es)
        P.barrier()
        ind8sb = None
        if "A" in stages:
            with ExitStack() as es_a:
                emit_stageA(P, nc, es_a, c, io, T)
            P.barrier()
        if any(st in stages for st in ("B1", "B2")):
            with ExitStack() as es_b12:
                vaugsb = es_b12.enter_context(
                    nc.sbuf_tensor("vaug", [128, (T // 128) * 1040], BF16))
                if "B1" in stages:
                    with ExitStack() as es_b1:
                        emit_stageB1(P, nc, es_b1, c, io, T, None, vaugsb)
                    P.barrier()
                if "B4" in stages:
                    for ei in range(8):
                        P.task("gpsimd", lambda e, ei=ei: e.dma_start(
                            out=io["w1b"][ei], in_=io["w1"][ei]), sem="wc1", amt=16)
                        P.task("gpsimd", lambda e, ei=ei: e.dma_start(
                            out=io["w2b"][ei], in_=io["w2"][ei]), sem="wc2", amt=16)
                if "B2" in stages:
                    with ExitStack() as es_b2:
                        d0gen = (emit_stageD0_gen(P, nc, es_b2, c, io, T)
                                 if "D" in stages else None)
                        emit_stageB2(P, nc, es_b2, c, io, T, None, vaugsb,
                                     d0gen=d0gen)
                        if d0gen is not None:
                            for _ in d0gen:
                                pass
                    P.barrier()
        if "B3" in stages:
            with ExitStack() as es_b3:
                emit_stageB3(P, nc, es_b3, c, io, T, ind8sb)
            P.barrier()
        if "B4" in stages:
            with ExitStack() as es_b4:
                emit_stageB4(P, nc, es_b4, c, io, T, ind8sb)
            P.barrier()
        if "D" in stages and "B2" not in stages:
            with ExitStack() as es_d0:
                for _ in emit_stageD0_gen(P, nc, es_d0, c, io, T):
                    pass
            P.barrier()
        if "C" in stages:
            with ExitStack() as es_c:
                emit_stageC(P, nc, es_c, c, io, T)
            P.barrier()
        if "D" in stages:
            with ExitStack() as es_d:
                emit_stageD(P, nc, es_d, c, io, T)
            P.barrier()
        if "dump_ind8" in expose:
            ind_o = nc.dram_tensor("ind8_dump", [1, 8 * T], BF16,
                                   kind="ExternalOutput").ap()
            P.task("sync", lambda e: e.dma_start(out=ind_o, in_=ind8sb[:, :]),
                   sem="dmp", amt=16)
        if "dump_b1" in expose:
            _dump_b1(P, nc, es, io, T, kTsb, vaugsb)
        P.finish()
        with nc.Block() as block:
            P.finalize(block)
    return nc


def _dump_b1(P, nc, es, io, T, kTsb, vaugsb):
    kT_o = nc.dram_tensor("kT_dump", [1024, T], BF16, kind="ExternalOutput").ap()
    va_o = nc.dram_tensor("va_dump", [128, (T // 128) * 1040], BF16,
                          kind="ExternalOutput").ap()
    for i in range(8):
        P.task("sync", lambda e, i=i: e.dma_start(
            out=kT_o[i * 128:(i + 1) * 128, :], in_=kTsb[i][:, :]),
            sem="dmp", amt=16)
    P.task("sync", lambda e: e.dma_start(out=va_o, in_=vaugsb[:, :]),
           sem="dmp", amt=16)



def np_ln(x, eps=1e-5):
    m = x.mean(-1, keepdims=True)
    v = x.var(-1, keepdims=True)
    return (x - m) / np.sqrt(v + eps)


def np_stageA(x, r):
    T, D = x.shape
    xn = np_ln(x)
    xh = xn.reshape(T, 16, 64)
    diff = xh[:, None, :, :] - xh[:, :, None, :]
    dist = (diff * diff).sum(-1)
    K = np.exp(-dist * r)
    y = xh + np.einsum('thg,tgd->thd', K, xh)
    return x + y.reshape(T, D)



def emit_pln(P, nc, c, pref, W, chunks_in, chunks_out, aux, deps_in,
             gain=None, bias=None, extra_res=None, fuse_act=None):
    ve, te, sc = pref + "ve", pref + "te", pref + "sc"
    sq2, pS1, pS2 = aux["sq2"], aux["pS1"], aux["pS2"]
    r_m, r_e, r_v, r_s = aux["rows"]
    pB1, pB2 = aux["pB1"], aux["pB2"]
    D = 128 * len(chunks_in)

    t_s1 = []
    for i, (ap, tok) in enumerate(chunks_in):
        t_s1.append(P.task("tensor", lambda e, ap=ap, i=i: e.matmul(
            pS1[:], c["onescb"], ap, start=(i == 0), stop=(i == len(chunks_in) - 1)),
            deps=[tok] + ([aux.get("f_pS1")] if i == 0 else []), sem=te))
    t_sq = []
    t_sqm = [None, None]
    for i, (ap, tok) in enumerate(chunks_in):
        kq = i % 2
        t_v = P.task("vector", lambda e, ap=ap, kq=kq: e.tensor_mul(
            sq2[kq][:], ap, ap),
            deps=[tok, t_sqm[kq] if t_sqm[kq] else aux.get(("f_sq", kq))], sem=ve)
        t_m_ = P.task("tensor", lambda e, i=i, kq=kq: e.matmul(
            pS2[:], c["onescb"], sq2[kq][:],
            start=(i == 0), stop=(i == len(chunks_in) - 1)),
            deps=[t_v] + ([aux.get("f_pS2")] if i == 0 else []), sem=te)
        t_sqm[kq] = t_m_
        t_sq.append(t_m_)
    t_m = P.task("vector", lambda e: e.tensor_scalar_mul(r_m[:], pS1[:], 1.0 / D),
                 deps=[t_s1[-1], t_sq[-1], aux.get("f_rows")], sem=ve)
    t_e2 = P.task("vector", lambda e: e.tensor_scalar(
        r_e[:], pS2[:], 1.0 / D, 1e-5, ALU.mult, ALU.add), deps=[t_sq[-1]], sem=ve)
    t_v = P.task("vector", lambda e: e.scalar_tensor_tensor(
        r_v[:], r_m[:], -1.0, r_m[:], ALU.mult, ALU.mult), deps=[t_m, t_e2], sem=ve)
    t_v2 = P.task("vector", lambda e: e.tensor_add(r_v[:], r_v[:], r_e[:]),
                  deps=[t_v], sem=ve)
    t_sd = P.task("scalar", lambda e: e.activation(r_s[:], r_v[:], AF.Sqrt),
                  deps=[t_v2, aux.get("f_rs")], sem=sc)
    t_rc = P.task("vector", lambda e: e.reciprocal(r_v[:], r_s[:]), deps=[t_sd], sem=ve)
    NW = (W + 511) // 512
    t_bc1 = []
    t_bc2 = []
    for i in range(NW):
        slw = slice(i * 512, min((i + 1) * 512, W))
        t_bc1.append(P.task("tensor", lambda e, slw=slw: e.matmul(
            pB1[:, slw], c["ones1f"], r_m[0:1, slw], start=True, stop=True),
            deps=[t_m] + ([aux.get("f_pB1")] if i == 0 else []), sem=te))
        t_bc2.append(P.task("tensor", lambda e, slw=slw: e.matmul(
            pB2[:, slw], c["ones1f"], r_v[0:1, slw], start=True, stop=True),
            deps=[t_rc] + ([aux.get("f_pB2")] if i == 0 else []), sem=te))
    t_mB = t_bc1[-1]
    t_rB = t_bc2[-1]
    aux["f_pS1"] = [t_m, t_e2]
    aux["f_pS2"] = t_e2
    aux[("f_sq", 0)] = t_sqm[0]
    aux[("f_sq", 1)] = t_sqm[1]
    aux["f_rows"] = [t_bc1[-1], t_bc2[-1], t_rc]
    aux["f_rs"] = t_rc

    t_out = []
    for i, (ap, tok) in enumerate(chunks_in):
        oap = chunks_out[i]
        t1 = P.task("vector", lambda e, ap=ap, oap=oap: e.tensor_sub(
            oap, ap, pB1[:]), deps=[t_mB, tok] + (list(deps_in) if i == 0 else []),
            sem=ve)
        t2 = P.task("vector", lambda e, oap=oap: e.tensor_mul(
            oap, oap, pB2[:]), deps=[t_rB, t1], sem=ve)
        if fuse_act is not None:
            t2 = P.task("scalar", lambda e, oap=oap, i=i: e.activation(
                oap, oap, fuse_act, scale=gain[i], bias=bias[i]),
                deps=[t2], sem=pref + "sc")
        elif gain is not None:
            t2 = P.task("vector", lambda e, oap=oap, i=i: e.tensor_scalar(
                oap, oap, gain[i], bias[i], ALU.mult, ALU.add), deps=[t2], sem=ve)
        if extra_res is not None:
            t2 = P.task("vector", lambda e, oap=oap, i=i: e.tensor_add(
                oap, oap, extra_res[i]), deps=[t2], sem=ve)
        t_out.append(t2)
    aux["f_pB1"] = t_out[-1]
    aux["f_pB2"] = t_out[-1]
    return t_out



def emit_stageB1(P, nc, es, c, io, T, kTsb, vaugsb):
    NT = T // 512
    y1n, y1T, xnT, qsT = io["y1n"], io["y1T"], io["xnT"], io["qsT"]

    sb = lambda n, s, d=F32: es.enter_context(nc.sbuf_tensor(n, s, d))
    ps = lambda n, s, d=F32: es.enter_context(nc.psum_tensor(n, s, d))

    qwb = sb("b1_qw", [128, 8192], BF16)
    kwb = sb("b1_kw", [128, 8192], BF16)
    vwb = sb("b1_vw", [128, 8192], BF16)
    swb = sb("b1_sw", [128, 128], BF16)
    t_w = []
    for wsb, wname in [(qwb, "qw"), (kwb, "kw"), (vwb, "vw")]:
        for ci in range(8):
            t_w.append(P.task("gpsimd", lambda e, wsb=wsb, wname=wname, ci=ci:
                e.dma_start(out=wsb[:, ci * 1024:(ci + 1) * 1024],
                            in_=io[wname][ci * 128:(ci + 1) * 128, :]),
                sem="b1_gw", amt=16))
    for ci in range(8):
        t_w.append(P.task("gpsimd", lambda e, ci=ci: e.dma_start(
            out=swb[:, ci * 16:(ci + 1) * 16],
            in_=io["sw_bd"][ci * 128:(ci + 1) * 128, :]), sem="b1_gw", amt=16))

    y1Tsb = [sb(f"b1_y1T{k}", [128, 4096], BF16) for k in range(2)]
    xnTsb = [sb(f"b1_xnT{k}", [128, 4096], BF16) for k in range(2)]
    scalT = [sb(f"b1_scal{k}", [16, 512], BF16) for k in range(2)]
    _spn = np.zeros((16, 1024), dtype=BF)
    for _cp in range(8):
        _spn[2 * _cp, _cp * 128: _cp * 128 + 64] = 0.25
        _spn[2 * _cp + 1, _cp * 128 + 64: _cp * 128 + 128] = 0.25
    _spd = nc.inline_tensor(_spn, name="cd_selpair").ap()
    selpair_sb = sb("b1_selpair", [16, 1024], BF16)
    t_sp_ld = P.task("sync", lambda e: e.dma_start(out=selpair_sb[:, :], in_=_spd),
                     sem="b1_spl", amt=16)
    qT8 = [sb(f"b1_qT8{k}", [128, 4096], BF16) for k in range(2)]
    qsTst = [sb(f"b1_qsT{k}", [128, 512], BF16) for k in range(2)]
    kTst = [sb(f"b1_kTst{k}", [128, 512], BF16) for k in range(2)]
    aux = {
        "sq2": [sb("b1_sq0", [128, 512], BF16), sb("b1_sq1", [128, 512], BF16)],
        "rows": [sb(f"b1_row{i}", [1, 512]) for i in range(4)],
    }
    pSrow = ps("b1_pSrow", [64, 512])
    aux["pS1"], aux["pS2"] = pSrow[0:1, :], pSrow[32:33, :]
    aux["pB1"] = ps("b1_pB1", [128, 512])
    aux["pB2"] = ps("b1_pB2", [128, 512])
    pQK = [ps(f"b1_pQK{k}", [128, 512]) for k in range(2)]
    pSP = ps("b1_pSP", [16, 512])
    pSB = ps("b1_pSB", [128, 512])

    te, ve, sc = "b1_te", "b1_ve", "b1_sc"
    f = {}

    for g in range(NT):
        k = g % 2
        t0 = g * 512
        t_ld = []
        for ci in range(8):
            t_ld.append(P.task("sync", lambda e, ci=ci, t0=t0, k=k: e.dma_start(
                out=y1Tsb[k][:, ci * 512:(ci + 1) * 512],
                in_=y1n[t0:t0 + 512, ci * 128:(ci + 1) * 128], transpose=True),
                deps=[f.get(("y1T", k, ci))], sem=f"b1_sl{k}", amt=16))
        t_ld_all = t_ld[-1]
        for ci in range(8):
            P.task("sync", lambda e, ci=ci, t0=t0, k=k: e.dma_start(
                out=y1T[ci * 128:(ci + 1) * 128, t0:t0 + 512],
                in_=y1Tsb[k][:, ci * 512:(ci + 1) * 512]),
                deps=[t_ld_all], sem=f"b1_sw{k}", amt=16)
        t_y1w_last = P.last_tok["sync"]

        chunks_in = [(y1Tsb[k][:, ci * 512:(ci + 1) * 512], t_ld_all) for ci in range(8)]
        chunks_out = [xnTsb[k][:, ci * 512:(ci + 1) * 512] for ci in range(8)]
        t_xn = emit_pln(P, nc, c, "b1_", 512, chunks_in, chunks_out, aux,
                        deps_in=[f.get(("xnT", k))])
        for ci in range(8):
            P.task("sync", lambda e, ci=ci, t0=t0, k=k: e.dma_start(
                out=xnT[ci * 128:(ci + 1) * 128, t0:t0 + 512],
                in_=xnTsb[k][:, ci * 512:(ci + 1) * 512]),
                deps=[t_xn[ci]], sem=f"b1_sx{k}", amt=16)
        t_xnw_last = P.last_tok["sync"]

        t_qev = []
        for cp in range(8):
            kq = cp % 2
            t_qm = []
            for ci in range(8):
                t_qm.append(P.task("tensor", lambda e, ci=ci, cp=cp, k=k, kq=kq: e.matmul(
                    pQK[kq][:], qwb[:, ci * 1024 + cp * 128: ci * 1024 + (cp + 1) * 128],
                    xnTsb[k][:, ci * 512:(ci + 1) * 512],
                    start=(ci == 0), stop=(ci == 7)),
                    deps=[t_xn[ci], t_w[-1]] + ([f.get(("pQK", kq))] if ci == 0 else []),
                    sem=te))
            t_qev.append(P.task("vector", lambda e, cp=cp, k=k, kq=kq: e.tensor_copy(
                qT8[k][:, cp * 512:(cp + 1) * 512], pQK[kq][:]),
                deps=[t_qm[-1]] + ([f.get(("qT8", k))] if cp == 0 else []), sem=ve))
            f[("pQK", kq)] = t_qev[-1]

        t_sp = []
        for ci in range(8):
            t_sp.append(P.task("tensor", lambda e, ci=ci, k=k: e.matmul(
                pSP[:], swb[:, ci * 16:(ci + 1) * 16], qT8[k][:, ci * 512:(ci + 1) * 512],
                start=(ci == 0), stop=(ci == 7)),
                deps=[t_qev[ci]] + ([f.get("pSP")] if ci == 0 else []), sem=te))
        t_sg = P.task("scalar", lambda e, k=k: e.activation(
            scalT[k][:], pSP[:], AF.Sigmoid), deps=[t_sp[-1], f.get(("scalT", k))], sem=sc)
        f["pSP"] = t_sg

        t_q_last = None
        for cp in range(8):
            kq = cp % 2
            t_sb2 = P.task("tensor", lambda e, cp=cp, k=k: e.matmul(
                pSB[:], selpair_sb[:, cp * 128:(cp + 1) * 128], scalT[k][:],
                start=True, stop=True), deps=[t_sg, t_sp_ld, f.get("pSB")], sem=te)
            t_qs = P.task("vector", lambda e, cp=cp, k=k, kq=kq: e.tensor_mul(
                qsTst[kq][:], qT8[k][:, cp * 512:(cp + 1) * 512], pSB[:]),
                deps=[t_sb2, f.get(("qsTst", kq))], sem=ve)
            f["pSB"] = t_qs
            t_qw = P.task("sync", lambda e, cp=cp, t0=t0, kq=kq: e.dma_start(
                out=qsT[cp * 128:(cp + 1) * 128, t0:t0 + 512], in_=qsTst[kq][:]),
                deps=[t_qs], sem=f"b1_sq{kq}", amt=16)
            f[("qsTst", kq)] = t_qw
            t_km = []
            for ci in range(8):
                t_km.append(P.task("tensor", lambda e, ci=ci, cp=cp, k=k, kq=kq: e.matmul(
                    pQK[kq][:], kwb[:, ci * 1024 + cp * 128: ci * 1024 + (cp + 1) * 128],
                    xnTsb[k][:, ci * 512:(ci + 1) * 512],
                    start=(ci == 0), stop=(ci == 7)),
                    deps=[f.get(("pQK", kq))] if ci == 0 else [], sem=te))
            t_ke = P.task("vector", lambda e, kq=kq: e.tensor_copy(
                kTst[kq][:], pQK[kq][:]), deps=[t_km[-1], f.get(("kTst", kq))], sem=ve)
            f[("pQK", kq)] = t_ke
            t_kw2 = P.task("sync", lambda e, cp=cp, t0=t0, kq=kq: e.dma_start(
                out=io["kTh"][cp * 128:(cp + 1) * 128, t0:t0 + 512], in_=kTst[kq][:]),
                deps=[t_ke], sem=f"b1_sk{kq}", amt=16)
            f[("kTst", kq)] = t_kw2
            t_q_last = t_ke

        for tt_ in range(4):
            ta = t0 // 128 + tt_
            for cg in range(2):
                kq = cg % 2
                t_vm = []
                for ci in range(8):
                    t_vm.append(P.task("tensor", lambda e, ci=ci, tt_=tt_, cg=cg, k=k, kq=kq: e.matmul(
                        pQK[kq][:],
                        xnTsb[k][:, ci * 512 + tt_ * 128: ci * 512 + (tt_ + 1) * 128],
                        vwb[:, ci * 1024 + cg * 512: ci * 1024 + (cg + 1) * 512],
                        start=(ci == 0), stop=(ci == 7)),
                        deps=[f.get(("pQK", kq))] if ci == 0 else [], sem=te))
                t_ve_ = P.task("vector", lambda e, ta=ta, cg=cg, kq=kq: e.tensor_copy(
                    vaugsb[:, ta * 1040 + cg * 520: ta * 1040 + (cg + 1) * 520]
                    .rearrange("p (h d) -> p h d", h=8, d=65)[:, :, 0:64],
                    pQK[kq].rearrange("p (h d) -> p h d", h=8, d=64)),
                    deps=[t_vm[-1]], sem=ve)
                f[("pQK", kq)] = t_ve_

        f[("xnT", k)] = [t_q_last, t_xnw_last, P.last_tok["tensor"]]
        f[("qT8", k)] = [P.last_tok["tensor"], P.last_tok["vector"]]
        for ci in range(8):
            f[("y1T", k, ci)] = [t_xn[ci], t_y1w_last]
        f[("scalT", k)] = P.last_tok["tensor"]

    t_ones = []
    for ta in range(T // 128):
        t_ones.append(P.task("vector", lambda e, ta=ta: e.memset(
            vaugsb[:, ta * 1040:(ta + 1) * 1040]
            .rearrange("p (h d) -> p h d", h=16, d=65)[:, :, 64:65], 1.0), sem=ve))
    return None



def emit_stageB2(P, nc, es, c, io, T, kTsb_unused, vaugsb, d0gen=None):
    QG = min(1024, T)
    NQ = T // QG
    NK = T // 128
    outT, qsT = io["outT"], io["qsT"]

    sb = lambda n, s, d=F32: es.enter_context(nc.sbuf_tensor(n, s, d))
    ps = lambda n, s, d=F32: es.enter_context(nc.psum_tensor(n, s, d))

    kTsb = [sb(f"b2_kT{i}", [128, T], BF16) for i in range(8)]
    t_lk = []
    for cp in range(8):
        t_lk.append(P.task("sync", lambda e, cp=cp: e.dma_start(
            out=kTsb[cp][:, :], in_=io["kTh"][cp * 128:(cp + 1) * 128, :]),
            sem="b2_lk", amt=16))
    qsg = sb("b2_qsg", [128, 8 * QG], BF16)
    expS = [[sb(f"b2_eS{hi}_{k}", [128, QG], BF16) for k in range(2)]
            for hi in range(2)]
    srow = [sb(f"b2_srow{hi}", [1, QG]) for hi in range(2)]
    rrow = [sb(f"b2_rrow{hi}", [1, QG]) for hi in range(2)]
    rrb = [sb(f"b2_rrb{hi}", [1, QG], BF16) for hi in range(2)]
    rBsb = [sb(f"b2_rB{hi}", [64, QG], BF16) for hi in range(2)]
    outst = [sb(f"b2_out{hi}", [64, QG], BF16) for hi in range(2)]
    outst32 = [sb(f"b2_o32{hi}", [64, QG]) for hi in range(2)]

    pS = [ps(f"b2_pS{hi}", [128, QG]) for hi in range(2)]
    pAV = [ps(f"b2_pAV{hi}", [65, QG]) for hi in range(2)]

    te, ve, sc = "b2_te", "b2_ve", "b2_sc"
    NCS = QG // 512
    f = {}

    for qg in range(NQ):
        q0 = QG * qg
        t_lq = []
        for cp in range(8):
            t_lq.append(P.task("sync", lambda e, cp=cp, q0=q0: e.dma_start(
                out=qsg[:, cp * QG:(cp + 1) * QG],
                in_=qsT[cp * 128:(cp + 1) * 128, q0:q0 + QG]),
                deps=[f.get("qsg")], sem="b2_lq", amt=16))
        t_lq_all = t_lq[-1]

        NFL = 3
        pend_norm = []
        for hp in range(8):
            heads = (2 * hp, 2 * hp + 1)
            t_ex = {0: [], 1: []}
            t_av = {0: [], 1: []}

            def avphase(kc, kp):
                for hi in (0, 1):
                    h = heads[hi]
                    for cs in range(NCS):
                        t_av[hi].append(P.task("tensor", lambda e, h=h, kc=kc,
                                               cs=cs, hi=hi, kp=kp: e.matmul(
                            pAV[hi][:, cs * 512:(cs + 1) * 512],
                            vaugsb[:, kc * 1040 + h * 65: kc * 1040 + (h + 1) * 65],
                            expS[hi][kp][:, cs * 512:(cs + 1) * 512],
                            start=(kc == 0), stop=(kc == NK - 1)),
                            deps=[t_ex[hi][kc]] +
                                 ([f.get(("pAV", hi))] if kc == 0 and cs == 0 else []),
                            sem=te))
                    f[("expS", hi, kp)] = t_av[hi][-1]

            def norm_tail(h, hi, t_rdy):
                t_rc = P.task("vector", lambda e, hi=hi: e.reciprocal(
                    rrow[hi][:], srow[hi][:]), deps=[t_rdy, f.get(("rrow", hi))],
                    sem=ve)
                t_rb = P.task("vector", lambda e, hi=hi: e.tensor_copy(
                    rrb[hi][:], rrow[hi][:]), deps=[t_rc, f.get(("rrb", hi))], sem=ve)
                t_bc = []
                for cs in range(NCS):
                    t_bc.append(P.task("tensor", lambda e, hi=hi, cs=cs: e.matmul(
                        pS[hi][0:64, cs * 512:(cs + 1) * 512], c["ones1b"][0:1, 0:64],
                        rrb[hi][0:1, cs * 512:(cs + 1) * 512], start=True, stop=True),
                        deps=[t_rb] + ([f.get(("pS", hi))] if cs == 0 else []), sem=te))
                t_rB = P.task("vector", lambda e, hi=hi: e.tensor_copy(
                    rBsb[hi][:], pS[hi][0:64, :]),
                    deps=[t_bc[-1], f.get(("rBsb", hi))], sem=ve)
                f[("pS", hi)] = t_rB
                t_o = P.task("vector", lambda e, hi=hi: e.tensor_mul(
                    outst[hi][:], outst32[hi][:], rBsb[hi][:]),
                    deps=[t_rB, f.get(("outst", hi))], sem=ve)
                f[("srow", hi)] = t_rc
                f[("rrow", hi)] = t_rb
                f[("rrb", hi)] = t_bc[-1]
                f[("rBsb", hi)] = t_o
                f[("o32", hi)] = t_o
                t_w = P.task("sync", lambda e, h=h, q0=q0, hi=hi: e.dma_start(
                    out=outT[h * 64:(h + 1) * 64, q0:q0 + QG], in_=outst[hi][:]),
                    deps=[t_o], sem=f"b2_so{hi}", amt=16)
                f[("outst", hi)] = t_w

            for kc in range(NK):
                kp = kc % 2
                for hi in (0, 1):
                    h = heads[hi]
                    hr = slice((h % 2) * 64, (h % 2) * 64 + 64)
                    qcol = (h // 2) * QG
                    t_s = []
                    for cs in range(NCS):
                        t_s.append(P.task("tensor", lambda e, h=h, kc=kc, cs=cs,
                                          hi=hi, hr=hr, qcol=qcol: e.matmul(
                            pS[hi][:, cs * 512:(cs + 1) * 512],
                            kTsb[h // 2][hr, kc * 128:(kc + 1) * 128],
                            qsg[hr, qcol + cs * 512: qcol + (cs + 1) * 512],
                            start=True, stop=True),
                            deps=([t_lq_all, t_lk[-1], f.get(("pS", hi))]
                                  if cs == 0 else []), sem=te))
                    t_e = P.task("scalar", lambda e, hi=hi, kp=kp: e.activation(
                        expS[hi][kp][:], pS[hi][:], AF.Exp),
                        deps=[t_s[-1], f.get(("expS", hi, kp))], sem=sc)
                    f[("pS", hi)] = t_e
                    t_ex[hi].append(t_e)
                if kc > 0:
                    avphase(kc - 1, 1 - kp)
                if kc == NFL and pend_norm:
                    for args in pend_norm:
                        norm_tail(*args)
                    pend_norm = []
            avphase(NK - 1, (NK - 1) % 2)
            for hi in (0, 1):
                t_sr = P.task("vector", lambda e, hi=hi: e.tensor_copy(
                    srow[hi][:], pAV[hi][64:65, :]),
                    deps=[t_av[hi][-1], f.get(("srow", hi))], sem=ve)
                t_oc = P.task("vector", lambda e, hi=hi: e.tensor_copy(
                    outst32[hi][:], pAV[hi][0:64, :]),
                    deps=[t_av[hi][-1], f.get(("o32", hi))], sem=ve)
                f[("pAV", hi)] = t_oc
                pend_norm.append((heads[hi], hi, [t_sr, t_oc]))
            if d0gen is not None:
                for _ in range(8):
                    try:
                        next(d0gen)
                    except StopIteration:
                        break
        for args in pend_norm:
            norm_tail(*args)
        pend_norm = []
        f["qsg"] = P.last_tok["tensor"]
    return None



def emit_stageB3(P, nc, es, c, io, T, ind8sb_unused):
    NT = T // 512
    outT, xnT, x1T = io["outT"], io["xnT"], io["x1T"]

    sb = lambda n, s, d=F32: es.enter_context(nc.sbuf_tensor(n, s, d))
    ps = lambda n, s, d=F32: es.enter_context(nc.psum_tensor(n, s, d))

    gwb = sb("b3_gw", [128, 8192], BF16)
    owb = sb("b3_ow", [128, 8192], BF16)
    mgwb = sb("b3_mgw", [128, 64], BF16)
    gbs = sb("b3_gb", [128, 8])
    n1gs = sb("b3_n1g", [128, 8])
    n1bs = sb("b3_n1b", [128, 8])
    mgbs = sb("b3_mgb", [1, 8])
    mgbB = sb("b3_mgbB", [128, 8])
    t_w = []
    for wsb, wname in [(gwb, "gw"), (owb, "ow")]:
        for ci in range(8):
            t_w.append(P.task("gpsimd", lambda e, wsb=wsb, wname=wname, ci=ci:
                e.dma_start(out=wsb[:, ci * 1024:(ci + 1) * 1024],
                            in_=io[wname][ci * 128:(ci + 1) * 128, :]),
                sem="b3_gw", amt=16))
    for ci in range(8):
        t_w.append(P.task("gpsimd", lambda e, ci=ci: e.dma_start(
            out=mgwb[:, ci * 8:(ci + 1) * 8],
            in_=io["mgw"][ci * 128:(ci + 1) * 128, :]), sem="b3_gw", amt=16))
    t_v = []
    for tsb, name in [(gbs, "gb"), (n1gs, "n1g"), (n1bs, "n1b"), (mgbs, "mgb")]:
        t_v.append(P.task("sync", lambda e, tsb=tsb, name=name: e.dma_start(
            out=tsb[:, :], in_=io[name]), sem="b3_sv", amt=16))

    oTsb = [sb(f"b3_oT{k}", [128, 4096], BF16) for k in range(2)]
    xnTsb = [sb(f"b3_xnT{k}", [128, 4096], BF16) for k in range(2)]
    gTst = sb("b3_gT", [128, 512], BF16)
    ogsb = sb("b3_og", [128, 4096], BF16)
    zsb = sb("b3_z", [128, 4096], BF16)
    x1sb = [sb(f"b3_x1{k}", [128, 4096], BF16) for k in range(2)]
    lgsb = sb("b3_lg", [128, 8])
    m8 = sb("b3_m8", [128, 8])
    i8 = sb("b3_i8", [128, 8], U32)
    chu = sb("b3_chu", [128, 1], U32)
    chb = sb("b3_chb", [128, 1], BF16)
    chrow = sb("b3_chrow", [1, 512], BF16)
    indloc = [sb(f"b3_indloc{k}", [1, 512], BF16) for k in range(2)]
    aux = {
        "sq2": [sb("b3_sq0", [128, 512], BF16), sb("b3_sq1", [128, 512], BF16)],
        "rows": [sb(f"b3_row{i}", [1, 512]) for i in range(4)],
    }
    pSrow = ps("b3_pSrow", [64, 512])
    aux["pS1"], aux["pS2"] = pSrow[0:1, :], pSrow[32:33, :]
    aux["pB1"] = ps("b3_pB1", [128, 512])
    aux["pB2"] = ps("b3_pB2", [128, 512])
    pG1 = ps("b3_pG1", [128, 512])
    pG2 = ps("b3_pG2", [128, 512])
    pM = ps("b3_pM", [128, 512])
    pChB = ps("b3_pChB", [1, 128], BF16)

    te, ve, sc = "b3_te", "b3_ve", "b3_sc"
    f = {}

    t_mb = P.task("tensor", lambda e: e.matmul(
        pM[:, 0:8], c["ones1f"], mgbs[:], start=True, stop=True),
        deps=[t_v[-1]], sem=te)
    t_mb2 = P.task("vector", lambda e: e.tensor_copy(mgbB[:, 0:8], pM[:, 0:8]),
                   deps=[t_mb], sem=ve)
    f["pM"] = t_mb2

    for g in range(NT):
        k = g % 2
        t0 = g * 512
        t_lo = []
        t_lx = []
        for ci in range(8):
            t_lo.append(P.task("sync", lambda e, ci=ci, t0=t0, k=k: e.dma_start(
                out=oTsb[k][:, ci * 512:(ci + 1) * 512],
                in_=outT[ci * 128:(ci + 1) * 128, t0:t0 + 512]),
                deps=[f.get(("oT", k))], sem=f"b3_so{k}", amt=16))
            t_lx.append(P.task("sync", lambda e, ci=ci, t0=t0, k=k: e.dma_start(
                out=xnTsb[k][:, ci * 512:(ci + 1) * 512],
                in_=xnT[ci * 128:(ci + 1) * 128, t0:t0 + 512]),
                deps=[f.get(("xnT", k))], sem=f"b3_sx{k}", amt=16))
        t_lo_all, t_lx_all = t_lo[-1], t_lx[-1]

        t_og = []
        for cp in range(8):
            t_gm = []
            for ci in range(8):
                t_gm.append(P.task("tensor", lambda e, ci=ci, cp=cp, k=k: e.matmul(
                    pG1[:], gwb[:, ci * 1024 + cp * 128: ci * 1024 + (cp + 1) * 128],
                    oTsb[k][:, ci * 512:(ci + 1) * 512],
                    start=(ci == 0), stop=(ci == 7)),
                    deps=[t_lo_all, t_w[-1]] + ([f.get("pG1")] if ci == 0 else []),
                    sem=te))
            t_sg = P.task("scalar", lambda e, cp=cp: e.activation(
                gTst[:], pG1[:], AF.Sigmoid, bias=gbs[:, cp:cp + 1]),
                deps=[t_gm[-1], f.get("gTst")], sem=sc)
            f["pG1"] = t_sg
            t_og.append(P.task("vector", lambda e, cp=cp, k=k: e.tensor_mul(
                ogsb[:, cp * 512:(cp + 1) * 512], oTsb[k][:, cp * 512:(cp + 1) * 512],
                gTst[:]), deps=[t_sg, f.get(("og", cp))], sem=ve))
            f["gTst"] = t_og[-1]

        t_z = []
        for cp in range(8):
            t_am = []
            for ci in range(8):
                t_am.append(P.task("tensor", lambda e, ci=ci, cp=cp: e.matmul(
                    pG2[:], owb[:, ci * 1024 + cp * 128: ci * 1024 + (cp + 1) * 128],
                    ogsb[:, ci * 512:(ci + 1) * 512],
                    start=(ci == 0), stop=(ci == 7)),
                    deps=[t_og[ci]] + ([f.get("pG2")] if ci == 0 else []), sem=te))
            t_z.append(P.task("vector", lambda e, cp=cp, k=k: e.tensor_add(
                zsb[:, cp * 512:(cp + 1) * 512], pG2[:],
                xnTsb[k][:, cp * 512:(cp + 1) * 512]),
                deps=[t_am[-1], t_lx_all, f.get(("z", cp))], sem=ve))
            f["pG2"] = t_z[-1]
        for cp in range(8):
            f[("og", cp)] = P.last_tok["tensor"]

        chunks_in = [(zsb[:, ci * 512:(ci + 1) * 512], t_z[ci]) for ci in range(8)]
        chunks_out = [x1sb[k][:, ci * 512:(ci + 1) * 512] for ci in range(8)]
        gain = [n1gs[:, ci:ci + 1] for ci in range(8)]
        bias = [n1bs[:, ci:ci + 1] for ci in range(8)]
        t_x1 = emit_pln(P, nc, c, "b3_", 512, chunks_in, chunks_out, aux,
                        deps_in=[f.get(("x1", k)), t_v[-1]], gain=gain, bias=bias)
        for ci in range(8):
            f[("z", ci)] = [t_x1[-1]] + [P.last_tok["tensor"]]
            P.task("sync", lambda e, ci=ci, t0=t0, k=k: e.dma_start(
                out=x1T[ci * 128:(ci + 1) * 128, t0:t0 + 512],
                in_=x1sb[k][:, ci * 512:(ci + 1) * 512]),
                deps=[t_x1[ci]], sem=f"b3_sw{k}", amt=16)
        t_x1w = P.last_tok["sync"]

        t_tile = []
        for tt_ in range(4):
            t_lm = []
            for ci in range(8):
                t_lm.append(P.task("tensor", lambda e, ci=ci, tt_=tt_, k=k: e.matmul(
                    pM[:, 0:8],
                    x1sb[k][:, ci * 512 + tt_ * 128: ci * 512 + (tt_ + 1) * 128],
                    mgwb[:, ci * 8:(ci + 1) * 8],
                    start=(ci == 0), stop=(ci == 7)),
                    deps=[t_x1[ci]] + ([f.get("pM")] if ci == 0 else []), sem=te))
            t_lg = P.task("vector", lambda e: e.tensor_add(
                lgsb[:], pM[:, 0:8], mgbB[:, 0:8]),
                deps=[t_lm[-1], t_mb2, f.get("lgsb")], sem=ve)
            f["pM"] = t_lg
            t_m8 = P.task("vector", lambda e: e.max(m8[:], lgsb[:]),
                          deps=[t_lg, f.get("m8")], sem=ve)
            t_i8 = P.task("vector", lambda e: e.max_index(i8[:], m8[:], lgsb[:]),
                          deps=[t_m8, f.get("i8")], sem=ve)
            f["lgsb"] = t_i8
            f["m8"] = t_i8
            t_ch = P.task("vector", lambda e: e.tensor_tensor(
                chu[:], i8[:, 0:1], i8[:, 1:2], ALU.max),
                deps=[t_i8, f.get("chu")], sem=ve)
            f["i8"] = t_ch
            t_cb = P.task("vector", lambda e: e.tensor_copy(chb[:], chu[:]),
                          deps=[t_ch, f.get("chb")], sem=ve)
            f["chu"] = t_cb
            t_tr = P.task("tensor", lambda e: e.transpose(
                pChB[:], chb[:], c["eyeb"]), deps=[t_cb, f.get("pChB")], sem=te)
            f["chb"] = t_tr
            t_cr = P.task("vector", lambda e, tt_=tt_: e.tensor_copy(
                chrow[0:1, tt_ * 128:(tt_ + 1) * 128], pChB[:]),
                deps=[t_tr] + ([f.get("chrow")] if tt_ == 0 else []), sem=ve)
            f["pChB"] = t_cr
            t_tile.append(t_cr)
        t_i = None
        for ei in range(8):
            ke = ei % 2
            t_i = P.task("vector", lambda e, ei=ei, t0=t0, ke=ke: e.tensor_scalar(
                indloc[ke][:], chrow[:], float(ei), None, ALU.is_equal),
                deps=[t_tile[-1], f.get(("indloc", ke))], sem=ve)
            t_iw = P.task("sync", lambda e, ei=ei, t0=t0, ke=ke: e.dma_start(
                out=io["indh"][0:1, ei * T + t0: ei * T + t0 + 512],
                in_=indloc[ke][:]), deps=[t_i], sem=f"b3_si{ke}", amt=16)
            f[("indloc", ke)] = t_iw
        f["chrow"] = t_i
        f[("oT", k)] = [P.last_tok["tensor"]]
        f[("xnT", k)] = [t_z[-1]]
        f[("x1", k)] = [P.last_tok["tensor"], t_x1w]
    return None



def emit_stageB4(P, nc, es, c, io, T, ind8sb):
    NT = T // 512
    x1T, y1T, y2T = io["x1T"], io["y1T"], io["y2T"]

    sb = lambda n, s, d=F32: es.enter_context(nc.sbuf_tensor(n, s, d))
    ps = lambda n, s, d=F32: es.enter_context(nc.psum_tensor(n, s, d))

    macc = [sb(f"b4_ma_{i}", [128, T], BF16) for i in range(8)]
    x1g = [sb(f"b4_x1g{k}", [128, 4096], BF16) for k in range(2)]
    indB2 = [sb(f"b4_indB{k}", [128, 512], BF16) for k in range(2)]
    w1sb = sb("b4_w1", [128, 16384], BF16)
    w2sb = sb("b4_w2", [128, 16384], BF16)
    heb2 = [sb("b4_he0", [128, 8192], BF16), sb("b4_he1", [128, 8192], BF16)]
    sqb = sb("b4_sqb", [128, 512], BF16)
    b1s = sb("b4_b1", [128, 16])
    lgs = sb("b4_lg", [128, 16])
    lbs = sb("b4_lb", [128, 16])
    b2s = sb("b4_b2", [128, 8])
    rscs = sb("b4_rsc", [1, 8])
    rscB = sb("b4_rscB", [128, 8])
    n2gs = sb("b4_n2g", [128, 8])
    n2bs = sb("b4_n2b", [128, 8])
    indB = indB2
    tmpo = sb("b4_tmpo", [128, 512], BF16)
    zq = heb2[0][:, 0:4096]
    ffn = heb2[0][:, 4096:8192]
    y1sb = w1sb[:, 0:4096]
    y2sb = w1sb[:, 4096:8192]
    _br = [sb(f"b4_row{i}", [1, 512]) for i in range(3)]
    aux = {
        "sq2": [sqb, sb("b4_sqb1", [128, 512], BF16)],
        "rows": [_br[0], _br[1], _br[2], _br[1]],
    }
    pSrow = ps("b4_pSrow", [64, 512])
    aux["pS1"], aux["pS2"] = pSrow[0:1, :], pSrow[32:33, :]
    aux["pB1"] = ps("b4_pB1", [128, 512])
    aux["pB2"] = ps("b4_pB2", [128, 512])
    pH = [ps(f"b4_pH{k}", [128, 512]) for k in range(3)]
    pO = [ps(f"b4_pO{k}", [128, 512]) for k in range(2)]

    te, ve, sc = "b4_te", "b4_ve", "b4_sc"
    f = {}

    x1r = x1T.rearrange("(c p) t -> p c t", p=128)
    t_lv = []
    for tsb, name in [(rscs, "rsc"), (n2gs, "n2g"), (n2bs, "n2b")]:
        t_lv.append(P.task("sync", lambda e, tsb=tsb, name=name: e.dma_start(
            out=tsb[:, :], in_=io[name]), sem="b4_sv", amt=16))
    t_rb = P.task("tensor", lambda e: e.matmul(
        pH[0][:, 0:8], c["ones1f"], rscs[:], start=True, stop=True),
        deps=[t_lv[-1]], sem=te)
    t_rb2 = P.task("vector", lambda e: e.tensor_copy(rscB[:, 0:8], pH[0][:, 0:8]),
                   deps=[t_rb], sem=ve)
    f[("pH", 0)] = t_rb2
    t_mz = []
    for ci in range(8):
        t_mz.append(P.task("vector", lambda e, ci=ci: e.memset(
            macc[ci][:, :], 0.0), sem=ve))

    def load_w1(ei, dep):
        tok = P.task("sync", lambda e, ei=ei: e.dma_start(
            out=w1sb.rearrange("p (ci m) -> p ci m", ci=8),
            in_=io["w1b"][ei].rearrange("(ci p) m -> p ci m", p=128)),
            deps=[dep], sem="b4_g1", amt=16)
        f["w1tok"] = tok
        return tok

    load_w1(0, None)
    for ei in range(8):
        par = ei % 2
        t_w1 = [f["w1tok"]]
        t_w2 = [P.task("sync", lambda e, ei=ei: e.dma_start(
            out=w2sb.rearrange("p (fc m) -> p fc m", fc=16),
            in_=io["w2b"][ei].rearrange("(fc p) m -> p fc m", p=128)),
            deps=[f.get("w2sb")], sem="b4_g2", amt=16)]
        t_vecs = []
        for tsb, name in [(b1s, "b1r"), (lgs, "lgr"), (lbs, "lbr"), (b2s, "b2r")]:
            t_vecs.append(P.task("sync", lambda e, tsb=tsb, name=name, ei=ei:
                e.dma_start(out=tsb[:, :], in_=io[name][ei]),
                deps=[f.get("vecs")], sem="b4_se", amt=16))
        t_w1a, t_w2a, t_va = t_w1[-1], t_w2[-1], t_vecs[-1]

        def ophase(kg, tgs, t_zg, t_ibc, ei):
            for cp in range(8):
                kp = cp % 2
                t_om = []
                for fc in range(16):
                    t_om.append(P.task("tensor", lambda e, fc=fc, cp=cp, kp=kp,
                                       kg=kg: e.matmul(
                        pO[kp][:],
                        w2sb[:, fc * 1024 + cp * 128: fc * 1024 + (cp + 1) * 128],
                        heb2[kg][:, fc * 512:(fc + 1) * 512],
                        start=(fc == 0), stop=(fc == 15)),
                        deps=[t_zg[fc], t_w2a] + ([f.get(("pO", kp))] if fc == 0 else []),
                        sem=te))
                t_o1 = P.task("vector", lambda e, kp=kp, cp=cp, ei=ei: e.tensor_scalar(
                    tmpo[:], pO[kp][:], b2s[:, cp:cp + 1], rscB[:, ei:ei + 1],
                    ALU.add, ALU.mult),
                    deps=[t_om[-1], t_rb2, t_va, f.get("tmpo")], sem=ve)
                f[("pO", kp)] = t_o1
                t_o2 = P.task("vector", lambda e, kg=kg: e.tensor_mul(
                    tmpo[:], tmpo[:], indB[kg][:]),
                    deps=[t_o1, t_ibc], sem=ve)
                t_o3 = P.task("vector", lambda e, cp=cp, tgs=tgs: e.tensor_add(
                    macc[cp][:, tgs], macc[cp][:, tgs], tmpo[:]),
                    deps=[t_o2, t_mz[cp]], sem=ve)
                f["tmpo"] = t_o3
            f[("indB", kg)] = t_o2
            for fc in range(16):
                f[("heb", kg, fc)] = P.last_tok["tensor"]

        pend = None
        last_te_of_e = None
        for g in range(NT):
            kg = g % 2
            t0 = g * 512
            tgs = slice(t0, t0 + 512)
            t_lx1 = P.task("sync" if kg == 0 else "gpsimd",
                           lambda e, t0=t0, kg=kg: e.dma_start(
                out=x1g[kg].rearrange("p (c t) -> p c t", c=8),
                in_=x1r[:, :, t0:t0 + 512]),
                deps=[f.get(("x1g", kg))], sem=f"b4_sx{kg}", amt=16)
            t_ibc = P.task("gpsimd", lambda e, ei=ei, t0=t0, kg=kg: e.dma_start(
                out=indB[kg][:],
                in_=bass.AP(io["indh"].tensor, ei * T + t0, [[0, 128], [1, 512]])),
                deps=[f.get(("indB", kg))], sem=f"b4_gi{kg}", amt=16)

            t_he = []
            for fc in range(16):
                kp = fc % 3
                t_hm = []
                for ci in range(8):
                    t_hm.append(P.task("tensor", lambda e, ei=ei, fc=fc, ci=ci,
                                       kp=kp, kg=kg, par=par, tgs=tgs: e.matmul(
                        pH[kp][:],
                        w1sb[:, ci * 2048 + fc * 128: ci * 2048 + (fc + 1) * 128],
                        x1g[kg][:, ci * 512:(ci + 1) * 512],
                        start=(ci == 0), stop=(ci == 7)),
                        deps=([t_w1a, t_lx1, f.get(("pH", kp))] if ci == 0 else []),
                        sem=te))
                t_he.append(P.task("vector", lambda e, fc=fc, kp=kp, kg=kg: e.tensor_scalar(
                    heb2[kg][:, fc * 512:(fc + 1) * 512], pH[kp][:], b1s[:, fc:fc + 1],
                    None, ALU.add),
                    deps=[t_hm[-1], t_va, f.get(("heb", kg, fc))], sem=ve))
                f[("pH", kp)] = t_he[-1]
            f[("x1g", kg)] = [P.last_tok["tensor"]]

            t_he_last_mm = P.last_tok["tensor"]
            if pend is not None:
                ophase(*pend)

            chunks_in = [(heb2[kg][:, fc * 512:(fc + 1) * 512], t_he[fc])
                         for fc in range(16)]
            chunks_out = [heb2[kg][:, fc * 512:(fc + 1) * 512] for fc in range(16)]
            gain = [lgs[:, fc:fc + 1] for fc in range(16)]
            bias = [lbs[:, fc:fc + 1] for fc in range(16)]
            t_zg = emit_pln(P, nc, c, "b4_", 512, chunks_in, chunks_out, aux,
                            deps_in=[t_va], gain=gain, bias=bias, fuse_act=AF.Gelu)
            pend = (kg, tgs, t_zg, t_ibc, ei)
            if g == NT - 1 and ei + 1 < 8:
                load_w1(ei + 1, t_he_last_mm)
        ophase(*pend)
        last_te_of_e = P.last_tok["tensor"]
        f["w2sb"] = last_te_of_e
        f["vecs"] = P.last_tok["vector"]

    for g in range(NT):
        kg = g % 2
        t0 = g * 512
        tgs = slice(t0, t0 + 512)
        t_lx1 = P.task("sync", lambda e, t0=t0, kg=kg: e.dma_start(
            out=x1g[kg].rearrange("p (c t) -> p c t", c=8),
            in_=x1r[:, :, t0:t0 + 512]),
            deps=[f.get(("x1g", kg))], sem=f"b4_sx{kg}", amt=16)
        t_ly = []
        for ci in range(8):
            t_ly.append(P.task("sync", lambda e, ci=ci, t0=t0: e.dma_start(
                out=y1sb[:, ci * 512:(ci + 1) * 512],
                in_=y1T[ci * 128:(ci + 1) * 128, t0:t0 + 512]),
                deps=[f.get("y1sb"), f.get("w1sb"), f.get("vecs")],
                sem="b4_sy", amt=16))
        t_zq = []
        for ci in range(8):
            t_zq.append(P.task("vector", lambda e, ci=ci, kg=kg, tgs=tgs:
                e.scalar_tensor_tensor(
                    zq[:, ci * 512:(ci + 1) * 512],
                    x1g[kg][:, ci * 512:(ci + 1) * 512], 2.0,
                    macc[ci][:, tgs], ALU.mult, ALU.add),
                deps=[P.last_tok["vector"], t_lx1, f.get(("zq", ci)),
                      f.get(("heb", 0, 0)), f.get(("heb", 0, 15)),
                      f.get(("heb", 1, 0)), f.get(("heb", 1, 15))], sem=ve))
        f[("x1g", kg)] = [t_zq[-1]]
        chunks_in = [(zq[:, ci * 512:(ci + 1) * 512], t_zq[ci]) for ci in range(8)]
        chunks_out = [ffn[:, ci * 512:(ci + 1) * 512] for ci in range(8)]
        gain = [n2gs[:, ci:ci + 1] for ci in range(8)]
        bias = [n2bs[:, ci:ci + 1] for ci in range(8)]
        t_f = emit_pln(P, nc, c, "b4_", 512, chunks_in, chunks_out, aux,
                       deps_in=[t_lv[-1], f.get("ffnw")], gain=gain, bias=bias)
        for ci in range(8):
            t_y2 = P.task("vector", lambda e, ci=ci: e.tensor_add(
                y2sb[:, ci * 512:(ci + 1) * 512], ffn[:, ci * 512:(ci + 1) * 512],
                y1sb[:, ci * 512:(ci + 1) * 512]),
                deps=[t_f[ci], t_ly[-1], f.get(("y2", ci))], sem=ve)
            t_wr = P.task("sync", lambda e, ci=ci, t0=t0: e.dma_start(
                out=y2T[ci * 128:(ci + 1) * 128, t0:t0 + 512],
                in_=y2sb[:, ci * 512:(ci + 1) * 512]),
                deps=[t_y2], sem="b4_sw", amt=16)
            f[("y2", ci)] = t_wr
            f[("zq", ci)] = t_f[-1]
        f["y1sb"] = [t_y2]
        f["ffnw"] = t_wr
    return None



def emit_stageC(P, nc, es, c, io, T):
    NT = T // 512
    NL = T // 128
    y2T, xn3T, y3T = io["y2T"], io["xn3T"], io["y3T"]

    sb = lambda n, s, d=F32: es.enter_context(nc.sbuf_tensor(n, s, d))

    vC = sb("c_v", [128, NL * 1024], BF16)
    medb = sb("c_med", [128, 512], BF16)
    mkv_all = sb("c_mkv", [64, 1024], BF16)
    t_w = []
    for ci in range(8):
        t_w.append(P.task("gpsimd", lambda e, ci=ci: e.dma_start(
            out=medb[:, ci * 64:(ci + 1) * 64],
            in_=io["medT"][ci * 128:(ci + 1) * 128, :]), sem="c_gw", amt=16))

    te, ve, sc = "c_te", "c_ve", "c_sc"
    f = {}

    es12 = ExitStack()
    kCT = [es12.enter_context(nc.sbuf_tensor(f"c_kT{i}", [128, T], BF16))
           for i in range(8)]
    with ExitStack() as es1:
        sb1 = lambda n, s, d=F32: es1.enter_context(nc.sbuf_tensor(n, s, d))
        ps1 = lambda n, s, d=F32: es1.enter_context(nc.psum_tensor(n, s, d))
        wkvb = sb1("c_wkv", [128, 16384], BF16)
        for ci in range(8):
            t_w.append(P.task("gpsimd", lambda e, ci=ci: e.dma_start(
                out=wkvb[:, ci * 2048:(ci + 1) * 2048],
                in_=io["wkv"][ci * 128:(ci + 1) * 128, :]), sem="c_gw", amt=16))
        t_w_all = t_w[-1]
        _y2t = sb1("c_y2", [128, 4096], BF16)
        _xnt = sb1("c_xn", [128, 4096], BF16)
        y2sb = [_y2t, _y2t]
        xnsb = [_xnt, _xnt]
        aux = {
            "sq2": [sb1("c_sq0", [128, 512], BF16), sb1("c_sq1", [128, 512], BF16)],
            "rows": [sb1(f"c_row{i}", [1, 512]) for i in range(4)],
        }
        pSrow = ps1("c_pSrow", [64, 512])
        aux["pS1"], aux["pS2"] = pSrow[0:1, :], pSrow[32:33, :]
        aux["pB1"] = ps1("c_pB1", [128, 512])
        aux["pB2"] = ps1("c_pB2", [128, 512])
        pKV = [ps1(f"c_pKV{k}", [128, 512]) for k in range(2)]
        for g in range(NT):
            k = g % 2
            t0 = g * 512
            t_ld = []
            for ci in range(8):
                t_ld.append(P.task("sync", lambda e, ci=ci, t0=t0, k=k: e.dma_start(
                    out=y2sb[k][:, ci * 512:(ci + 1) * 512],
                    in_=y2T[ci * 128:(ci + 1) * 128, t0:t0 + 512]),
                    deps=[f.get(("y2",))], sem=f"c_sl{k}", amt=16))
            t_ld_all = t_ld[-1]
            chunks_in = [(y2sb[k][:, ci * 512:(ci + 1) * 512], t_ld_all) for ci in range(8)]
            chunks_out = [xnsb[k][:, ci * 512:(ci + 1) * 512] for ci in range(8)]
            t_xn = emit_pln(P, nc, c, "c_", 512, chunks_in, chunks_out, aux,
                            deps_in=[f.get(("xn",))])
            for ci in range(8):
                P.task("sync", lambda e, ci=ci, t0=t0, k=k: e.dma_start(
                    out=xn3T[ci * 128:(ci + 1) * 128, t0:t0 + 512],
                    in_=xnsb[k][:, ci * 512:(ci + 1) * 512]),
                    deps=[t_xn[ci]], sem=f"c_sx{k}", amt=16)
            t_xnw = P.last_tok["sync"]
            for cp in range(8):
                kq = cp % 2
                t_km = []
                for ci in range(8):
                    t_km.append(P.task("tensor", lambda e, ci=ci, cp=cp, k=k, kq=kq: e.matmul(
                        pKV[kq][:], wkvb[:, ci * 2048 + cp * 128: ci * 2048 + (cp + 1) * 128],
                        xnsb[k][:, ci * 512:(ci + 1) * 512],
                        start=(ci == 0), stop=(ci == 7)),
                        deps=[t_xn[ci]] + ([t_w_all, f.get(("pKV", kq))] if ci == 0 else []),
                        sem=te))
                t_ke = P.task("vector", lambda e, cp=cp, t0=t0, kq=kq: e.tensor_copy(
                    kCT[cp][:, t0:t0 + 512], pKV[kq][:]), deps=[t_km[-1]], sem=ve)
                f[("pKV", kq)] = t_ke
            for tt_ in range(4):
                ta = t0 // 128 + tt_
                for cg in range(2):
                    kq = cg % 2
                    t_vm = []
                    for ci in range(8):
                        t_vm.append(P.task("tensor", lambda e, ci=ci, tt_=tt_, cg=cg,
                                           k=k, kq=kq: e.matmul(
                            pKV[kq][:],
                            xnsb[k][:, ci * 512 + tt_ * 128: ci * 512 + (tt_ + 1) * 128],
                            wkvb[:, ci * 2048 + 1024 + cg * 512: ci * 2048 + 1024 + (cg + 1) * 512],
                            start=(ci == 0), stop=(ci == 7)),
                            deps=[f.get(("pKV", kq))] if ci == 0 else [], sem=te))
                    t_ve_ = P.task("vector", lambda e, ta=ta, cg=cg, kq=kq: e.tensor_copy(
                        vC[:, ta * 1024 + cg * 512: ta * 1024 + (cg + 1) * 512],
                        pKV[kq][:]), deps=[t_vm[-1]], sem=ve)
                    f[("pKV", kq)] = t_ve_
            f[("y2",)] = [t_xn[-1], P.last_tok["tensor"]]
            f[("xn",)] = [P.last_tok["tensor"], t_xnw]
        t_c1 = [P.last_tok[e] for e in ("tensor", "vector", "sync")]
    P.barrier()

    with ExitStack() as es2:
        sb2 = lambda n, s, d=F32: es2.enter_context(nc.sbuf_tensor(n, s, d))
        ps2 = lambda n, s, d=F32: es2.enter_context(nc.psum_tensor(n, s, d))
        expMK = [sb2(f"c_eMK{k}", [128, NL * 64], BF16) for k in range(2)]
        srow = sb2("c_ms", [1, 64])
        rrow = sb2("c_mr", [1, 64])
        rrowb = sb2("c_mrb", [1, 64], BF16)
        rcol = sb2("c_rcol", [64, 1])
        pMS = [ps2(f"c_pMS{k}", [128, 512]) for k in range(2)]
        pMKV = ps2("c_pMKV", [64, 64])
        pSums = ps2("c_pSums", [64, 64])
        pTc = ps2("c_pTc", [64, 64], BF16)
        for h in range(16):
            kh = h % 2
            hr = slice((h % 2) * 64, (h % 2) * 64 + 64)
            mc = (h // 2) * 64
            t_ex = []
            for lc in range(NL):
                kb = lc % 2
                t_ms = P.task("tensor", lambda e, h=h, lc=lc, kb=kb, hr=hr,
                              mc=mc: e.matmul(
                    pMS[kb][:, 0:64],
                    kCT[h // 2][hr, lc * 128:(lc + 1) * 128],
                    medb[hr, mc:mc + 64], start=True, stop=True),
                    deps=t_c1 + [f.get(("pMS", kb))], sem=te)
                t_e = P.task("scalar", lambda e, lc=lc, kb=kb, kh=kh: e.activation(
                    expMK[kh][:, lc * 64:(lc + 1) * 64], pMS[kb][:, 0:64],
                    AF.Exp, scale=0.125),
                    deps=[t_ms, f.get(("expMK", kh))] if lc == 0 else [t_ms], sem=sc)
                f[("pMS", kb)] = t_e
                t_ex.append(t_e)
            t_sm = []
            t_mk = []
            for lc in range(NL):
                t_sm.append(P.task("tensor", lambda e, lc=lc, kh=kh: e.matmul(
                    pSums[0:1, :], c["onescb"], expMK[kh][:, lc * 64:(lc + 1) * 64],
                    start=(lc == 0), stop=(lc == NL - 1)),
                    deps=[t_ex[lc]] + ([f.get("pSums")] if lc == 0 else []), sem=te))
                t_mk.append(P.task("tensor", lambda e, lc=lc, kh=kh, h=h: e.matmul(
                    pMKV[:], expMK[kh][:, lc * 64:(lc + 1) * 64],
                    vC[:, lc * 1024 + h * 64: lc * 1024 + (h + 1) * 64],
                    start=(lc == 0), stop=(lc == NL - 1)),
                    deps=[t_ex[lc]] + ([f.get("pMKV")] if lc == 0 else []), sem=te))
            f[("expMK", kh)] = [t_sm[-1], t_mk[-1]]
            t_s1 = P.task("vector", lambda e: e.tensor_copy(srow[:], pSums[0:1, :]),
                          deps=[t_sm[-1], f.get("srow")], sem=ve)
            f["pSums"] = t_s1
            t_r1 = P.task("vector", lambda e: e.reciprocal(rrow[:], srow[:]),
                          deps=[t_s1, f.get("rrow")], sem=ve)
            f["srow"] = t_r1
            t_r2 = P.task("vector", lambda e: e.tensor_copy(rrowb[:], rrow[:]),
                          deps=[t_r1, f.get("rrowb")], sem=ve)
            f["rrow"] = t_r2
            t_tr = P.task("tensor", lambda e: e.transpose(
                pTc[:, 0:1], rrowb[:], c["eyeb"][0:1, 0:1]),
                deps=[t_r2, f.get("pTc")], sem=te)
            f["rrowb"] = t_tr
            t_rc = P.task("vector", lambda e: e.tensor_copy(rcol[:], pTc[:, 0:1]),
                          deps=[t_tr, f.get("rcol")], sem=ve)
            f["pTc"] = t_rc
            t_mv = P.task("vector", lambda e, h=h: e.tensor_scalar(
                mkv_all[:, h * 64:(h + 1) * 64], pMKV[:], rcol[:, 0:1], None,
                ALU.mult), deps=[t_mk[-1], t_rc, f.get(("mkv", h))], sem=ve)
            f["pMKV"] = t_mv
            f["rcol"] = t_mv
        t_c2v = P.last_tok["vector"]
    es12.close()
    P.barrier()

    with ExitStack() as es3:
        sb3 = lambda n, s, d=F32: es3.enter_context(nc.sbuf_tensor(n, s, d))
        ps3 = lambda n, s, d=F32: es3.enter_context(nc.psum_tensor(n, s, d))
        wqb = sb3("c_wq", [128, 8192], BF16)
        wob = sb3("c_wo", [128, 8192], BF16)
        bos = sb3("c_bo", [128, 8])
        t_w2 = []
        for ci in range(8):
            t_w2.append(P.task("gpsimd", lambda e, ci=ci: e.dma_start(
                out=wqb[:, ci * 1024:(ci + 1) * 1024],
                in_=io["wq"][ci * 128:(ci + 1) * 128, :]), sem="c_gq", amt=16))
            t_w2.append(P.task("gpsimd", lambda e, ci=ci: e.dma_start(
                out=wob[:, ci * 1024:(ci + 1) * 1024],
                in_=io["wo"][ci * 128:(ci + 1) * 128, :]), sem="c_gq", amt=16))
        t_bo = P.task("sync", lambda e: e.dma_start(out=bos[:, :], in_=io["bo"]),
                      sem="c_sb", amt=16)
        t_w2_all = t_w2[-1]

        xng = [sb3(f"c_xng{k}", [128, 4096], BF16) for k in range(2)]
        y2g = [sb3(f"c_y2g{k}", [128, 4096], BF16) for k in range(2)]
        qTg = sb3("c_qTg", [128, 4096], BF16)
        oCT = sb3("c_oCT", [128, 4096], BF16)
        expQM = [sb3(f"c_eQM{k}", [64, 512], BF16) for k in range(2)]
        qsr = [sb3(f"c_qsr{k}", [1, 512]) for k in range(2)]
        qrr = [sb3(f"c_qrr{k}", [1, 512]) for k in range(2)]
        qrb = [sb3(f"c_qrb{k}", [1, 512], BF16) for k in range(2)]
        rb2 = [sb3(f"c_rb2{k}", [64, 512], BF16) for k in range(2)]
        y3sb = sb3("c_y3", [128, 4096], BF16)
        pQ = [ps3(f"c_pQ{k}", [128, 512]) for k in range(2)]
        pQM = ps3("c_pQM", [64, 512])
        pQS2 = ps3("c_pQS", [33, 512])
        pRB2 = ps3("c_pRB2", [64, 512])
        pOU = [ps3(f"c_pOU{k}", [64, 512]) for k in range(2)]
        pA = ps3("c_pA", [128, 512])

        for g in range(NT):
            k = g % 2
            t0 = g * 512
            t_lx = []
            t_ly = []
            for ci in range(8):
                t_lx.append(P.task("sync", lambda e, ci=ci, t0=t0, k=k: e.dma_start(
                    out=xng[k][:, ci * 512:(ci + 1) * 512],
                    in_=xn3T[ci * 128:(ci + 1) * 128, t0:t0 + 512]),
                    deps=[f.get(("xng", k))], sem=f"c_s3{k}", amt=16))
                t_ly.append(P.task("sync", lambda e, ci=ci, t0=t0, k=k: e.dma_start(
                    out=y2g[k][:, ci * 512:(ci + 1) * 512],
                    in_=y2T[ci * 128:(ci + 1) * 128, t0:t0 + 512]),
                    deps=[f.get(("y2g", k))], sem=f"c_s4{k}", amt=16))
            t_lx_all, t_ly_all = t_lx[-1], t_ly[-1]
            t_qe = []
            for cp in range(8):
                kq = cp % 2
                t_qm = []
                for ci in range(8):
                    t_qm.append(P.task("tensor", lambda e, ci=ci, cp=cp, k=k, kq=kq: e.matmul(
                        pQ[kq][:], wqb[:, ci * 1024 + cp * 128: ci * 1024 + (cp + 1) * 128],
                        xng[k][:, ci * 512:(ci + 1) * 512],
                        start=(ci == 0), stop=(ci == 7)),
                        deps=([t_lx_all, t_w2_all, f.get(("pQ", kq))] if ci == 0 else []),
                        sem=te))
                t_qe.append(P.task("vector", lambda e, cp=cp, kq=kq: e.tensor_copy(
                    qTg[:, cp * 512:(cp + 1) * 512], pQ[kq][:]),
                    deps=[t_qm[-1]] + ([f.get("qTg")] if cp == 0 else []), sem=ve))
                f[("pQ", kq)] = t_qe[-1]
            t_on = []

            def c3_norm(h, kh, t_ss, t_ou):
                hr = slice((h % 2) * 64, (h % 2) * 64 + 64)
                t_s2 = P.task("vector", lambda e, kh=kh: e.tensor_copy(
                    qsr[kh][:], pQS2[kh * 32:kh * 32 + 1, :]),
                    deps=[t_ss, f.get(("qsr", kh))], sem=ve)
                f[("pQS", kh)] = t_s2
                t_r1 = P.task("vector", lambda e, kh=kh: e.reciprocal(
                    qrr[kh][:], qsr[kh][:]), deps=[t_s2, f.get(("qrr", kh))], sem=ve)
                f[("qsr", kh)] = t_r1
                t_r2 = P.task("vector", lambda e, kh=kh: e.tensor_copy(
                    qrb[kh][:], qrr[kh][:]), deps=[t_r1, f.get(("qrb", kh))], sem=ve)
                f[("qrr", kh)] = t_r2
                t_bc = P.task("tensor", lambda e, kh=kh: e.matmul(
                    pRB2[:], c["ones1b"][0:1, 0:64], qrb[kh][:],
                    start=True, stop=True),
                    deps=[t_r2, f.get("pRB2")], sem=te)
                f[("qrb", kh)] = t_bc
                t_rb = P.task("vector", lambda e, kh=kh: e.tensor_copy(
                    rb2[kh][:], pRB2[:]), deps=[t_bc, f.get(("rb2", kh))], sem=ve)
                f["pRB2"] = t_rb
                t_o = P.task("vector", lambda e, h=h, hr=hr, kh=kh: e.tensor_mul(
                    oCT[hr, (h // 2) * 512:(h // 2) * 512 + 512], pOU[kh][:],
                    rb2[kh][:]),
                    deps=[t_ou, t_rb, f.get(("oCT", h))], sem=ve)
                f[("pOU", kh)] = t_o
                f[("rb2", kh)] = t_o
                t_on.append(t_o)

            pend_n = None
            for h in range(16):
                kh = h % 2
                hr = slice((h % 2) * 64, (h % 2) * 64 + 64)
                mc = (h // 2) * 64
                t_qs = P.task("tensor", lambda e, h=h, hr=hr, mc=mc: e.matmul(
                    pQM[:], medb[hr, mc:mc + 64],
                    qTg[hr, (h // 2) * 512:(h // 2) * 512 + 512],
                    start=True, stop=True),
                    deps=[t_qe[h // 2], f.get("pQM"), t_c2v], sem=te)
                t_eq = P.task("scalar", lambda e, kh=kh: e.activation(
                    expQM[kh][:], pQM[:], AF.Exp, scale=0.125),
                    deps=[t_qs, f.get(("expQM", kh))], sem=sc)
                f["pQM"] = t_eq
                t_ss = P.task("tensor", lambda e, kh=kh: e.matmul(
                    pQS2[kh * 32:kh * 32 + 1, :], c["onescb"][0:64, :], expQM[kh][:],
                    start=True, stop=True),
                    deps=[t_eq, f.get(("pQS", kh))], sem=te)
                t_ou = P.task("tensor", lambda e, h=h, kh=kh: e.matmul(
                    pOU[kh][:], mkv_all[:, h * 64:(h + 1) * 64], expQM[kh][:],
                    start=True, stop=True),
                    deps=[t_eq, t_c2v, f.get(("pOU", kh))], sem=te)
                f[("expQM", kh)] = [t_ss, t_ou]
                if pend_n is not None:
                    c3_norm(*pend_n)
                pend_n = (h, kh, t_ss, t_ou)
            c3_norm(*pend_n)
            for cp in range(8):
                t_am = []
                for ci in range(8):
                    t_am.append(P.task("tensor", lambda e, ci=ci, cp=cp: e.matmul(
                        pA[:], wob[:, ci * 1024 + cp * 128: ci * 1024 + (cp + 1) * 128],
                        oCT[:, ci * 512:(ci + 1) * 512],
                        start=(ci == 0), stop=(ci == 7)),
                        deps=([t_on[-1], f.get("pA")] if ci == 0 else []), sem=te))
                t_y3 = P.task("vector", lambda e, cp=cp, k=k: e.scalar_tensor_tensor(
                    y3sb[:, cp * 512:(cp + 1) * 512], pA[:], bos[:, cp:cp + 1],
                    y2g[k][:, cp * 512:(cp + 1) * 512], ALU.add, ALU.add),
                    deps=[t_am[-1], t_ly_all, t_bo, f.get(("y3", cp))], sem=ve)
                f["pA"] = t_y3
                t_wr = P.task("sync", lambda e, cp=cp, t0=t0: e.dma_start(
                    out=y3T[cp * 128:(cp + 1) * 128, t0:t0 + 512],
                    in_=y3sb[:, cp * 512:(cp + 1) * 512]),
                    deps=[t_y3], sem="c_sw", amt=16)
            t_wr_last = P.last_tok["sync"]
            for cp in range(8):
                f[("y3", cp)] = t_wr_last
            for h in range(16):
                f[("oCT", h)] = P.last_tok["tensor"]
            f[("xng", k)] = [P.last_tok["tensor"]]
            f[("y2g", k)] = [t_y3]
            f["qTg"] = [P.last_tok["tensor"]]
    return None



def emit_stageD0_gen(P, nc, es, c, io, T):
    sb0 = lambda n, s, d=F32: es.enter_context(nc.sbuf_tensor(n, s, d))
    ps0 = lambda n, s, d=F32: es.enter_context(nc.psum_tensor(n, s, d))
    te, ve, sc = "d0_te", "d0_ve", "d0_sc"
    f = {}
    wst = [sb0(f"d0_wst{k}", [128, 512]) for k in range(2)]
    absv = sb0("d0_absv", [128, 512])
    sgn = sb0("d0_sgn", [128, 512], BF16)
    stp = sb0("d0_stp", [128, 512], BF16)
    qout = [sb0(f"d0_q{k}", [128, 512], BF16) for k in range(2)]
    gacc = sb0("d0_gacc", [128, 64])
    grow = sb0("d0_grow", [1, 64])
    g11 = sb0("d0_g11", [1, 1])
    nthr = sb0("d0_nthr", [128, 1])
    nthr_h = io["d0thr"]

    for mi, (srcname, dstname, W, NC) in enumerate(
            [("wfcT", "wqfch", 4096, 8), ("wpT", "wqph", 1024, 32)]):
        src = io[srcname]
        dst = io[dstname]
        SW = 512
        blocks = [(ch, co) for ch in range(NC) for co in range(0, W, SW)]
        NB = len(blocks)
        t_r = []
        for bi, (ch, co) in enumerate(blocks):
            k = bi % 2
            gi = bi % 64
            t_l = P.task("gpsimd", lambda e, src=src, ch=ch, co=co, k=k: e.dma_start(
                out=wst[k][:, :], in_=src[ch * 128:(ch + 1) * 128, co:co + 512]),
                deps=[f.get(("wst", k))], sem=f"d0_sl{k}", amt=16)
            t_a = P.task("scalar", lambda e, k=k: e.activation(
                absv[:, :], wst[k][:, :], AF.Abs),
                deps=[t_l, f.get("absv")], sem=sc)
            f[("wst", k)] = t_a
            t_r.append(P.task("vector", lambda e, gi=gi: e.tensor_reduce(
                gacc[:, gi:gi + 1], absv[:, :], AX.X, ALU.add),
                deps=[t_a, f.get("gacc") if bi == 0 else None], sem=ve))
            f["absv"] = t_r[-1]
            yield
        t_g2 = P.task("gpsimd", lambda e: e.tensor_reduce(
            grow[0:1, 0:64], gacc[:, 0:64], mybir.AxisListType.C, ALU.add),
            deps=[t_r[-1], f.get("grow")], sem="d0_gp")
        f["gacc"] = t_g2
        t_g3 = P.task("vector", lambda e: e.tensor_reduce(
            g11[:], grow[0:1, 0:64], AX.X, ALU.add), deps=[t_g2, f.get("g11")], sem=ve)
        NEL = float(NC * 128 * W)
        t_g4 = P.task("vector", lambda e, NEL=NEL: e.tensor_scalar(
            g11[:], g11[:], 1.0 / NEL, 1e-6, ALU.mult, ALU.add), deps=[t_g3], sem=ve)
        t_g5 = P.task("vector", lambda e: e.tensor_scalar_mul(
            g11[:], g11[:], -0.5), deps=[t_g4], sem=ve)
        t_g6 = P.task("gpsimd", lambda e: e.dma_start(out=nthr_h, in_=g11[:]),
                      deps=[t_g5], sem="d0_th", amt=16)
        t_g7 = P.task("gpsimd", lambda e: e.dma_start(
            out=nthr[:], in_=bass.AP(nthr_h.tensor, 0, [[0, 128], [1, 1]])),
            deps=[t_g6, f.get("nthr")], sem="d0_th2", amt=16)
        f["grow"] = t_g3
        f["g11"] = t_g6
        for bi, (ch, co) in enumerate(blocks):
            k = bi % 2
            t_l = P.task("gpsimd", lambda e, src=src, ch=ch, co=co, k=k: e.dma_start(
                out=wst[k][:, :], in_=src[ch * 128:(ch + 1) * 128, co:co + 512]),
                deps=[f.get(("wst", k))], sem=f"d0_sl{k}", amt=16)
            t_a = P.task("scalar", lambda e, k=k: e.activation(
                absv[:, :], wst[k][:, :], AF.Abs),
                deps=[t_l, f.get("absv")], sem=sc)
            t_s = P.task("scalar", lambda e, k=k: e.activation(
                sgn[:, :], wst[k][:, :], AF.Sign),
                deps=[t_l, f.get("sgn")], sem=sc)
            f[("wst", k)] = t_s
            t_t = P.task("scalar", lambda e: e.activation(
                stp[:, :], absv[:, :], AF.Sign, bias=nthr[:, 0:1]),
                deps=[t_a, t_g7, f.get("stp")], sem=sc)
            f["absv"] = t_t
            t_t2 = P.task("vector", lambda e: e.tensor_scalar_max(
                stp[:, :], stp[:, :], 0.0), deps=[t_t], sem=ve)
            t_q = P.task("vector", lambda e, k=k: e.tensor_mul(
                qout[k][:, :], sgn[:, :], stp[:, :]),
                deps=[t_t2, t_s, f.get(("qout", k))], sem=ve)
            f["sgn"] = t_q
            f["stp"] = t_q
            t_w = P.task("gpsimd", lambda e, dst=dst, ch=ch, co=co, k=k: e.dma_start(
                out=dst[ch * 128:(ch + 1) * 128, co:co + 512], in_=qout[k][:, :]),
                deps=[t_q], sem=f"d0_sw{k}", amt=16)
            f[("qout", k)] = t_w
            yield


def emit_stageD(P, nc, es, c, io, T):
    NT = T // 512
    y3T, yout = io["y3T"], io["y"]

    sb = lambda n, s, d=F32: es.enter_context(nc.sbuf_tensor(n, s, d))
    ps = lambda n, s, d=F32: es.enter_context(nc.psum_tensor(n, s, d))

    wqfc = sb("d_wqfc", [128, 32768], BF16)
    wqp = sb("d_wqp", [128, 32768], BF16)
    te, ve, sc = "d_te", "d_ve", "d_sc"
    f = {}
    t_lwq = []
    for ci in range(8):
        t_lwq.append(P.task("sync", lambda e, ci=ci: e.dma_start(
            out=wqfc[:, ci * 4096:(ci + 1) * 4096],
            in_=io["wqfch"][ci * 128:(ci + 1) * 128, :]), sem="d_lq", amt=16))
    for ch in range(32):
        t_lwq.append(P.task("sync", lambda e, ch=ch: e.dma_start(
            out=wqp[:, ch * 1024:(ch + 1) * 1024],
            in_=io["wqph"][ch * 128:(ch + 1) * 128, :]), sem="d_lq", amt=16))
    t_lwq_all = t_lwq[-1]
    with ExitStack() as es1:
        sb1 = lambda n, s, d=F32: es1.enter_context(nc.sbuf_tensor(n, s, d))
        ps1 = lambda n, s, d=F32: es1.enter_context(nc.psum_tensor(n, s, d))
        y3g = [sb1(f"d_y3{k}", [128, 4096], BF16) for k in range(2)]
        xn4 = sb1("d_xn4", [128, 4096], BF16)
        zsb = sb1("d_z", [128, 16384], BF16)
        lnws = sb1("d_lnw", [128, 32])
        lnbs = sb1("d_lnb", [128, 32])
        y3nat = sb1("d_y3n", [128, 1024], BF16)
        y4sb = [sb1(f"d_y4{k}", [128, 512]) for k in range(2)]
        _dr = [sb1(f"d_row{i}", [1, 512]) for i in range(3)]
        aux = {
            "sq2": [sb1("d_sq0", [128, 512], BF16), sb1("d_sq1", [128, 512], BF16)],
            "rows": [_dr[0], _dr[1], _dr[2], _dr[1]],
        }
        pSrow = ps1("d_pSrow", [64, 512])
        aux["pS1"], aux["pS2"] = pSrow[0:1, :], pSrow[32:33, :]
        aux["pB1"] = ps1("d_pB1", [128, 512])
        aux["pB2"] = ps1("d_pB2", [128, 512])
        pH = [ps1(f"d_pH{k}", [128, 512]) for k in range(2)]
        pO = [ps1(f"d_pO{k}", [128, 512]) for k in range(2)]
        pTt = ps1("d_pTt", [128, 128], BF16)

        t_lw = P.task("sync", lambda e: e.dma_start(out=lnws[:, :], in_=io["lnw"]),
                      sem="d_sv", amt=16)
        t_lb = P.task("sync", lambda e: e.dma_start(out=lnbs[:, :], in_=io["lnb"]),
                      sem="d_sv", amt=16)

        for g in range(NT):
            k = g % 2
            t0 = g * 512
            t_ld = []
            for ci in range(8):
                t_ld.append(P.task("sync", lambda e, ci=ci, t0=t0, k=k: e.dma_start(
                    out=y3g[k][:, ci * 512:(ci + 1) * 512],
                    in_=y3T[ci * 128:(ci + 1) * 128, t0:t0 + 512]),
                    deps=[f.get(("y3g", k))], sem=f"d_s{k}", amt=16))
            t_ld_all = t_ld[-1]
            chunks_in = [(y3g[k][:, ci * 512:(ci + 1) * 512], t_ld_all) for ci in range(8)]
            chunks_out = [xn4[:, ci * 512:(ci + 1) * 512] for ci in range(8)]
            t_xn = emit_pln(P, nc, c, "d_", 512, chunks_in, chunks_out, aux,
                            deps_in=[f.get("xn4")])
            t_z = []
            for fc in range(32):
                kp = fc % 2
                t_hm = []
                for ci in range(8):
                    t_hm.append(P.task("tensor", lambda e, fc=fc, ci=ci, kp=kp: e.matmul(
                        pH[kp][:],
                        wqfc[:, ci * 4096 + fc * 128: ci * 4096 + (fc + 1) * 128],
                        xn4[:, ci * 512:(ci + 1) * 512],
                        start=(ci == 0), stop=(ci == 7)),
                        deps=[t_xn[ci]] + ([f.get(("pH", kp)), t_lwq_all]
                                           if ci == 0 else []), sem=te))
                t_z.append(P.task("scalar", lambda e, fc=fc, kp=kp: e.activation(
                    zsb[:, fc * 512:(fc + 1) * 512], pH[kp][:], AF.Gelu),
                    deps=[t_hm[-1], f.get(("z", fc))], sem=sc))
                f[("pH", kp)] = t_z[-1]
            chunks_in2 = [(zsb[:, fc * 512:(fc + 1) * 512], t_z[fc]) for fc in range(32)]
            chunks_out2 = [zsb[:, fc * 512:(fc + 1) * 512] for fc in range(32)]
            gain = [lnws[:, fc:fc + 1] for fc in range(32)]
            bias = [lnbs[:, fc:fc + 1] for fc in range(32)]
            t_hg = emit_pln(P, nc, c, "d_", 512, chunks_in2, chunks_out2, aux,
                            deps_in=[t_lb], gain=gain, bias=bias)
            for tt_ in range(4):
                t_tp = []
                for ci in range(8):
                    t_t = P.task("tensor", lambda e, ci=ci, tt_=tt_, k=k: e.transpose(
                        pTt[:], y3g[k][:, ci * 512 + tt_ * 128: ci * 512 + (tt_ + 1) * 128],
                        c["eyeb"]), deps=[t_ld_all, f.get("pTt")], sem=te)
                    t_c = P.task("vector", lambda e, ci=ci: e.tensor_copy(
                        y3nat[:, ci * 128:(ci + 1) * 128], pTt[:]),
                        deps=[t_t] + ([f.get("y3nat")] if ci == 0 else []), sem=ve)
                    f["pTt"] = t_c
                    t_tp.append(t_c)
                for cg in range(2):
                    kp = cg % 2
                    t_om = []
                    for fc in range(32):
                        t_om.append(P.task("tensor", lambda e, fc=fc, tt_=tt_, cg=cg,
                                           kp=kp: e.matmul(
                            pO[kp][:],
                            zsb[:, fc * 512 + tt_ * 128: fc * 512 + (tt_ + 1) * 128],
                            wqp[:, fc * 1024 + cg * 512: fc * 1024 + (cg + 1) * 512],
                            start=(fc == 0), stop=(fc == 31)),
                            deps=[t_hg[fc]] + ([f.get(("pO", kp)), t_lwq_all]
                                               if fc == 0 else []), sem=te))
                    t_y4 = P.task("vector", lambda e, cg=cg, kp=kp: e.tensor_add(
                        y4sb[kp][:], pO[kp][:], y3nat[:, cg * 512:(cg + 1) * 512]),
                        deps=[t_om[-1], t_tp[-1], f.get(("y4", kp))], sem=ve)
                    f[("pO", kp)] = t_y4
                    t_wr = P.task("sync", lambda e, t0=t0, tt_=tt_, cg=cg, kp=kp:
                        e.dma_start(
                            out=yout[t0 + tt_ * 128: t0 + (tt_ + 1) * 128,
                                     cg * 512:(cg + 1) * 512],
                            in_=y4sb[kp][:]), deps=[t_y4], sem=f"d_sw{kp}", amt=16)
                    f[("y4", kp)] = t_wr
                f["y3nat"] = P.last_tok["vector"]
            f[("y3g", k)] = [P.last_tok["tensor"], t_xn[-1]]
            for fc in range(32):
                f[("z", fc)] = P.last_tok["tensor"]
            f["xn4"] = [P.last_tok["tensor"]]
    return None




def _core_inputs(b, L0_attn_x, L0_attn_r_sigma, L0_ffn_q_w, L0_ffn_k_w,
                 L0_ffn_v_w, L0_ffn_out_w, L0_ffn_gate_w, L0_ffn_gate_b,
                 L0_ffn_scale_w, L0_ffn_n1_g, L0_ffn_n1_b, L0_ffn_n2_g,
                 L0_ffn_n2_b, L0_ffn_moe_gate_w, L0_ffn_moe_gate_b, L0_ffn_w1,
                 L0_ffn_b1, L0_ffn_ln_g, L0_ffn_ln_b, L0_ffn_w2, L0_ffn_b2,
                 L0_ffn_res_scale, L1_attn_Wq, L1_attn_Wkv, L1_attn_Wo,
                 L1_attn_bo, L1_attn_mediator, L1_ffn_W_fc, L1_ffn_ln_w,
                 L1_ffn_ln_b, L1_ffn_W_proj, _shared={}):
    if not _shared:
        sw_bd = np.zeros((1024, 16), dtype=np.float32)
        for h in range(16):
            sw_bd[h * 64:(h + 1) * 64, h] = L0_ffn_scale_w[h]
        col8 = lambda a: np.ascontiguousarray(np.asarray(a, np.float32).reshape(8, 128).T)
        col32 = lambda a: np.ascontiguousarray(np.asarray(a, np.float32).reshape(32, 128).T)
        f32c = lambda a: np.ascontiguousarray(np.asarray(a, np.float32))
        _shared.update({
            "r_sigma": f32c(L0_attn_r_sigma).reshape(1, 1),
            "qw": f32c(L0_ffn_q_w), "kw": f32c(L0_ffn_k_w), "vw": f32c(L0_ffn_v_w),
            "sw_bd": sw_bd, "gw": f32c(L0_ffn_gate_w), "ow": f32c(L0_ffn_out_w),
            "gb": col8(L0_ffn_gate_b),
            "n1g": col8(L0_ffn_n1_g), "n1b": col8(L0_ffn_n1_b),
            "n2g": col8(L0_ffn_n2_g), "n2b": col8(L0_ffn_n2_b),
            "mgw": f32c(L0_ffn_moe_gate_w),
            "mgb": f32c(L0_ffn_moe_gate_b).reshape(1, 8),
            "w1": f32c(L0_ffn_w1), "w2": f32c(L0_ffn_w2),
            "b1r": np.ascontiguousarray(np.asarray(L0_ffn_b1, np.float32)
                                        .reshape(8, 16, 128).transpose(0, 2, 1)),
            "lgr": np.ascontiguousarray(np.asarray(L0_ffn_ln_g, np.float32)
                                        .reshape(8, 16, 128).transpose(0, 2, 1)),
            "lbr": np.ascontiguousarray(np.asarray(L0_ffn_ln_b, np.float32)
                                        .reshape(8, 16, 128).transpose(0, 2, 1)),
            "b2r": np.ascontiguousarray(np.asarray(L0_ffn_b2, np.float32)
                                        .reshape(8, 8, 128).transpose(0, 2, 1)),
            "rsc": f32c(L0_ffn_res_scale).reshape(1, 8),
            "wq": f32c(L1_attn_Wq), "wkv": f32c(L1_attn_Wkv), "wo": f32c(L1_attn_Wo),
            "bo": col8(L1_attn_bo),
            "medT": np.ascontiguousarray(np.asarray(L1_attn_mediator, np.float32)[0].T),
            "wfcT": np.ascontiguousarray(np.asarray(L1_ffn_W_fc, np.float32).T),
            "wpT": np.ascontiguousarray(np.asarray(L1_ffn_W_proj, np.float32).T),
            "lnw": col32(L1_ffn_ln_w), "lnb": col32(L1_ffn_ln_b),
        })
    d = dict(_shared)
    d["x"] = np.ascontiguousarray(np.asarray(L0_attn_x, np.float32)[b])
    return d


def kernel(**inputs):
    T = 4096
    nc = build_program(T=T, stages=("A", "B1", "B2", "B3", "B4", "C", "D"))
    shared = {}
    in_maps = [_core_inputs(b, _shared=shared, **inputs) for b in range(8)]
    res = run_bass_kernel_spmd(nc, in_maps, core_ids=list(range(8)))
    return np.stack([np.asarray(r["y"]) for r in res.results], axis=0)
